# version 03
# baseline (speedup 1.0000x reference)
import math
from contextlib import ExitStack

import numpy as np
import ml_dtypes

import concourse.bass as bass
import concourse.mybir as mybir
from concourse.bass_utils import run_bass_kernel_spmd

F32 = mybir.dt.float32
BF16 = mybir.dt.bfloat16
BF = ml_dtypes.bfloat16
AF = mybir.ActivationFunctionType
ALU = mybir.AluOpType
AXX = mybir.AxisListType.X

D = 1024
NH = 16
HD = 64
KVH = 4
NE = 8
HF = 4096
NS = 64
SDQ = 256
RDH = 512
EPS = 1e-5

ENGS = ("sync", "gpsimd", "tensor", "scalar", "vector")


class KC:
    def __init__(self, nc, ctx):
        self.nc = nc
        self.ctx = ctx
        self.sems = {}
        self.c = {}
        self.dsems = set()

    def sem(self, name):
        if name not in self.sems:
            self.sems[name] = self.ctx.enter_context(self.nc.semaphore(name))
            self.c[name] = 0
        return self.sems[name]

    def inc(self, ins, name, n=1):
        ins.then_inc(self.sem(name), n)
        self.c[name] += n
        if n >= 16:
            self.dsems.add(name)
        return self.c[name]


def barrier(K, engines=ENGS):
    nc = K.nc
    for e in engines:
        K.sem("B" + e)
    snap = {e: K.c["B" + e] for e in engines}
    dwaits = [(s, K.c[s]) for s in sorted(K.dsems) if K.c[s] > 0]
    with nc.Block(None) as b:
        def mk(e):
            def f(eng):
                if e == "sync":
                    for s, v in dwaits:
                        eng.wait_ge(K.sems[s], v)
                eng.drain()
                eng.sem_inc(K.sems["B" + e], 1)
                for o in engines:
                    if o != e:
                        eng.wait_ge(K.sems["B" + o], snap[o] + 1)
            return f
        for e in engines:
            getattr(b, e)(mk(e))
    for e in engines:
        K.c["B" + e] += 1


def build_program(T=2048, blocks=("l0a", "l0f", "l1a", "l1f")):
    TT = T // 128
    nc = bass.Bass("TRN2", target_bir_lowering=False, debug=False)

    inp = {}

    def I(name, shape):
        inp[name] = nc.dram_tensor(name, list(shape), F32, kind="ExternalInput").ap()

    I("x", (T, D))
    I("L0_attn_Wq", (D, D)); I("L0_attn_bq", (D,))
    I("L0_attn_Wk", (D, D)); I("L0_attn_bk", (D,))
    I("L0_attn_Wv", (D, D)); I("L0_attn_bv", (D,))
    I("L0_attn_Wo", (D, D)); I("L0_attn_bo", (D,))
    I("L0_ffn_router_w", (NE, D))
    I("L0_ffn_w_fc", (NE, D, HF)); I("L0_ffn_b_fc", (NE, HF))
    I("L0_ffn_w_proj", (NE, HF, D)); I("L0_ffn_b_proj", (NE, D))
    I("L1_attn_Wq", (D, D)); I("L1_attn_Wk", (KVH * HD, D)); I("L1_attn_Wv", (KVH * HD, D))
    I("L1_attn_Wo", (D, D))
    I("L1_ffn_slot_init", (NS, D))
    I("L1_ffn_w1", (2 * D, RDH)); I("L1_ffn_b1", (RDH,))
    I("L1_ffn_w2", (RDH, NS)); I("L1_ffn_b2", (NS,))
    I("L1_ffn_tau", (1,))
    I("L1_ffn_Wih", (3 * D, D)); I("L1_ffn_Whh", (3 * D, D))
    I("L1_ffn_bih", (3 * D,)); I("L1_ffn_bhh", (3 * D,))
    I("L1_ffn_wq", (D, SDQ)); I("L1_ffn_bq", (SDQ,))
    I("L1_ffn_wk", (D, SDQ)); I("L1_ffn_bk", (SDQ,))
    I("L1_ffn_wv", (D, D)); I("L1_ffn_bv", (D,))
    I("L1_ffn_ln_g", (D,)); I("L1_ffn_ln_b", (D,))
    I("L1_ffn_wvp", (D, D)); I("L1_ffn_bvp", (D,))
    I("L1_ffn_wo", (D, D)); I("L1_ffn_bo", (D,))

    y_out = nc.dram_tensor("y", [T, D], F32, kind="ExternalOutput").ap()
    x_park = nc.dram_tensor("x_park", [T, D], F32, kind="Internal").ap()
    sc_park = nc.dram_tensor("sc_park", [64, 64], F32, kind="Internal").ap()

    pos = np.arange(T, dtype=np.float64).reshape(TT, 128).T
    invf = 10000.0 ** (-(np.arange(32, dtype=np.float64) / 32.0))
    ang = pos[:, :, None] * invf[None, None, :]
    cos_c = nc.inline_tensor(np.ascontiguousarray(np.cos(ang).astype(np.float32).reshape(128, TT * 32)), "cos_c").ap()
    sin_c = nc.inline_tensor(np.ascontiguousarray(np.sin(ang).astype(np.float32).reshape(128, TT * 32)), "sin_c").ap()
    ident_c = nc.inline_tensor(np.eye(128, dtype=BF), "ident_c").ap()
    identf_c = nc.inline_tensor(np.eye(128, dtype=np.float32), "identf_c").ap()
    tri_c = nc.inline_tensor(np.triu(np.ones((128, 128), np.float32)).astype(BF), "tri_c").ap()
    hc = np.zeros((128, 2), np.float32); hc[0:64, 0] = 1.0; hc[64:128, 1] = 1.0
    hcols_c = nc.inline_tensor(hc, "hcols_c").ap()

    with ExitStack() as ctx:
        En = ctx.enter_context
        K = KC(nc, ctx)

        AXA = En(nc.sbuf_tensor("AXA", [128, 16 * 1024], F32))
        AA = En(nc.sbuf_tensor("AA", [128, 16384], BF16))
        AB = En(nc.sbuf_tensor("AB", [128, 16384], BF16))
        AC = En(nc.sbuf_tensor("AC", [128, 16640], BF16))
        WB = En(nc.sbuf_tensor("WB", [128, 8192], BF16))
        S32 = En(nc.sbuf_tensor("S32", [128, 3072], F32))
        SM = En(nc.sbuf_tensor("SM", [128, 1280], F32))
        SMB = En(nc.sbuf_tensor("SMB", [128, 2048], BF16))
        ONEB = En(nc.sbuf_tensor("ONEB", [128, 2048], BF16))
        COS = En(nc.sbuf_tensor("COS", [128, TT * 32], F32))
        SIN = En(nc.sbuf_tensor("SIN", [128, TT * 32], F32))
        IDN = En(nc.sbuf_tensor("IDN", [128, 128], BF16))
        IDF = En(nc.sbuf_tensor("IDF", [128, 128], F32))
        TRI = En(nc.sbuf_tensor("TRI", [128, 128], BF16))
        HCF = En(nc.sbuf_tensor("HCF", [128, 2], F32))

        AXb = AXA[:].bitcast(BF16)

        st = dict(T=T, TT=TT, inp=inp, y=y_out, x_park=x_park, sc_park=sc_park,
                  AX=AXA, AXb=AXb, AA=AA, AB=AB, AC=AC, WB=WB, S32=S32, SM=SM,
                  SMB=SMB, ONEB=ONEB, COS=COS, SIN=SIN, IDN=IDN, IDF=IDF,
                  TRI=TRI, HCF=HCF)

        with nc.Block("init") as b:
            @b.sync
            def _(sy):
                for t in range(TT):
                    ins = sy.dma_start(out=AXA[:, t * 1024:(t + 1) * 1024],
                                       in_=inp["x"][t * 128:(t + 1) * 128, :])
                    K.inc(ins, "LD", 16)
                K.inc(sy.dma_start(out=COS[:], in_=cos_c), "LD", 16)
                K.inc(sy.dma_start(out=SIN[:], in_=sin_c), "LD", 16)
                K.inc(sy.dma_start(out=IDN[:], in_=ident_c), "LD", 16)
                K.inc(sy.dma_start(out=IDF[:], in_=identf_c), "LD", 16)
                K.inc(sy.dma_start(out=TRI[:], in_=tri_c), "LD", 16)
                K.inc(sy.dma_start(out=HCF[:], in_=hcols_c), "LD", 16)

            @b.vector
            def _(v):
                K.inc(v.memset(ONEB[:], 1.0), "VE")
        barrier(K)

        if "l0a" in blocks:
            emit_attn(K, st, layer=0)
            barrier(K)
        if "l0f" in blocks:
            emit_moe(K, st)
            barrier(K)
        if "l1a" in blocks:
            emit_attn(K, st, layer=1)
            barrier(K)
        if "l1f" in blocks:
            emit_slotffn(K, st)
            barrier(K)

        with nc.Block("fin") as b:
            @b.sync
            def _(sy):
                for t in range(TT):
                    ins = sy.dma_start(out=y_out[t * 128:(t + 1) * 128, :],
                                       in_=AXA[:, t * 1024:(t + 1) * 1024])
                    K.inc(ins, "ST", 16)
                sy.wait_ge(K.sem("ST"), K.c["ST"])

    return nc



def emit_ln_lnT(K, st, park_x, sink=None, sink_persistent=False):
    nc = K.nc
    K.uid = getattr(K, "uid", 0) + 1
    TT = st["TT"]
    AX, S32, SM, AA, WB = st["AX"], st["S32"], st["SM"], st["AA"], st["WB"]
    IDN, x_park = st["IDN"], st["x_park"]
    if sink is None:
        def sink(t):
            return WB[:, (t % 2) * 1024:((t % 2) + 1) * 1024]
    b_lns = K.c.get("LNS", 0); K.sem("LNS")
    b_lnv = K.c.get("LNV", 0); K.sem("LNV")
    b_trt = K.c.get("TRT", 0); K.sem("TRT")
    b_tre = K.c.get("TRE", 0); K.sem("TRE")
    K.sem("XPK")

    def smo(t, i):
        return SM[:, 16 * (t % 2) + i: 16 * (t % 2) + i + 1]

    with (nc.psum_tensor("ps_lntr%d" % K.uid, [128, 2, 8, 128], BF16) as ps_tr,
          nc.Block("ln%d" % K.uid) as b):

        @b.scalar
        def _(s):
            for t in range(TT):
                if t >= 2:
                    s.wait_ge(K.sems["LNV"], b_lnv + 2 * (t - 2) + 2)
                ins = s.activation(S32[:, 1024:2048], AX[:, t * 1024:(t + 1) * 1024],
                                   AF.Square, accum_out=smo(t, 6))
                K.inc(ins, "LNS")
                s.wait_ge(K.sems["LNV"], b_lnv + 2 * t + 1)
                ins = s.activation(smo(t, 3), smo(t, 2), AF.Sqrt)
                K.inc(ins, "LNS")

        @b.vector
        def _(v):
            for t in range(TT):
                xt = AX[:, t * 1024:(t + 1) * 1024]
                if t >= 2:
                    v.wait_ge(K.sems["LNS"], b_lns + 2 * (t - 2) + 2)
                ins = v.reduce_sum(smo(t, 0), xt, axis=AXX)
                K.inc(ins, "VEq")
                v.wait_ge(K.sems["LNS"], b_lns + 2 * t + 1)
                v.wait_ge(K.sems["VEq"], K.c["VEq"])
                ins = v.tensor_scalar(smo(t, 0), smo(t, 0), 1.0 / D, None, op0=ALU.mult)
                K.inc(ins, "VEq")
                ins = v.tensor_scalar(smo(t, 1), smo(t, 6), 1.0 / D, None, op0=ALU.mult)
                K.inc(ins, "VEq")
                v.wait_ge(K.sems["VEq"], K.c["VEq"])
                ins = v.tensor_tensor(smo(t, 2), smo(t, 0), smo(t, 0), op=ALU.mult)
                K.inc(ins, "VEq")
                v.wait_ge(K.sems["VEq"], K.c["VEq"])
                ins = v.tensor_tensor(smo(t, 2), smo(t, 1), smo(t, 2), op=ALU.subtract)
                K.inc(ins, "VEq")
                v.wait_ge(K.sems["VEq"], K.c["VEq"])
                ins = v.tensor_scalar(smo(t, 2), smo(t, 2), EPS, None, op0=ALU.add)
                K.inc(ins, "LNV")
                v.wait_ge(K.sems["LNS"], b_lns + 2 * t + 2)
                ins = v.reciprocal(smo(t, 4), smo(t, 3))
                K.inc(ins, "VEq")
                if (not sink_persistent) and t >= 2:
                    v.wait_ge(K.sems["TRT"], b_trt + (t - 2) + 1)
                v.wait_ge(K.sems["VEq"], K.c["VEq"])
                ins = v.tensor_scalar(sink(t), xt, smo(t, 0), smo(t, 4),
                                      op0=ALU.subtract, op1=ALU.mult)
                K.inc(ins, "LNV")

                if t >= 1:
                    tp = t - 1
                    v.wait_ge(K.sems["TRT"], b_trt + tp + 1)
                    dst = AA[:].rearrange("p (c n) -> p c n", c=8)[:, :, tp * 128:(tp + 1) * 128]
                    ins = v.tensor_copy(dst, ps_tr[:, tp % 2, :, :])
                    K.inc(ins, "TRE")
            tp = TT - 1
            v.wait_ge(K.sems["TRT"], b_trt + tp + 1)
            dst = AA[:].rearrange("p (c n) -> p c n", c=8)[:, :, tp * 128:(tp + 1) * 128]
            ins = v.tensor_copy(dst, ps_tr[:, tp % 2, :, :])
            K.inc(ins, "TRE")

        @b.tensor
        def _(te):
            for t in range(TT):
                te.wait_ge(K.sems["LNV"], b_lnv + 2 * t + 2)
                if t >= 2:
                    te.wait_ge(K.sems["TRE"], b_tre + (t - 2) + 1)
                lt = sink(t)
                for c in range(8):
                    ins = te.transpose(ps_tr[:, t % 2, c, :], lt[:, c * 128:(c + 1) * 128], IDN[:])
                K.inc(ins, "TRT")

        if park_x:
            @b.sync
            def _(sy):
                for t in range(TT):
                    sy.wait_ge(K.sems["LNV"], b_lnv + 2 * t + 2)
                    ins = sy.dma_start(out=x_park[t * 128:(t + 1) * 128, :],
                                       in_=AX[:, t * 1024:(t + 1) * 1024])
                    K.inc(ins, "XPK", 16)
    barrier(K)





def emit_transpose_tiles(K, st, src, dst, nt, nch=8):
    nc = K.nc
    K.uid = getattr(K, "uid", 0) + 1
    IDN = st["IDN"]
    b_t = K.c.get("TRT", 0); K.sem("TRT")
    b_e = K.c.get("TRE", 0); K.sem("TRE")
    with (nc.psum_tensor("ps_gtr%d" % K.uid, [128, 2, nch, 128], BF16) as ps_tr,
          nc.Block("gtr%d" % K.uid) as b):
        @b.tensor
        def _(te):
            for t in range(nt):
                if t >= 2:
                    te.wait_ge(K.sems["TRE"], b_e + (t - 2) + 1)
                s = src(t)
                for c in range(nch):
                    ins = te.transpose(ps_tr[:, t % 2, c, :], s[:, c * 128:(c + 1) * 128], IDN[:])
                K.inc(ins, "TRT")

        @b.vector
        def _(v):
            for t in range(nt):
                v.wait_ge(K.sems["TRT"], b_t + t + 1)
                ins = v.tensor_copy(dst(t), ps_tr[:, t % 2, 0:nch, :])
                K.inc(ins, "TRE")
    barrier(K)



def emit_attn(K, st, layer):
    nc = K.nc
    L_ = layer
    TT = st["TT"]; T = st["T"]
    AX, AXb, AA, AB, AC = st["AX"], st["AXb"], st["AA"], st["AB"], st["AC"]
    WB, S32, SM, SMB = st["WB"], st["S32"], st["SM"], st["SMB"]
    ONEB, COS, SIN, IDN, IDF, TRI, HCF = (st["ONEB"], st["COS"], st["SIN"],
                                          st["IDN"], st["IDF"], st["TRI"], st["HCF"])
    inp, x_park, sc_park = st["inp"], st["x_park"], st["sc_park"]
    causal = (layer == 1)
    L = "L0_attn_" if layer == 0 else "L1_attn_"
    nvg = NH if layer == 0 else KVH
    vw = 65 * nvg

    def lnT(c):
        return AA[:, c * 2048:(c + 1) * 2048]

    def qT(c):
        return AB[:, c * 2048:(c + 1) * 2048]

    def kTp(c):
        return AXb[:, c * 2048:(c + 1) * 2048]

    def stag(par):
        return AXb[:, 16384 + par * 2048:16384 + (par + 1) * 2048]

    def expb(par):
        return AXb[:, 20480 + par * 2048:20480 + (par + 1) * 2048]

    def qrot(par):
        return AXb[:, 24576 + par * 1024:24576 + (par + 1) * 1024]

    def ktokbuf(par):
        return AXb[:, 26624 + par * 384:26624 + (par + 1) * 384]

    def vext(t):
        return AC[:, t * vw:t * vw + vw]

    def wqt(dcin):
        return AC[:, 4352 + dcin * 1024:4352 + (dcin + 1) * 1024]

    def wkt(dcin):
        return AC[:, 12544 + dcin * 256:12544 + (dcin + 1) * 256]

    def wvt(dcin):
        return AC[:, 14592 + dcin * 256:14592 + (dcin + 1) * 256]

    emit_ln_lnT(K, st, park_x=True)

    with nc.Block("aprep%d" % layer) as b:
        @b.vector
        def _(v):
            v.memset(stag(0)[64:128, :], 0.0)
            v.memset(stag(1)[0:64, :], 0.0)
            v.memset(SM[:, 64:80], 0.0)
            if layer == 1:
                v.memset(ktokbuf(0), 0.0)
                v.memset(ktokbuf(1), 0.0)
            ins = v.memset(AC[:], 1.0)
            K.inc(ins, "VE")
    barrier(K)

    if layer == 0:
        _l0_projections(K, st, L, lnT, qT, kTp, qrot, vext)
    else:
        _l1_projections(K, st, L, lnT, qT, kTp, ktokbuf, vext, wqt, wkt, wvt)
    barrier(K)

    _emit_negM(K, st, layer)
    barrier(K)

    QC = (TT * 128) // 512 if TT >= 4 else 1
    qcw = 512 if TT >= 4 else TT * 128

    with (nc.psum_tensor("ps_sc_l%d" % layer, [128, 2, 512], F32) as ps_sc,
          nc.psum_tensor("ps_ctx_l%d" % layer, [128, TT * 128], F32) as ps_ctx):
        for h in range(NH):
            barrier(K, engines=("gpsimd", "tensor", "scalar", "vector"))
            b_sg = K.c.get("SG", 0); K.sem("SG")
            b_ss = K.c.get("SS", 0); K.sem("SS")
            b_se = K.c.get("SE", 0); K.sem("SE")
            b_sav = K.c.get("SAV", 0); K.sem("SAV")
            b_msk = K.c.get("MSK", 0); K.sem("MSK")
            par_h = h % 2
            ch = h // 2
            g = h if layer == 0 else h // 4

            def qcs(kt):
                if not causal:
                    return list(range(QC))
                return [qc for qc in range(QC) if (qc + 1) * qcw > kt * 128]

            def qts(kt):
                return [q for q in range(TT) if (q >= kt or not causal)]

            exp_after = []
            tot = 0
            for kt in range(TT):
                tot += len(qcs(kt))
                exp_after.append(tot)
            msk_after = [kt + 1 if causal else 0 for kt in range(TT)]

            with nc.Block("hd%d_%d" % (layer, h)) as b:
                @b.gpsimd
                def _(gp):
                    src_c = ch if layer == 0 else g
                    ins = gp.tensor_copy(stag(par_h)[64 * par_h:64 * par_h + 64, 0:T],
                                         kTp(src_c)[64 * par_h:64 * par_h + 64, 0:T])
                    K.inc(ins, "SG")

                @b.tensor
                def _(te):
                    te.wait_ge(K.sems["SG"], b_sg + 1)
                    nbank = max(1, (TT * 128 * 4) // 2048)
                    zw = min(512, TT * 128)
                    for bk in range(nbank):
                        ins = te.matmul(ps_ctx[:, bk * zw:(bk + 1) * zw],
                                        stag(0)[64:65, 0:128], stag(0)[64:65, 0:zw],
                                        start=True, stop=True)
                    idx = 0
                    for kt in range(TT):
                        if kt >= 2:
                            te.wait_ge(K.sems["SAV"], b_sav + kt - 1)
                        for qc in qcs(kt):
                            if idx >= 2:
                                te.wait_ge(K.sems["SE"], b_se + idx - 1)
                            ins = te.matmul(ps_sc[:, idx % 2, 0:qcw],
                                            stag(par_h)[:, kt * 128:(kt + 1) * 128],
                                            qT(ch)[:, qc * qcw:(qc + 1) * qcw],
                                            start=True, stop=True)
                            K.inc(ins, "SS")
                            idx += 1
                        te.wait_ge(K.sems["SE"], b_se + exp_after[kt])
                        if causal:
                            te.wait_ge(K.sems["MSK"], b_msk + msk_after[kt])
                        for qt in qts(kt):
                            last = (kt == (qt if causal else TT - 1))
                            ins = te.matmul(ps_ctx[:, qt * 128:qt * 128 + 65],
                                            expb(kt % 2)[:, qt * 128:(qt + 1) * 128],
                                            vext(kt)[:, 65 * g:65 * g + 65],
                                            start=False, stop=last,
                                            skip_group_check=True)
                        K.inc(ins, "SAV")

                @b.scalar
                def _(s):
                    idx = 0
                    for kt in range(TT):
                        if causal and kt >= 2:
                            s.wait_ge(K.sems["MSK"], b_msk + kt - 1)
                        for qc in qcs(kt):
                            s.wait_ge(K.sems["SS"], b_ss + idx + 1)
                            ins = s.activation(expb(kt % 2)[:, qc * qcw:(qc + 1) * qcw],
                                               ps_sc[:, idx % 2, 0:qcw], AF.Exp,
                                               bias=SM[:, 112 + h:113 + h], scale=0.125)
                            K.inc(ins, "SE")
                            idx += 1

                @b.vector
                def _(v):
                    if causal:
                        for kt in range(TT):
                            v.wait_ge(K.sems["SE"], b_se + exp_after[kt])
                            sl = expb(kt % 2)[:, kt * 128:(kt + 1) * 128]
                            ins = v.tensor_tensor(sl, sl, TRI[:], op=ALU.mult)
                            K.inc(ins, "MSK")
                    v.wait_ge(K.sems["SAV"], b_sav + TT)
                    for qt in range(TT):
                        v.wait_ge(K.sems["VE"], K.c["VE"])
                        ins = v.reciprocal(SM[:, 70:71], ps_ctx[:, qt * 128 + 64:qt * 128 + 65])
                        K.inc(ins, "VE")
                        v.wait_ge(K.sems["VE"], K.c["VE"])
                        ins = v.tensor_scalar(AA[:, qt * 1024 + 64 * h:qt * 1024 + 64 * h + 64],
                                              ps_ctx[:, qt * 128:qt * 128 + 64],
                                              SM[:, 70:71], None, op0=ALU.mult)
                        K.inc(ins, "VE")

    barrier(K)

    emit_transpose_tiles(K, st,
                         src=lambda t: AA[:, t * 1024:(t + 1) * 1024],
                         dst=lambda t: AB[:].rearrange("p (c n) -> p c n", c=8)[:, :, t * 128:(t + 1) * 128],
                         nt=TT)

    b_wl = K.c.get("WL", 0); K.sem("WL")
    b_po = K.c.get("PO", 0); K.sem("PO")
    b_res = K.c.get("RES", 0); K.sem("RES")
    b_xlb = [K.c.get("XLB0", 0), K.c.get("XLB1", 0)]; K.sem("XLB0"); K.sem("XLB1")
    b_wtt = K.c.get("WTT", 0); K.sem("WTT")
    b_wte = K.c.get("WTE", 0); K.sem("WTE")

    with (nc.psum_tensor("ps_o_l%d" % layer, [128, 2, 1024], F32) as ps_o,
          nc.psum_tensor("ps_wt_l%d" % layer, [128, 2, 8, 128], BF16) as ps_wt,
          nc.Block("attout%d" % layer) as b):

        @b.gpsimd
        def _(gp):
            for c in range(8):
                ins = gp.dma_start(out=WB[:, c * 1024:(c + 1) * 1024],
                                   in_=inp[L + "Wo"][c * 128:(c + 1) * 128, :])
                K.inc(ins, "WL", 16)
            if layer == 0:
                ins = gp.dma_start(out=SMB[0:1, 0:1024],
                                   in_=inp[L + "bo"].rearrange("(a d) -> a d", a=1))
                K.inc(ins, "WL", 16)

        if layer == 1:
            @b.tensor
            def _(te):
                te.wait_ge(K.sems["WL"], b_wl + 16 * 8)
                for rc in range(8):
                    if rc >= 2:
                        te.wait_ge(K.sems["WTE"], b_wte + rc - 1)
                    for dc in range(8):
                        ins = te.transpose(ps_wt[:, rc % 2, dc, :],
                                           WB[:, rc * 1024 + dc * 128:rc * 1024 + (dc + 1) * 128],
                                           IDN[:])
                    K.inc(ins, "WTT")

            @b.vector
            def _(v):
                for rc in range(8):
                    v.wait_ge(K.sems["WTT"], b_wtt + rc + 1)
                    dst = AC[:, 0:8192].rearrange("p (c n) -> p c n", c=8)[:, :, rc * 128:(rc + 1) * 128]
                    ins = v.tensor_copy(dst, ps_wt[:, rc % 2, 0:8, :])
                    K.inc(ins, "WTE")

        @b.sync
        def _(sy):
            for t in range(TT):
                if t >= 2:
                    sy.wait_ge(K.sems["RES"], b_res + t - 1)
                ins = sy.dma_start(out=S32[:, (t % 2) * 1024:((t % 2) + 1) * 1024],
                                   in_=x_park[t * 128:(t + 1) * 128, :])
                K.inc(ins, "XLB" + str(t % 2), 16)

        def wo_rhs(c):
            if layer == 0:
                return WB[:, c * 1024:(c + 1) * 1024]
            return AC[:, c * 1024:(c + 1) * 1024]

        @b.tensor
        def _(te):
            if layer == 0:
                te.wait_ge(K.sems["WL"], b_wl + 16 * 9)
            else:
                te.wait_ge(K.sems["WTE"], b_wte + 8)
            for t in range(TT):
                if t >= 2:
                    te.wait_ge(K.sems["RES"], b_res + t - 1)
                for c in range(8):
                    for n in range(2):
                        ins = te.matmul(ps_o[:, t % 2, n * 512:(n + 1) * 512],
                                        AB[:, c * 2048 + t * 128:c * 2048 + (t + 1) * 128],
                                        wo_rhs(c)[:, n * 512:(n + 1) * 512],
                                        start=(c == 0), stop=(c == 7 and layer == 1))
                if layer == 0:
                    for n in range(2):
                        ins = te.matmul(ps_o[:, t % 2, n * 512:(n + 1) * 512],
                                        ONEB[0:1, 0:128],
                                        SMB[0:1, n * 512:(n + 1) * 512],
                                        start=False, stop=True)
                K.inc(ins, "PO")

        @b.vector
        def _(v):
            for t in range(TT):
                v.wait_ge(K.sems["PO"], b_po + t + 1)
                v.wait_ge(K.sems["XLB" + str(t % 2)], b_xlb[t % 2] + 16 * (t // 2 + 1))
                ins = v.tensor_tensor(AX[:, t * 1024:(t + 1) * 1024],
                                      ps_o[:, t % 2, :],
                                      S32[:, (t % 2) * 1024:((t % 2) + 1) * 1024],
                                      op=ALU.add)
                K.inc(ins, "RES")
    barrier(K)



def _l0_projections(K, st, L, lnT, qT, kTp, qrot, vext):
    nc = K.nc
    TT = st["TT"]
    AX, AA, AB, AC = st["AX"], st["AA"], st["AB"], st["AC"]
    WB, S32, SM, SMB = st["WB"], st["S32"], st["SM"], st["SMB"]
    ONEB, COS, SIN, IDN, IDF = st["ONEB"], st["COS"], st["SIN"], st["IDN"], st["IDF"]
    inp = st["inp"]

    def q32s(t):
        return S32[:, (t % 2) * 1024:((t % 2) + 1) * 1024]

    T1 = S32[:, 2048:2560]
    T2 = S32[:, 2560:3072]

    for proj in ("q", "k", "v"):
        b_wl = K.c.get("WL", 0); K.sem("WL")
        b_pp = K.c.get("PP", 0); K.sem("PP")
        b_qe = K.c.get("QE", 0); K.sem("QE")
        b_qr = K.c.get("QR", 0); K.sem("QR")
        b_trt = K.c.get("TRT", 0); K.sem("TRT")
        b_tre = K.c.get("TRE", 0); K.sem("TRE")
        b_nrt = K.c.get("NRT", 0); K.sem("NRT")
        b_nrm = K.c.get("NRM", 0); K.sem("NRM")
        Wn, bn = L + "W" + proj, L + "b" + proj
        runslot = {"q": 64, "k": 65}.get(proj)

        def tdst(tp):
            if proj == "q":
                return AB[:].rearrange("p (c n) -> p c n", c=8)[:, :, tp * 128:(tp + 1) * 128]
            return st["AXb"][:, 0:16384].rearrange("p (c n) -> p c n", c=8)[:, :, tp * 128:(tp + 1) * 128]

        with (nc.psum_tensor("ps_p" + proj, [128, 2, 1024], F32) as ps_p,
              nc.psum_tensor("ps_t" + proj, [128, 2, 8, 128], BF16) as ps_tr,
              nc.psum_tensor("ps_n" + proj, [16, 128], F32) as ps_n,
              nc.Block("proj_" + L + proj) as b):

            @b.gpsimd
            def _(gp):
                for c in range(8):
                    ins = gp.dma_start(out=WB[:, c * 1024:(c + 1) * 1024],
                                       in_=inp[Wn][c * 128:(c + 1) * 128, :])
                    K.inc(ins, "WL", 16)
                ins = gp.dma_start(out=SMB[0:1, 0:1024],
                                   in_=inp[bn].rearrange("(a d) -> a d", a=1))
                K.inc(ins, "WL", 16)

            def emit_proj_mms(te, t):
                for c in range(8):
                    for n in range(2):
                        te.matmul(ps_p[:, t % 2, n * 512:(n + 1) * 512],
                                  lnT(c)[:, t * 128:(t + 1) * 128],
                                  WB[:, c * 1024 + n * 512:c * 1024 + (n + 1) * 512],
                                  start=(c == 0), stop=False)
                for n in range(2):
                    ins = te.matmul(ps_p[:, t % 2, n * 512:(n + 1) * 512],
                                    ONEB[0:1, 0:128],
                                    SMB[0:1, n * 512:(n + 1) * 512],
                                    start=False, stop=True)
                K.inc(ins, "PP")

            def emit_transposes(te, tp):
                te.wait_ge(K.sems["QR"], b_qr + tp + 1)
                te.wait_ge(K.sems["NRM"], b_nrm + 2 * tp)
                nrm = SM[:, 32 + 16 * (tp % 2):48 + 16 * (tp % 2)]
                ins = te.transpose(ps_n[:], nrm, IDF[:])
                K.inc(ins, "NRT")
                if tp >= 2:
                    te.wait_ge(K.sems["TRE"], b_tre + tp - 1)
                for c in range(8):
                    ins = te.transpose(ps_tr[:, tp % 2, c, :],
                                       qrot(tp % 2)[:, c * 128:(c + 1) * 128], IDN[:])
                K.inc(ins, "TRT")

            if proj == "v":
                @b.tensor
                def _(te):
                    te.wait_ge(K.sems["WL"], b_wl + 16 * 9)
                    for t in range(TT):
                        if t >= 2:
                            te.wait_ge(K.sems["QE"], b_qe + t - 1)
                        emit_proj_mms(te, t)

                @b.scalar
                def _(s):
                    for t in range(TT):
                        s.wait_ge(K.sems["PP"], b_pp + t + 1)
                        dst = vext(t).rearrange("p (g w) -> p g w", w=65)[:, :, 0:64]
                        ins = s.copy(dst, ps_p[:, t % 2, :].rearrange("p (g w) -> p g w", w=64))
                        K.inc(ins, "QE")
            else:
                @b.tensor
                def _(te):
                    te.wait_ge(K.sems["WL"], b_wl + 16 * 9)
                    for t in range(TT):
                        if t >= 2:
                            te.wait_ge(K.sems["QE"], b_qe + t - 1)
                        emit_proj_mms(te, t)
                        if t >= 1:
                            emit_transposes(te, t - 1)
                    emit_transposes(te, TT - 1)

                @b.scalar
                def _(s):
                    for t in range(TT):
                        s.wait_ge(K.sems["PP"], b_pp + t + 1)
                        if t >= 2:
                            s.wait_ge(K.sems["QR"], b_qr + t - 1)
                        ins = s.copy(q32s(t), ps_p[:, t % 2, :])
                        K.inc(ins, "QE")

                @b.vector
                def _(v):
                    for t in range(TT):
                        v.wait_ge(K.sems["QE"], b_qe + t + 1)
                        if t >= 2:
                            v.wait_ge(K.sems["TRT"], b_trt + t - 1)
                        if t >= 1:
                            v.wait_ge(K.sems["QR"], b_qr + t)
                        x32 = q32s(t).rearrange("p (h two c) -> p h two c", two=2, c=32)
                        x1, x2 = x32[:, :, 0, :], x32[:, :, 1, :]
                        qr = qrot(t % 2).rearrange("p (h two c) -> p h two c", two=2, c=32)
                        cosb = COS[:, t * 32:(t + 1) * 32].unsqueeze(1).broadcast_to([128, 16, 32])
                        sinb = SIN[:, t * 32:(t + 1) * 32].unsqueeze(1).broadcast_to([128, 16, 32])
                        t1v = T1.rearrange("p (h c) -> p h c", c=32)
                        t2v = T2.rearrange("p (h c) -> p h c", c=32)
                        v.wait_ge(K.sem("VEq"), K.c["VEq"])
                        ins = v.tensor_tensor(t1v, x2, sinb, op=ALU.mult)
                        K.inc(ins, "VEq")
                        ins = v.tensor_tensor(t2v, x1, cosb, op=ALU.mult)
                        K.inc(ins, "VEq")
                        v.wait_ge(K.sems["VEq"], K.c["VEq"])
                        ins = v.tensor_tensor(qr[:, :, 0, :], t2v, t1v, op=ALU.subtract)
                        K.inc(ins, "VEq")
                        v.wait_ge(K.sems["VEq"], K.c["VEq"])
                        ins = v.tensor_tensor(t1v, x1, sinb, op=ALU.mult)
                        K.inc(ins, "VEq")
                        ins = v.tensor_tensor(t2v, x2, cosb, op=ALU.mult)
                        K.inc(ins, "VEq")
                        v.wait_ge(K.sems["VEq"], K.c["VEq"])
                        ins = v.tensor_tensor(qr[:, :, 1, :], t1v, t2v, op=ALU.add)
                        K.inc(ins, "VEq")
                        nrm = SM[:, 32 + 16 * (t % 2):48 + 16 * (t % 2)]
                        for hh in range(2):
                            v.wait_ge(K.sems["VEq"], K.c["VEq"])
                            ins = v.tensor_tensor(T1, sl8 := q32s(t)[:, hh * 512:(hh + 1) * 512], sl8, op=ALU.mult)
                            K.inc(ins, "VEq")
                            v.wait_ge(K.sems["VEq"], K.c["VEq"])
                            ins = v.reduce_sum(nrm[:, hh * 8:(hh + 1) * 8],
                                               T1.rearrange("p (h c) -> p h c", c=64), axis=AXX)
                            if hh == 0:
                                K.inc(ins, "VEq")
                        K.inc(ins, "QR")
                        if t >= 1:
                            _norm_and_qt_evac(K, v, t - 1, b_nrt, b_trt, ps_n, ps_tr,
                                              SM, runslot, tdst)
                    _norm_and_qt_evac(K, v, TT - 1, b_nrt, b_trt, ps_n, ps_tr,
                                      SM, runslot, tdst)
        barrier(K)


def _norm_and_qt_evac(K, v, tp, b_nrt, b_trt, ps_n, ps_tr, SM, runslot, tdst):
    v.wait_ge(K.sems["NRT"], b_nrt + tp + 1)
    ins = v.reduce_max(SM[0:16, 70:71], ps_n[0:16, :], axis=AXX)
    K.inc(ins, "NRM")
    v.wait_ge(K.sems["NRM"], K.c["NRM"])
    ins = v.tensor_tensor(SM[0:16, runslot:runslot + 1],
                          SM[0:16, runslot:runslot + 1],
                          SM[0:16, 70:71], op=ALU.max)
    K.inc(ins, "NRM")
    v.wait_ge(K.sems["TRT"], b_trt + tp + 1)
    ins = v.tensor_copy(tdst(tp), ps_tr[:, tp % 2, :, :])
    K.inc(ins, "TRE")



def _emit_negM(K, st, layer):
    nc = K.nc
    SM, SMB, ONEB = st["SM"], st["SMB"], st["ONEB"]
    sc_park = st["sc_park"]
    b_sy = K.c.get("NMS", 0); K.sem("NMS")
    b_ve = K.c.get("NMV", 0); K.sem("NMV")
    with (nc.psum_tensor("ps_nm_l%d" % layer, [128, 16], F32) as ps_nm,
          nc.Block("negM%d" % layer) as b):
        @b.sync
        def _(sy):
            if layer == 0:
                ins = sy.dma_start(out=sc_park[0:16, 0:2], in_=SM[0:16, 64:66],
                                   allow_slow_non_contiguous=True)
                K.inc(ins, "NMS", 16)
                sy.wait_ge(K.sems["NMS"], b_sy + 16)
                ins = sy.dma_start(out=SM[0:1, 80:96],
                                   in_=sc_park[0:16, 0:1].rearrange("p c -> c p"),
                                   allow_slow_non_contiguous=True)
                K.inc(ins, "NMS", 16)
                ins = sy.dma_start(out=SM[0:1, 96:112],
                                   in_=sc_park[0:16, 1:2].rearrange("p c -> c p"),
                                   allow_slow_non_contiguous=True)
                K.inc(ins, "NMS", 16)
            else:
                ins = sy.dma_start(out=sc_park[0:2, 0:8], in_=SM[0:2, 64:72],
                                   allow_slow_non_contiguous=True)
                K.inc(ins, "NMS", 16)
                ins = sy.dma_start(out=sc_park[0:4, 8:9], in_=SM[0:4, 72:73],
                                   allow_slow_non_contiguous=True)
                K.inc(ins, "NMS", 16)
                sy.wait_ge(K.sems["NMS"], b_sy + 32)
                ins = sy.dma_start(out=SM[0:1, 80:96].rearrange("a (c p) -> a c p", c=8),
                                   in_=sc_park[0:2, 0:8].rearrange("p c -> c p").unsqueeze(0),
                                   allow_slow_non_contiguous=True)
                K.inc(ins, "NMS", 16)
                ins = sy.dma_start(out=SM[0:1, 96:100],
                                   in_=sc_park[0:4, 8:9].rearrange("p c -> c p"),
                                   allow_slow_non_contiguous=True)
                K.inc(ins, "NMS", 16)

        @b.vector
        def _(v):
            v.wait_ge(K.sems["NMS"], K.c["NMS"])
            if layer == 0:
                ins = v.tensor_tensor(SM[0:1, 80:96], SM[0:1, 80:96], SM[0:1, 96:112], op=ALU.mult)
            else:
                a = SM[0:1, 80:96].rearrange("a (g r) -> a g r", g=4)
                nk = SM[0:1, 96:100].rearrange("a (g r) -> a g r", g=4).broadcast_to([1, 4, 4])
                ins = v.tensor_tensor(a, a, nk, op=ALU.mult)
            K.inc(ins, "NMV")

        @b.scalar
        def _(s):
            s.wait_ge(K.sems["NMV"], b_ve + 1)
            ins = s.activation(SM[0:1, 80:96], SM[0:1, 80:96], AF.Sqrt)
            K.inc(ins, "NMV")

        @b.vector
        def _(v):
            v.wait_ge(K.sems["NMV"], b_ve + 2)
            ins = v.tensor_scalar(SM[0:1, 80:96], SM[0:1, 80:96], -0.125, -0.25,
                                  op0=ALU.mult, op1=ALU.add)
            K.inc(ins, "NMV")
            v.wait_ge(K.sems["NMV"], b_ve + 3)
            ins = v.tensor_copy(SMB[0:1, 1024:1040], SM[0:1, 80:96])
            K.inc(ins, "NMV")

        @b.tensor
        def _(te):
            te.wait_ge(K.sems["NMV"], b_ve + 4)
            ins = te.matmul(ps_nm[:], ONEB[0:1, 0:128], SMB[0:1, 1024:1040],
                            start=True, stop=True)
            K.inc(ins, "NMV")

        @b.scalar
        def _(s):
            s.wait_ge(K.sems["NMV"], b_ve + 5)
            ins = s.copy(SM[:, 112:128], ps_nm[:])
            K.inc(ins, "NMV")



def _l1_projections(K, st, L, lnT, qT, kTp, ktokbuf, vext, wqt, wkt, wvt):
    nc = K.nc
    TT = st["TT"]
    AB, AC, WB, S32, SM = st["AB"], st["AC"], st["WB"], st["S32"], st["SM"]
    IDN, IDF, HCF = st["IDN"], st["IDF"], st["HCF"]
    inp = st["inp"]
    T1 = S32[:, 2048:2560]
    T2 = S32[:, 2560:3072]

    b_wl = K.c.get("WL", 0); K.sem("WL")
    b_wtl = [K.c.get("WTL0", 0), K.c.get("WTL1", 0)]; K.sem("WTL0"); K.sem("WTL1")
    b_wtt = K.c.get("WTT", 0); K.sem("WTT")
    b_wte = K.c.get("WTE", 0); K.sem("WTE")
    plan = [("Wq", 8, wqt, 1024), ("Wk", 2, wkt, 256), ("Wv", 2, wvt, 256)]
    with (nc.psum_tensor("ps_wt1", [128, 2, 8, 128], BF16) as ps_wt,
          nc.Block("l1wt") as b):
        @b.gpsimd
        def _(gp):
            i = 0
            for name, nrc, dstf, ncol in plan:
                for rc in range(nrc):
                    if i >= 2:
                        gp.wait_ge(K.sems["WTT"], b_wtt + i - 1)
                    ins = gp.dma_start(out=WB[:, (i % 2) * 1024:((i % 2) + 1) * 1024],
                                       in_=inp[L + name][rc * 128:(rc + 1) * 128, :])
                    K.inc(ins, "WTL" + str(i % 2), 16)
                    i += 1

        @b.tensor
        def _(te):
            i = 0
            for name, nrc, dstf, ncol in plan:
                for rc in range(nrc):
                    te.wait_ge(K.sems["WTL" + str(i % 2)], b_wtl[i % 2] + 16 * (i // 2 + 1))
                    if i >= 2:
                        te.wait_ge(K.sems["WTE"], b_wte + i - 1)
                    for dc in range(8):
                        ins = te.transpose(ps_wt[:, i % 2, dc, :],
                                           WB[:, (i % 2) * 1024 + dc * 128:(i % 2) * 1024 + (dc + 1) * 128],
                                           IDN[:])
                    K.inc(ins, "WTT")
                    i += 1

        @b.vector
        def _(v):
            i = 0
            for name, nrc, dstf, ncol in plan:
                for rc in range(nrc):
                    v.wait_ge(K.sems["WTT"], b_wtt + i + 1)
                    for dc in range(8):
                        ins = v.tensor_copy(dstf(dc)[:, rc * 128:(rc + 1) * 128],
                                            ps_wt[:, i % 2, dc, :])
                    K.inc(ins, "WTE")
                    i += 1
    barrier(K)

    b_qp = K.c.get("QP", 0); K.sem("QP")
    b_qev = K.c.get("QEv", 0); K.sem("QEv")
    b_ns = K.c.get("NS", 0); K.sem("NS")
    b_nmm = K.c.get("NMM", 0); K.sem("NMM")
    b_nrm = K.c.get("NRM", 0); K.sem("NRM")
    NQ = (TT * 128) // 512 if TT >= 4 else 1
    qw = 512 if TT >= 4 else TT * 128
    iters = [(dco, n) for dco in range(8) for n in range(NQ)]
    with (nc.psum_tensor("ps_q1", [128, 2, 512], F32) as ps_q,
          nc.psum_tensor("ps_nn", [2, 2, 512], F32) as ps_nn,
          nc.Block("l1q") as b):
        @b.tensor
        def _(te):
            for i, (dco, n) in enumerate(iters):
                if i >= 2:
                    te.wait_ge(K.sems["QEv"], b_qev + i - 1)
                for dci in range(8):
                    ins = te.matmul(ps_q[:, i % 2, 0:qw],
                                    wqt(dci)[:, dco * 128:(dco + 1) * 128],
                                    lnT(dci)[:, n * qw:(n + 1) * qw],
                                    start=(dci == 0), stop=(dci == 7))
                K.inc(ins, "QP")
                if i >= 1:
                    te.wait_ge(K.sems["NS"], b_ns + i)
                    if i >= 3:
                        te.wait_ge(K.sems["NRM"], b_nrm + 2 * (i - 2))
                    ins = te.matmul(ps_nn[:, (i - 1) % 2, 0:qw], HCF[:],
                                    (T1 if (i - 1) % 2 == 0 else T2)[:, 0:qw],
                                    start=True, stop=True)
                    K.inc(ins, "NMM")
            i = len(iters)
            te.wait_ge(K.sems["NS"], b_ns + i)
            te.wait_ge(K.sems["NRM"], b_nrm + 2 * (i - 2))
            ins = te.matmul(ps_nn[:, (i - 1) % 2, 0:qw], HCF[:],
                            (T1 if (i - 1) % 2 == 0 else T2)[:, 0:qw],
                            start=True, stop=True)
            K.inc(ins, "NMM")

        @b.scalar
        def _(s):
            for i, (dco, n) in enumerate(iters):
                s.wait_ge(K.sems["QP"], b_qp + i + 1)
                dst = qT(dco)[:, n * qw:(n + 1) * qw]
                ins = s.copy(dst, ps_q[:, i % 2, 0:qw])
                K.inc(ins, "QEv")
                if i >= 2:
                    s.wait_ge(K.sems["NMM"], b_nmm + i - 1)
                s.wait_ge(K.sems["QEv"], K.c["QEv"])
                ins = s.activation((T1 if i % 2 == 0 else T2)[:, 0:qw], dst, AF.Square)
                K.inc(ins, "NS")

        @b.vector
        def _(v):
            for i, (dco, n) in enumerate(iters):
                v.wait_ge(K.sems["NMM"], b_nmm + i + 1)
                v.wait_ge(K.sems["NRM"], K.c["NRM"])
                ins = v.reduce_max(SM[0:2, 70:71], ps_nn[:, i % 2, 0:qw], axis=AXX)
                K.inc(ins, "NRM")
                v.wait_ge(K.sems["NRM"], K.c["NRM"])
                ins = v.tensor_tensor(SM[0:2, 64 + dco:65 + dco],
                                      SM[0:2, 64 + dco:65 + dco],
                                      SM[0:2, 70:71], op=ALU.max)
                K.inc(ins, "NRM")
    barrier(K)

    b_kp = K.c.get("KP", 0); K.sem("KP"); K.sem("VEq")
    b_ke = K.c.get("KE", 0); K.sem("KE")
    b_kr = K.c.get("KR", 0); K.sem("KR")
    b_ktt = K.c.get("KTT", 0); K.sem("KTT")
    b_kte = K.c.get("KTE", 0); K.sem("KTE")
    b_nrt = K.c.get("NRT", 0); K.sem("NRT")
    b_nrm2 = K.c.get("NRM", 0)
    gp_pairs = [(j, g, p) for j in range(5) for g in range(4) for p in range(2)
                if 1 + g - p == j]
    with (nc.psum_tensor("ps_k1", [128, 2, 512], F32) as ps_k,
          nc.psum_tensor("ps_kt", [128, 2, 8, 128], BF16) as ps_kt,
          nc.psum_tensor("ps_n1", [16, 128], F32) as ps_n,
          nc.Block("l1k") as b):
        @b.tensor
        def _(te):
            for t in range(TT):
                if t >= 2:
                    te.wait_ge(K.sems["KE"], b_ke + t - 1)
                for dci in range(8):
                    ins = te.matmul(ps_k[:, t % 2, 0:256],
                                    lnT(dci)[:, t * 128:(t + 1) * 128],
                                    wkt(dci),
                                    start=(dci == 0), stop=(dci == 7))
                K.inc(ins, "KP")
                if t >= 1:
                    _l1k_transposes(K, te, t - 1, ktokbuf, ps_kt, ps_n, SM, IDN, IDF,
                                    b_kr, b_kte, b_nrm2)
            _l1k_transposes(K, te, TT - 1, ktokbuf, ps_kt, ps_n, SM, IDN, IDF,
                            b_kr, b_kte, b_nrm2)

        @b.scalar
        def _(s):
            for t in range(TT):
                s.wait_ge(K.sems["KP"], b_kp + t + 1)
                if t >= 2:
                    s.wait_ge(K.sems["KR"], b_kr + t - 1)
                ins = s.copy(ktokbuf(t % 2)[:, 64:320], ps_k[:, t % 2, 0:256])
                K.inc(ins, "KE")

        @b.vector
        def _(v):
            for t in range(TT):
                v.wait_ge(K.sems["KE"], b_ke + t + 1)
                if t >= 1:
                    v.wait_ge(K.sems["KR"], b_kr + t)
                sl = ktokbuf(t % 2)[:, 64:320]
                ins = v.tensor_tensor(T1[:, 0:256], sl, sl, op=ALU.mult)
                K.inc(ins, "VEq")
                v.wait_ge(K.sems["VEq"], K.c["VEq"])
                ins = v.reduce_sum(SM[:, 32 + 16 * (t % 2):36 + 16 * (t % 2)],
                                   T1[:, 0:256].rearrange("p (g c) -> p g c", c=64),
                                   axis=AXX)
                K.inc(ins, "KR")
                if t >= 1:
                    _l1k_nrm_evac(K, v, t - 1, ps_kt, ps_n, kTp, SM, gp_pairs,
                                  b_ktt, b_nrt)
            _l1k_nrm_evac(K, v, TT - 1, ps_kt, ps_n, kTp, SM, gp_pairs,
                          b_ktt, b_nrt)
    barrier(K)

    b_vp = K.c.get("VP", 0); K.sem("VP")
    b_vev = K.c.get("VEv", 0); K.sem("VEv")
    with (nc.psum_tensor("ps_v1", [128, 2, 512], F32) as ps_v,
          nc.Block("l1v") as b):
        @b.tensor
        def _(te):
            for t in range(TT):
                if t >= 2:
                    te.wait_ge(K.sems["VEv"], b_vev + t - 1)
                for dci in range(8):
                    ins = te.matmul(ps_v[:, t % 2, 0:256],
                                    lnT(dci)[:, t * 128:(t + 1) * 128],
                                    wvt(dci),
                                    start=(dci == 0), stop=(dci == 7))
                K.inc(ins, "VP")

        @b.scalar
        def _(s):
            for t in range(TT):
                s.wait_ge(K.sems["VP"], b_vp + t + 1)
                dst = vext(t).rearrange("p (g w) -> p g w", w=65)[:, :, 0:64]
                ins = s.copy(dst, ps_v[:, t % 2, 0:256].rearrange("p (g w) -> p g w", w=64))
                K.inc(ins, "VEv")
    barrier(K)


def _l1k_transposes(K, te, tp, ktokbuf, ps_kt, ps_n, SM, IDN, IDF, b_kr, b_kte, b_nrm2):
    te.wait_ge(K.sems["KR"], b_kr + tp + 1)
    te.wait_ge(K.sems["NRM"], b_nrm2 + 2 * tp)
    ins = te.transpose(ps_n[:], SM[:, 32 + 16 * (tp % 2):48 + 16 * (tp % 2)], IDF[:])
    K.inc(ins, "NRT")
    if tp >= 2:
        te.wait_ge(K.sems["KTE"], b_kte + tp - 1)
    for j in range(5):
        ins = te.transpose(ps_kt[:, tp % 2, j, :],
                           ktokbuf(tp % 2)[:, j * 64:j * 64 + 128], IDN[:])
    K.inc(ins, "KTT")


def _l1k_nrm_evac(K, v, tp, ps_kt, ps_n, kTp, SM, gp_pairs, b_ktt, b_nrt):
    v.wait_ge(K.sems["NRT"], b_nrt + tp + 1)
    ins = v.reduce_max(SM[0:4, 70:71], ps_n[0:4, :], axis=AXX)
    K.inc(ins, "NRM")
    v.wait_ge(K.sems["NRM"], K.c["NRM"])
    ins = v.tensor_tensor(SM[0:4, 72:73], SM[0:4, 72:73], SM[0:4, 70:71], op=ALU.max)
    K.inc(ins, "NRM")
    v.wait_ge(K.sems["KTT"], b_ktt + tp + 1)
    for j, g, p in gp_pairs:
        ins = v.tensor_copy(kTp(g)[64 * p:64 * p + 64, tp * 128:(tp + 1) * 128],
                            ps_kt[64 * p:64 * p + 64, tp % 2, j, :])
    K.inc(ins, "KTE")



def emit_moe(K, st):
    nc = K.nc
    TT = st["TT"]; T = st["T"]
    AX, AA, AB, AC = st["AX"], st["AA"], st["AB"], st["AC"]
    WB, S32, SM, SMB, ONEB = st["WB"], st["S32"], st["SM"], st["SMB"], st["ONEB"]
    inp = st["inp"]
    NTT = max(1, T // 256)
    nsub = 2 if T >= 256 else 1

    def lnT(c):
        return AA[:, c * 2048:(c + 1) * 2048]

    emit_ln_lnT(K, st, park_x=False)

    b_rl = K.c.get("RL", 0); K.sem("RL")
    b_rt = K.c.get("RTT", 0); K.sem("RTT")
    b_rp = K.c.get("RP", 0); K.sem("RP")
    b_rv = K.c.get("RV", 0); K.sem("RV")
    b_rs = K.c.get("RS", 0); K.sem("RS")
    with (nc.psum_tensor("ps_rt", [128, 2, 512], F32) as ps_rt,
          nc.Block("router") as b):
        @b.vector
        def _(v):
            ins = v.memset(WB[0:16, 4224:5248], 0.0)
            K.inc(ins, "RMZ")

        @b.gpsimd
        def _(gp):
            gp.wait_ge(K.sem("RMZ"), K.c["RMZ"])
            ins = gp.dma_start(out=WB[0:8, 4224:5248], in_=inp["L0_ffn_router_w"])
            K.inc(ins, "RL", 16)

        @b.sync
        def _(sy):
            sy.wait_ge(K.sems["RL"], b_rl + 16)
            for c in range(8):
                ins = sy.dma_start_transpose(WB[:, 4096 + c * 16:4096 + (c + 1) * 16],
                                             WB[0:16, 4224 + c * 128:4224 + (c + 1) * 128])
                K.inc(ins, "RTT", 16)

        @b.tensor
        def _(te):
            te.wait_ge(K.sems["RTT"], b_rt + 128)
            for t in range(TT):
                if t >= 2:
                    te.wait_ge(K.sems["RV"], b_rv + 16 * (t - 2) + 1)
                for c in range(8):
                    ins = te.matmul(ps_rt[:, t % 2, 0:8],
                                    lnT(c)[:, t * 128:(t + 1) * 128],
                                    WB[:, 4096 + c * 16:4096 + c * 16 + 8],
                                    start=(c == 0), stop=(c == 7))
                K.inc(ins, "RP")

        @b.vector
        def _(v):
            for t in range(TT):
                v.wait_ge(K.sems["RP"], b_rp + t + 1)
                lg = SM[:, 256:264]; p = SM[:, 264:272]; pw = SM[:, 272:280]
                c0 = SM[:, 280:281]; c1 = SM[:, 281:282]; c2 = SM[:, 282:283]
                c3 = SM[:, 283:284]; c4 = SM[:, 284:285]
                if t > 0:
                    v.wait_ge(K.sems["RV"], K.c["RV"])
                ins = v.tensor_copy(lg, ps_rt[:, t % 2, 0:8]); K.inc(ins, "RV")
                v.wait_ge(K.sems["RV"], K.c["RV"])
                ins = v.reduce_max(c0, lg, axis=AXX); K.inc(ins, "RV")
                v.wait_ge(K.sems["RV"], K.c["RV"])
                ins = v.tensor_scalar(c0, c0, -1.0, None, op0=ALU.mult); K.inc(ins, "RV")
                v.wait_ge(K.sems["RS"], b_rs + t + 1)
                ins = v.reciprocal(c2, c1); K.inc(ins, "RV")
                v.wait_ge(K.sems["RV"], K.c["RV"])
                ins = v.tensor_scalar(p, p, c2, None, op0=ALU.mult); K.inc(ins, "RV")
                v.wait_ge(K.sems["RV"], K.c["RV"])
                ins = v.reduce_max(c3, p, axis=AXX); K.inc(ins, "RV")
                v.wait_ge(K.sems["RV"], K.c["RV"])
                ins = v.tensor_scalar(pw, p, c3, None, op0=ALU.is_ge); K.inc(ins, "RV")
                v.wait_ge(K.sems["RV"], K.c["RV"])
                ins = v.tensor_tensor(lg, p, pw, op=ALU.subtract); K.inc(ins, "RV")
                v.wait_ge(K.sems["RV"], K.c["RV"])
                ins = v.reduce_max(c4, lg, axis=AXX); K.inc(ins, "RV")
                v.wait_ge(K.sems["RV"], K.c["RV"])
                ins = v.tensor_scalar(p, lg, c4, None, op0=ALU.is_ge); K.inc(ins, "RV")
                v.wait_ge(K.sems["RV"], K.c["RV"])
                ins = v.tensor_tensor(c0, c3, c4, op=ALU.add); K.inc(ins, "RV")
                v.wait_ge(K.sems["RV"], K.c["RV"])
                ins = v.reciprocal(c1, c0); K.inc(ins, "RV")
                v.wait_ge(K.sems["RV"], K.c["RV"])
                ins = v.tensor_tensor(c3, c3, c1, op=ALU.mult); K.inc(ins, "RV")
                v.wait_ge(K.sems["RV"], K.c["RV"])
                ins = v.tensor_tensor(c4, c4, c1, op=ALU.mult); K.inc(ins, "RV")
                v.wait_ge(K.sems["RV"], K.c["RV"])
                ins = v.tensor_scalar(pw, pw, c3, None, op0=ALU.mult); K.inc(ins, "RV")
                v.wait_ge(K.sems["RV"], K.c["RV"])
                ins = v.scalar_tensor_tensor(SM[:, 128 + 8 * t:136 + 8 * t],
                                             p, c4, pw, op0=ALU.mult, op1=ALU.add)
                K.inc(ins, "RV")

        @b.scalar
        def _(s):
            for t in range(TT):
                s.wait_ge(K.sems["RV"], b_rv + 16 * t + 3)
                ins = s.activation(SM[:, 264:272], SM[:, 256:264], AF.Exp,
                                   bias=SM[:, 280:281], accum_out=SM[:, 281:282])
                K.inc(ins, "RS")
    barrier(K)

    b_fcl = K.c.get("FCL", 0); K.sem("FCL")
    b_wpl = K.c.get("WPL", 0); K.sem("WPL")
    b_hp = K.c.get("HP", 0); K.sem("HP")
    b_hg = K.c.get("HG", 0); K.sem("HG")
    b_yp = K.c.get("YP", 0); K.sem("YP")
    b_ye = K.c.get("YE", 0); K.sem("YE")
    b_bf = K.c.get("BFC", 0); K.sem("BFC")
    b_bft = K.c.get("BFT", 0); K.sem("BFT")
    b_bfv = K.c.get("BFV", 0); K.sem("BFV")
    b_bpj = [K.c.get("BPJ0", 0), K.c.get("BPJ1", 0)]; K.sem("BPJ0"); K.sem("BPJ1")
    b_ye2 = K.c.get("YE2", 0); K.sem("YE2")

    NH2 = 2
    JH = 16
    ehalves = [(e, h) for e in range(NE) for h in range(NH2)]

    def bfc_cols(e):
        return SM[:, 32 + 32 * (e % 2):64 + 32 * (e % 2)]

    def bprojB(e):
        return WB[:, 5248 + 1024 * (e % 2):5248 + 1024 * (e % 2) + 1024]

    with (nc.psum_tensor("ps_a", [128, 2, 512], F32) as ps_a,
          nc.psum_tensor("ps_y", [128, nsub, 1024], F32) as ps_y,
          nc.Block("experts") as b):

        @b.gpsimd
        def _(gp):
            for i, (e, h) in enumerate(ehalves):
                for j in range(JH):
                    if i >= 1:
                        gp.wait_ge(K.sems["HG"], b_hg + (i - 1) * NTT * JH + (NTT - 1) * JH + j + 1)
                    hc0 = (16 * h + j) * 128
                    ins = gp.dma_start(
                        out=AB[:, j * 1024:(j + 1) * 1024].rearrange("p (dc f) -> p dc f", dc=8),
                        in_=inp["L0_ffn_w_fc"][e, :, hc0:hc0 + 128].rearrange("(dc p) f -> p dc f", p=128))
                    K.inc(ins, "FCL", 16)
                for j in range(JH):
                    if i >= 1:
                        gp.wait_ge(K.sems["YE"], b_ye + i * NTT * nsub)
                    hc0 = (16 * h + j) * 128
                    ins = gp.dma_start(out=AC[:, j * 1024:(j + 1) * 1024],
                                       in_=inp["L0_ffn_w_proj"][e, hc0:hc0 + 128, :])
                    K.inc(ins, "WPL", 16)
                if h == 0:
                    if e >= 1:
                        gp.wait_ge(K.sems["BFT"], b_bft + 16 * e)
                    ins = gp.dma_start(out=WB[0:32, 7296:7424],
                                       in_=inp["L0_ffn_b_fc"][e].rearrange("(a b) -> a b", a=32))
                    K.inc(ins, "BFC", 16)
                    if e >= 2:
                        gp.wait_ge(K.sems["YE2"], b_ye2 + TT * (e - 1))
                    ins = gp.dma_start(out=bprojB(e),
                                       in_=inp["L0_ffn_b_proj"][e:e + 1, :].broadcast_to([128, 1024]))
                    K.inc(ins, "BPJ" + str(e % 2), 16)

        @b.sync
        def _(sy):
            for e in range(NE):
                sy.wait_ge(K.sems["BFC"], b_bf + 16 * (e + 1))
                if e >= 2:
                    sy.wait_ge(K.sems["BFV"], b_bfv + e - 1)
                ins = sy.dma_start_transpose(SMB[:, 32 * (e % 2):32 * (e % 2) + 32],
                                             WB[0:32, 7296:7424])
                K.inc(ins, "BFT", 16)

        @b.vector
        def _(v):
            for e in range(NE):
                v.wait_ge(K.sems["BFT"], b_bft + 16 * (e + 1))
                ins = v.tensor_copy(bfc_cols(e), SMB[:, 32 * (e % 2):32 * (e % 2) + 32])
                K.inc(ins, "BFV")
                for h in range(NH2):
                    i = e * 2 + h
                    for tt in range(NTT):
                        v.wait_ge(K.sems["YP"], b_yp + i * NTT + tt + 1)
                        for sub in range(nsub):
                            tile = tt * nsub + sub
                            ins = v.scalar_tensor_tensor(
                                AX[:, tile * 1024:(tile + 1) * 1024],
                                ps_y[:, sub, :],
                                SM[:, 128 + 8 * tile + e:129 + 8 * tile + e],
                                AX[:, tile * 1024:(tile + 1) * 1024],
                                op0=ALU.mult, op1=ALU.add)
                            K.inc(ins, "YE")
                        v.wait_ge(K.sems["YE"], K.c["YE"])
                v.wait_ge(K.sems["BPJ" + str(e % 2)], b_bpj[e % 2] + 16 * (e // 2 + 1))
                for tile in range(TT):
                    ins = v.scalar_tensor_tensor(
                        AX[:, tile * 1024:(tile + 1) * 1024],
                        bprojB(e),
                        SM[:, 128 + 8 * tile + e:129 + 8 * tile + e],
                        AX[:, tile * 1024:(tile + 1) * 1024],
                        op0=ALU.mult, op1=ALU.add)
                    K.inc(ins, "YE2")
                    v.wait_ge(K.sems["YE2"], K.c["YE2"])

        @b.tensor
        def _(te):
            for i, (e, h) in enumerate(ehalves):
                for tt in range(NTT):
                    if tt == 0:
                        te.wait_ge(K.sems["FCL"], b_fcl + 16 * JH * (i + 1))
                    for j in range(JH):
                        it = tt * JH + j
                        if it >= 2 or i > 0:
                            te.wait_ge(K.sems["HG"], b_hg + i * NTT * JH + max(0, it - 1))
                        for dci in range(8):
                            ins = te.matmul(ps_a[:, it % 2, 0:256],
                                            AB[:, j * 1024 + dci * 128:j * 1024 + (dci + 1) * 128],
                                            lnT(dci)[:, tt * 256:tt * 256 + 256],
                                            start=(dci == 0), stop=(dci == 7))
                        K.inc(ins, "HP")
                    te.wait_ge(K.sems["HG"], b_hg + i * NTT * JH + (tt + 1) * JH)
                    if i * NTT + tt >= 1:
                        te.wait_ge(K.sems["YE"], b_ye + (i * NTT + tt) * nsub)
                    if tt == 0:
                        te.wait_ge(K.sems["WPL"], b_wpl + 16 * JH * (i + 1))
                    for j in range(JH):
                        for sub in range(nsub):
                            for n2 in range(2):
                                ins = te.matmul(ps_y[:, sub, n2 * 512:(n2 + 1) * 512],
                                                WB[:, j * 256 + sub * 128:j * 256 + sub * 128 + 128],
                                                AC[:, j * 1024 + n2 * 512:j * 1024 + (n2 + 1) * 512],
                                                start=(j == 0), stop=(j == JH - 1))
                    K.inc(ins, "YP")

        @b.scalar
        def _(s):
            for i, (e, h) in enumerate(ehalves):
                if h == 0:
                    s.wait_ge(K.sems["BFV"], b_bfv + e + 1)
                for tt in range(NTT):
                    for j in range(JH):
                        it = tt * JH + j
                        s.wait_ge(K.sems["HP"], b_hp + i * NTT * JH + it + 1)
                        if tt > 0 or i > 0:
                            s.wait_ge(K.sems["YP"], b_yp + i * NTT + tt)
                        ins = s.activation(WB[:, j * 256:(j + 1) * 256],
                                           ps_a[:, it % 2, 0:256], AF.Gelu_apprx_tanh,
                                           bias=bfc_cols(e)[:, j + 16 * h:j + 16 * h + 1])
                        K.inc(ins, "HG")
    barrier(K)



SF_MAX = 99

def emit_slotffn(K, st):
    nc = K.nc
    TT = st["TT"]; T = st["T"]
    AX, AXb, AA, AB, AC = st["AX"], st["AXb"], st["AA"], st["AB"], st["AC"]
    WB, S32, SM, SMB, ONEB = st["WB"], st["S32"], st["SM"], st["SMB"], st["ONEB"]
    IDN, IDF = st["IDN"], st["IDF"]
    inp, x_park, sc_park = st["inp"], st["x_park"], st["sc_park"]
    P = "L1_ffn_"

    def lnT(c):
        return AA[:, c * 2048:(c + 1) * 2048]

    alphaT = AXb[:, 0:2048]
    svp = AXb[:, 2048:3072]
    vslp = AXb[:, 3072:4096]

    def hTs(hc):
        return AXb[:, 4096 + hc * 2048:4096 + (hc + 1) * 2048]

    probsT = AXb[:, 12288:12352]
    S0T = AXb[:, 12416:12928].rearrange("p (c s) -> p c s", s=64)
    SnTb = AXb[:, 12928:13440].rearrange("p (c s) -> p c s", s=64)
    ynb = AXb[:, 13440:13952].rearrange("p (c s) -> p c s", s=64)
    siT = AXb[:, 13952:14464].rearrange("p (c s) -> p c s", s=64)

    def wtb(par):
        return AXb[:, 14464 + par * 1024:14464 + (par + 1) * 1024]

    def alpha_t(t):
        return AXb[:, 16576 + t * 128:16576 + t * 128 + 128]

    qsT = AXb[:, 8448:8576].rearrange("p (c s) -> p c s", s=64)
    ksT = AXb[:, 8576:8704].rearrange("p (c s) -> p c s", s=64)
    s0stage = AXb[0:64, 4096:5120]
    sistage = AXb[0:64, 5120:6144]
    pstage = AXb[0:64, 6144:6272]
    giT = AX[:, 9312:10848].rearrange("p (c s) -> p c s", s=64)
    ghT = AX[:, 10848:12384].rearrange("p (c s) -> p c s", s=64)
    SnT = AX[:, 12384:12896].rearrange("p (c s) -> p c s", s=64)
    yf = AX[:, 12896:13408].rearrange("p (c s) -> p c s", s=64)
    ynf = AX[:, 13408:13920].rearrange("p (c s) -> p c s", s=64)
    S0f = AX[:, 15872:16384].rearrange("p (c s) -> p c s", s=64)
    ScT = AX[:, 15360:15872].rearrange("p (c s) -> p c s", s=64)
    sif = AX[:, 14336:15360]

    def wq_c(dc):
        return AC[:, dc * 256:(dc + 1) * 256]

    def wk_c(dc):
        return AC[:, 2048 + dc * 256:2048 + (dc + 1) * 256]

    def wbig(dc):
        return AC[:, 4096 + dc * 1024:4096 + (dc + 1) * 1024]

    emit_ln_lnT(K, st, park_x=True,
                sink=lambda t: AB[:, t * 1024:(t + 1) * 1024],
                sink_persistent=True)

    b_ld = K.c.get("LF", 0); K.sem("LF")
    with nc.Block("sfprep") as b:
        @b.vector
        def _(v):
            v.memset(alphaT[64:128, :], 0.0)
            v.memset(svp[64:128, :], 0.0)
            v.memset(vslp[64:128, :], 0.0)
            v.memset(probsT[64:128, :], 0.0)
            v.memset(AXb[:, 16576:18624], 0.0)
            v.memset(AXb[0:64, 6144:6272], 0.0)
            ins = v.memset(SM[:, 1279:1280], 1.0)
            K.inc(ins, "VE")

        @b.gpsimd
        def _(gp):
            for c in range(8):
                ins = gp.dma_start(out=WB[:, c * 512:(c + 1) * 512],
                                   in_=inp[P + "w1"][c * 128:(c + 1) * 128, :])
                K.inc(ins, "LF", 16)
            for c in range(8):
                ins = gp.dma_start(out=WB[:, 4096 + c * 512:4096 + (c + 1) * 512],
                                   in_=inp[P + "w1"][1024 + c * 128:1024 + (c + 1) * 128, :])
                K.inc(ins, "LF", 16)
            for c in range(4):
                ins = gp.dma_start(out=SMB[:, c * 64:(c + 1) * 64],
                                   in_=inp[P + "w2"][c * 128:(c + 1) * 128, :])
                K.inc(ins, "LF", 16)
            ins = gp.dma_start(out=s0stage, in_=inp[P + "slot_init"])
            K.inc(ins, "LF", 16)
            ins = gp.dma_start(out=SMB[0:1, 256:768], in_=inp[P + "b1"].rearrange("(a d) -> a d", a=1))
            K.inc(ins, "LF", 16)
            ins = gp.dma_start(out=SMB[0:1, 768:832], in_=inp[P + "b2"].rearrange("(a d) -> a d", a=1))
            K.inc(ins, "LF", 16)
            ins = gp.dma_start(out=SMB[0:1, 832:1856], in_=inp[P + "bv"].rearrange("(a d) -> a d", a=1))
            K.inc(ins, "LF", 16)
            ins = gp.dma_start(out=SM[0:1, 300:301], in_=inp[P + "tau"].rearrange("(a d) -> a d", a=1))
            K.inc(ins, "LF", 16)
            for dc in range(8):
                ins = gp.dma_start(out=wq_c(dc), in_=inp[P + "wq"][dc * 128:(dc + 1) * 128, :])
                K.inc(ins, "LF", 16)
            for dc in range(8):
                ins = gp.dma_start(out=wk_c(dc), in_=inp[P + "wk"][dc * 128:(dc + 1) * 128, :])
                K.inc(ins, "LF", 16)
            for dc in range(8):
                ins = gp.dma_start(out=wbig(dc), in_=inp[P + "wv"][dc * 128:(dc + 1) * 128, :])
                K.inc(ins, "LF", 16)

        @b.sync
        def _(sy):
            sy.wait_ge(K.sem("LF"), K.c["LF"])
            for c in range(8):
                ins = sy.dma_start_transpose(S0T[:, c, :], s0stage[:, c * 128:(c + 1) * 128])
                K.inc(ins, "LFS", 16)
            ins = sy.dma_start(out=sc_park[0:1, 16:17], in_=SM[0:1, 300:301])
            K.inc(ins, "LFS", 16)
    barrier(K)

    bc_plan = [("bih", 302, 24), ("bhh", 326, 24), ("ln_g", 350, 8),
               ("ln_b", 358, 8), ("bq", 366, 2), ("bk", 368, 2)]
    bstage = AXb[0:32, 6400:6528]
    btr = AXb[:, 6528:6560]
    K.sem("BCL"); K.sem("BCT"); K.sem("BCV"); K.sem("BCM")
    b_bcl = K.c["BCL"]; b_bct = K.c["BCT"]; b_bcv = K.c["BCV"]
    with nc.Block("sfbcols") as b:
        @b.vector
        def _(v):
            ins = v.memset(AXb[0:32, 6400:6528], 0.0)
            K.inc(ins, "BCM")
            for i, (nm, col0, ncol) in enumerate(bc_plan):
                v.wait_ge(K.sems["BCT"], b_bct + 16 * (i + 1))
                ins = v.tensor_copy(SM[:, col0:col0 + ncol], btr[:, 0:ncol])
                K.inc(ins, "BCV")

        @b.gpsimd
        def _(gp):
            gp.wait_ge(K.sems["BCM"], K.c["BCM"])
            for i, (nm, col0, ncol) in enumerate(bc_plan):
                if i >= 1:
                    gp.wait_ge(K.sems["BCT"], b_bct + 16 * i)
                ins = gp.dma_start(out=bstage[0:ncol * 128 // 128, 0:128] if False else bstage[0:ncol, 0:128],
                                   in_=inp[P + nm].rearrange("(c p) -> c p", p=128))
                K.inc(ins, "BCL", 16)

        @b.sync
        def _(sy):
            for i, (nm, col0, ncol) in enumerate(bc_plan):
                sy.wait_ge(K.sems["BCL"], b_bcl + 16 * (i + 1))
                if i >= 1:
                    sy.wait_ge(K.sems["BCV"], b_bcv + i)
                ins = sy.dma_start_transpose(btr, bstage)
                K.inc(ins, "BCT", 16)
    barrier(K)

    if SF_MAX < 3:
        return
    with nc.Block("sftau") as b:
        @b.scalar
        def _(s):
            ins = s.activation(SM[0:1, 300:301], SM[0:1, 300:301], AF.Abs)
            K.inc(ins, "SC")

        @b.vector
        def _(v):
            v.wait_ge(K.sem("SC"), K.c["SC"])
            ins = v.tensor_scalar(SM[0:1, 300:301], SM[0:1, 300:301], 0.1, None, op0=ALU.add)
            K.inc(ins, "VE")
            v.wait_ge(K.sem("VE"), K.c["VE"])
            ins = v.reciprocal(SM[0:1, 300:301], SM[0:1, 300:301])
            K.inc(ins, "VE")

        @b.sync
        def _(sy):
            sy.wait_ge(K.sem("VE"), K.c["VE"])
            ins = sy.dma_start(out=sc_park[0:1, 17:18], in_=SM[0:1, 300:301])
            K.inc(ins, "LFS", 16)
            sy.wait_ge(K.sem("LFS"), K.c["LFS"])
            ins = sy.dma_start(out=SM[:, 301:302],
                               in_=sc_park[0:1, 17:18].broadcast_to([128, 1]))
            K.inc(ins, "LFS", 16)
    barrier(K)

    with nc.Block("s0widen") as b:
        @b.vector
        def _(v):
            ins = v.tensor_copy(S0f, S0T)
            K.inc(ins, "VE")
    barrier(K)

    with (nc.psum_tensor("ps_sm", [1, 1024], F32) as ps_sm,
          nc.psum_tensor("ps_cr", [1, 512], F32) as ps_cr,
          nc.Block("sfsmean") as b):
        @b.tensor
        def _(te):
            for n2 in range(2):
                ins = te.matmul(ps_sm[:, n2 * 512:(n2 + 1) * 512], ONEB[0:64, 0:1],
                                s0stage[:, n2 * 512:(n2 + 1) * 512], start=True, stop=True)
            K.inc(ins, "TE")

        @b.scalar
        def _(s):
            s.wait_ge(K.sem("TE"), K.c["TE"])
            ins = s.activation(S32[0:1, 0:1024], ps_sm[:], AF.Identity, scale=1.0 / NS)
            K.inc(ins, "SC")

        @b.sync
        def _(sy):
            pass
    barrier(K)
    with nc.Block("sfsmean2") as b:
        @b.sync
        def _(sy):
            ins = sy.dma_start(out=sc_park[4:20, 0:64].rearrange("r c -> (r c)").unsqueeze(0),
                               in_=S32[0:1, 0:1024])
            K.inc(ins, "LFS", 16)

        @b.vector
        def _(v):
            ins = v.memset(AXb[0:16, 6208:6336], 0.0)
            K.inc(ins, "VE")

        @b.gpsimd
        def _(gp):
            gp.wait_ge(K.sem("LFS"), K.c["LFS"])
            gp.wait_ge(K.sem("VE"), K.c["VE"])
            ins = gp.dma_start(out=AXb[0:8, 6208:6336],
                               in_=sc_park[4:20, 0:64].rearrange("(a r) c -> a (r c)", a=8))
            K.inc(ins, "LF", 16)

        @b.sync
        def _(sy2):
            pass
    barrier(K)
    with nc.Block("sfsmean3") as b:
        @b.sync
        def _(sy):
            ins = sy.dma_start_transpose(AXb[:, 6336:6352], AXb[0:16, 6208:6336])
            K.inc(ins, "LFS", 16)
    barrier(K)

    with (nc.psum_tensor("ps_cr2", [1, 512], F32) as ps_cr,
          nc.Block("sfcrow") as b):
        @b.tensor
        def _(te):
            smT = AXb[:, 6336:6352]
            for dc in range(8):
                te.matmul(ps_cr[:], smT[:, dc:dc + 1],
                          WB[:, 4096 + dc * 512:4096 + (dc + 1) * 512],
                          start=(dc == 0), stop=False)
            ins = te.matmul(ps_cr[:], ONEB[0:1, 0:1], SMB[0:1, 256:768],
                            start=False, stop=True)
            K.inc(ins, "TE")

        @b.scalar
        def _(s):
            s.wait_ge(K.sem("TE"), K.c["TE"])
            ins = s.copy(S32[0:1, 1024:1536], ps_cr[:])
            K.inc(ins, "SC")

        @b.sync
        def _(sy):
            sy.wait_ge(K.sem("SC"), K.c["SC"])
            ins = sy.dma_start(out=sc_park[20:28, 0:64].rearrange("r c -> (r c)").unsqueeze(0),
                               in_=S32[0:1, 1024:1536])
            K.inc(ins, "LFS", 16)

        @b.gpsimd
        def _(gp):
            gp.wait_ge(K.sem("LFS"), K.c["LFS"])
            gp.wait_ge(K.sem("BCM"), K.c["BCM"])
            ins = gp.dma_start(out=bstage[0:4, 0:128],
                               in_=sc_park[20:28, 0:64].rearrange("(a r) c -> a (r c)", a=4))
            K.inc(ins, "LF", 16)
    barrier(K)
    with nc.Block("sfcrow2") as b:
        @b.sync
        def _(sy):
            ins = sy.dma_start_transpose(btr, bstage)
            K.inc(ins, "LFS", 16)

        @b.vector
        def _(v):
            v.wait_ge(K.sem("LFS"), K.c["LFS"])
            ins = v.tensor_copy(SM[:, 370:374], btr[:, 0:4])
            K.inc(ins, "VE")
    barrier(K)

    if SF_MAX < 6:
        return
    b_hp = K.c.get("HP2", 0); K.sem("HP2")
    b_hg = K.c.get("HG2", 0); K.sem("HG2")
    NQ = max(1, (T) // 512)
    qw = 512 if T >= 512 else T
    with (nc.psum_tensor("ps_h2", [128, 2, 512], F32) as ps_h,
          nc.Block("sfh") as b):
        @b.tensor
        def _(te):
            i = 0
            for hc in range(4):
                for n in range(NQ):
                    if i >= 2:
                        te.wait_ge(K.sems["HG2"], b_hg + i - 1)
                    for c in range(8):
                        ins = te.matmul(ps_h[:, i % 2, 0:qw],
                                        WB[:, c * 512 + hc * 128:c * 512 + (hc + 1) * 128],
                                        lnT(c)[:, n * qw:(n + 1) * qw],
                                        start=(c == 0), stop=(c == 7))
                    K.inc(ins, "HP2")
                    i += 1

        @b.scalar
        def _(s):
            i = 0
            for hc in range(4):
                for n in range(NQ):
                    s.wait_ge(K.sems["HP2"], b_hp + i + 1)
                    ins = s.activation(hTs(hc)[:, n * qw:(n + 1) * qw],
                                       ps_h[:, i % 2, 0:qw], AF.Gelu,
                                       bias=SM[:, 370 + hc:371 + hc])
                    K.inc(ins, "HG2")
                    i += 1
    barrier(K)

    if SF_MAX < 7:
        return
    b_lp = K.c.get("LP", 0); K.sem("LP")
    b_lv = K.c.get("LV", 0); K.sem("LV")
    b_ls = K.c.get("LS", 0); K.sem("LS")
    b_lt = K.c.get("LT", 0); K.sem("LT")
    lg = S32[:, 2048:2112]; ew = S32[:, 2112:2176]; wk_ = S32[:, 2176:2240]
    m1c = SM[:, 374:375]; m8c = SM[:, 375:376]; tmpc = SM[:, 376:377]; sumc = SM[:, 377:378]
    with (nc.psum_tensor("ps_lg", [128, 2, 512], F32) as ps_lg,
          nc.Block("sflg") as b):
        @b.tensor
        def _(te):
            for t in range(TT):
                if t >= 2:
                    te.wait_ge(K.sems["LV"], b_lv + 31 * (t - 2) + 1)
                for hc in range(4):
                    te.matmul(ps_lg[:, t % 2, 0:64],
                              hTs(hc)[:, t * 128:(t + 1) * 128],
                              SMB[:, hc * 64:(hc + 1) * 64],
                              start=(hc == 0), stop=False)
                ins = te.matmul(ps_lg[:, t % 2, 0:64], ONEB[0:1, 0:128],
                                SMB[0:1, 768:832], start=False, stop=True)
                K.inc(ins, "LP")

        @b.vector
        def _(v):
            for t in range(TT):
                v.wait_ge(K.sems["LP"], b_lp + t + 1)
                if t > 0:
                    v.wait_ge(K.sems["LV"], K.c["LV"])
                ins = v.tensor_scalar(lg, ps_lg[:, t % 2, 0:64], SM[:, 301:302], None, op0=ALU.mult)
                K.inc(ins, "LV")
                v.wait_ge(K.sems["LV"], K.c["LV"])
                ins = v.tensor_copy(wk_, lg); K.inc(ins, "LV")
                for it in range(8):
                    v.wait_ge(K.sems["LV"], K.c["LV"])
                    ins = v.reduce_max(tmpc if it > 0 else m1c, wk_, axis=AXX)
                    K.inc(ins, "LV")
                    if it < 7:
                        v.wait_ge(K.sems["LV"], K.c["LV"])
                        mm = m1c if it == 0 else tmpc
                        ins = v.tensor_scalar(ew, wk_, mm, None, op0=ALU.is_ge)
                        K.inc(ins, "LV")
                        v.wait_ge(K.sems["LV"], K.c["LV"])
                        ins = v.scalar_tensor_tensor(wk_, ew, -1e30, wk_,
                                                     op0=ALU.mult, op1=ALU.add)
                        K.inc(ins, "LV")
                    else:
                        v.wait_ge(K.sems["LV"], K.c["LV"])
                        ins = v.tensor_copy(m8c, tmpc); K.inc(ins, "LV")
                v.wait_ge(K.sems["LV"], K.c["LV"])
                ins = v.tensor_scalar(ew, lg, m8c, None, op0=ALU.is_ge); K.inc(ins, "LV")
                v.wait_ge(K.sems["LV"], K.c["LV"])
                ins = v.tensor_scalar(tmpc, m1c, -1.0, None, op0=ALU.mult); K.inc(ins, "LV")
                v.wait_ge(K.sems["LS"], b_ls + t + 1)
                ins = v.tensor_tensor(ew, ew, wk_, op=ALU.mult); K.inc(ins, "LV")
                v.wait_ge(K.sems["LV"], K.c["LV"])
                ins = v.reduce_sum(sumc, ew, axis=AXX); K.inc(ins, "LV")
                v.wait_ge(K.sems["LV"], K.c["LV"])
                ins = v.reciprocal(sumc, sumc); K.inc(ins, "LV")
                v.wait_ge(K.sems["LV"], K.c["LV"])
                ins = v.tensor_scalar(alpha_t(t)[:, 0:64], ew, sumc, None, op0=ALU.mult)
                K.inc(ins, "LV")

        @b.scalar
        def _(s):
            for t in range(TT):
                s.wait_ge(K.sems["LV"], b_lv + 31 * t + 27)
                ins = s.activation(wk_, lg, AF.Exp, bias=tmpc)
                K.inc(ins, "LS")

        @b.sync
        def _(sy):
            for t in range(TT):
                sy.wait_ge(K.sems["LV"], b_lv + 31 * (t + 1))
                ins = sy.dma_start_transpose(alphaT[:, t * 128:(t + 1) * 128], alpha_t(t))
                K.inc(ins, "LT", 16)
    barrier(K)

    if SF_MAX < 8:
        return
    with (nc.psum_tensor("ps_si", [64, 1024], F32) as ps_si,
          nc.psum_tensor("ps_dn", [64, 512], F32) as ps_dn,
          nc.Block("sfsi") as b):
        @b.tensor
        def _(te):
            for t in range(TT):
                for n2 in range(2):
                    ins = te.matmul(ps_si[:, n2 * 512:(n2 + 1) * 512],
                                    alpha_t(t)[:, 0:64],
                                    AB[:, t * 1024 + n2 * 512:t * 1024 + (n2 + 1) * 512],
                                    start=(t == 0), stop=(t == TT - 1))
                ins2 = te.matmul(ps_dn[:, 0:1], alpha_t(t)[:, 0:64], ONEB[:, 0:1],
                                 start=(t == 0), stop=(t == TT - 1))
            K.inc(ins2, "TE")

        @b.vector
        def _(v):
            v.wait_ge(K.sem("TE"), K.c["TE"])
            ins = v.tensor_scalar(SM[0:64, 378:379], ps_dn[:, 0:1], 1e-8, None, op0=ALU.add)
            K.inc(ins, "VE")
            v.wait_ge(K.sem("VE"), K.c["VE"])
            ins = v.reciprocal(SM[0:64, 378:379], SM[0:64, 378:379])
            K.inc(ins, "VE")
            v.wait_ge(K.sem("VE"), K.c["VE"])
            ins = v.tensor_scalar(sif[0:64, :], ps_si[:], SM[0:64, 378:379], None, op0=ALU.mult)
            K.inc(ins, "VE")
            v.wait_ge(K.sem("VE"), K.c["VE"])
            ins = v.tensor_copy(sistage, sif[0:64, :])
            K.inc(ins, "VE")

        @b.sync
        def _(sy):
            sy.wait_ge(K.sem("VE"), K.c["VE"])
            for c in range(8):
                ins = sy.dma_start_transpose(siT[:, c, :], sistage[:, c * 128:(c + 1) * 128])
                K.inc(ins, "LFS", 16)
    barrier(K)

    if SF_MAX < 10:
        return
    def wtb2(par):
        return AXb[:, 6400 + par * 1024:6400 + (par + 1) * 1024]

    for wname, rhsv, dstv, bcol0 in (("Wih", siT, giT, 302), ("Whh", S0T, ghT, 326)):
        b_gl = [K.c.get("GL0", 0), K.c.get("GL1", 0)]; K.sem("GL0"); K.sem("GL1")
        b_gt = K.c.get("GT", 0); K.sem("GT")
        b_ge = K.c.get("GE", 0); K.sem("GE")
        b_gp2 = K.c.get("GP2", 0); K.sem("GP2")
        b_gs = K.c.get("GS", 0); K.sem("GS")
        with (nc.psum_tensor("ps_gt" + wname, [128, 2, 8, 128], BF16) as ps_gt,
              nc.psum_tensor("ps_gg" + wname, [128, 2, 512], F32) as ps_gg,
              nc.Block("gru" + wname) as b):
            @b.gpsimd
            def _(gp):
                for gc in range(24):
                    if gc >= 2:
                        gp.wait_ge(K.sems["GT"], b_gt + gc - 1)
                    ins = gp.dma_start(out=wtb(gc % 2),
                                       in_=inp[P + wname][gc * 128:(gc + 1) * 128, :])
                    K.inc(ins, "GL" + str(gc % 2), 16)

            @b.tensor
            def _(te):
                for gc in range(24):
                    te.wait_ge(K.sems["GL" + str(gc % 2)], b_gl[gc % 2] + 16 * (gc // 2 + 1))
                    if gc >= 2:
                        te.wait_ge(K.sems["GE"], b_ge + gc - 1)
                    for dc in range(8):
                        ins = te.transpose(ps_gt[:, gc % 2, dc, :],
                                           wtb(gc % 2)[:, dc * 128:(dc + 1) * 128], IDN[:])
                    K.inc(ins, "GT")
                    if gc >= 1:
                        gp_ = gc - 1
                        te.wait_ge(K.sems["GE"], b_ge + gp_ + 1)
                        if gp_ >= 2:
                            te.wait_ge(K.sems["GS"], b_gs + gp_ - 1)
                        for dc in range(8):
                            ins = te.matmul(ps_gg[:, gp_ % 2, 0:64],
                                            wtb2(gp_ % 2)[:, dc * 128:(dc + 1) * 128],
                                            rhsv[:, dc, :],
                                            start=(dc == 0), stop=(dc == 7))
                        K.inc(ins, "GP2")
                gp_ = 23
                te.wait_ge(K.sems["GE"], b_ge + gp_ + 1)
                te.wait_ge(K.sems["GS"], b_gs + gp_ - 1)
                for dc in range(8):
                    ins = te.matmul(ps_gg[:, gp_ % 2, 0:64],
                                    wtb2(gp_ % 2)[:, dc * 128:(dc + 1) * 128],
                                    rhsv[:, dc, :],
                                    start=(dc == 0), stop=(dc == 7))
                K.inc(ins, "GP2")

            @b.vector
            def _(v):
                for gc in range(24):
                    v.wait_ge(K.sems["GT"], b_gt + gc + 1)
                    if gc >= 2:
                        v.wait_ge(K.sems["GP2"], b_gp2 + gc - 1)
                    ins = v.tensor_copy(wtb2(gc % 2).rearrange("p (c n) -> p c n", c=8),
                                        ps_gt[:, gc % 2, 0:8, :])
                    K.inc(ins, "GE")

            @b.scalar
            def _(s):
                for gc in range(24):
                    s.wait_ge(K.sems["GP2"], b_gp2 + gc + 1)
                    ins = s.activation(dstv[:, gc, :], ps_gg[:, gc % 2, 0:64], AF.Identity,
                                       bias=SM[:, bcol0 + gc:bcol0 + gc + 1])
                    K.inc(ins, "GS")
        barrier(K)

    if SF_MAX < 11:
        return
    rr = S32[:, 2048:2112]; zz = S32[:, 2112:2176]; nn = S32[:, 2176:2240]
    tp1 = S32[:, 2240:2304]
    b_gv = K.c.get("GV", 0); K.sem("GV")
    b_gsc = K.c.get("GSC", 0); K.sem("GSC")
    with nc.Block("sfgate") as b:
        @b.vector
        def _(v):
            for dc in range(8):
                if dc > 0:
                    v.wait_ge(K.sems["GSC"], b_gsc + 3 * dc)
                v.wait_ge(K.sems["GV"], K.c["GV"])
                ins = v.tensor_tensor(rr, giT[:, dc, :], ghT[:, dc, :], op=ALU.add)
                K.inc(ins, "GV")
                ins = v.tensor_tensor(zz, giT[:, dc + 8, :], ghT[:, dc + 8, :], op=ALU.add)
                K.inc(ins, "GV")
                v.wait_ge(K.sems["GSC"], b_gsc + 3 * dc + 2)
                ins = v.tensor_tensor(tp1, rr, ghT[:, dc + 16, :], op=ALU.mult)
                K.inc(ins, "GV")
                v.wait_ge(K.sems["GV"], K.c["GV"])
                ins = v.tensor_tensor(nn, giT[:, dc + 16, :], tp1, op=ALU.add)
                K.inc(ins, "GV")
                v.wait_ge(K.sems["GSC"], b_gsc + 3 * dc + 3)
                ins = v.tensor_tensor(tp1, S0f[:, dc, :], nn, op=ALU.subtract)
                K.inc(ins, "GV")
                v.wait_ge(K.sems["GV"], K.c["GV"])
                ins = v.tensor_tensor(tp1, zz, tp1, op=ALU.mult)
                K.inc(ins, "GV")
                v.wait_ge(K.sems["GV"], K.c["GV"])
                ins = v.tensor_tensor(SnT[:, dc, :], nn, tp1, op=ALU.add)
                K.inc(ins, "GV")
                v.wait_ge(K.sems["GV"], K.c["GV"])
                ins = v.tensor_copy(SnTb[:, dc, :], SnT[:, dc, :])
                K.inc(ins, "GV")

        @b.scalar
        def _(s):
            for dc in range(8):
                s.wait_ge(K.sems["GV"], b_gv + 8 * dc + 2)
                ins = s.activation(rr, rr, AF.Sigmoid)
                K.inc(ins, "GSC")
                ins = s.activation(zz, zz, AF.Sigmoid)
                K.inc(ins, "GSC")
                s.wait_ge(K.sems["GV"], b_gv + 8 * dc + 4)
                ins = s.activation(nn, nn, AF.Tanh)
                K.inc(ins, "GSC")
    barrier(K)

    if SF_MAX < 12:
        return
    with (nc.psum_tensor("ps_qs", [128, 4, 512], F32) as ps_qs,
          nc.Block("sfqk") as b):
        @b.tensor
        def _(te):
            for i, (wc, bcol) in enumerate(((wq_c, 366), (wk_c, 368))):
                for qc in range(2):
                    for dc in range(8):
                        ins = te.matmul(ps_qs[:, i * 2 + qc, 0:64],
                                        wc(dc)[:, qc * 128:(qc + 1) * 128],
                                        SnTb[:, dc, :],
                                        start=(dc == 0), stop=(dc == 7))
                    K.inc(ins, "TE")

        @b.scalar
        def _(s):
            for i in range(2):
                for qc in range(2):
                    s.wait_ge(K.sem("TE"), K.c["TE"] - 3 + i * 2 + qc)
                    dv = qsT if i == 0 else ksT
                    bcol = 366 if i == 0 else 368
                    ins = s.activation(dv[:, qc, :], ps_qs[:, i * 2 + qc, 0:64], AF.Identity,
                                       bias=SM[:, bcol + qc:bcol + qc + 1])
                    K.inc(ins, "SC")
    barrier(K)

    if SF_MAX < 13:
        return
    with (nc.psum_tensor("ps_sa", [64, 64], F32) as ps_sa,
          nc.Block("sfsa") as b):
        @b.tensor
        def _(te):
            for qc in range(2):
                ins = te.matmul(ps_sa[:], qsT[:, qc, :], ksT[:, qc, :],
                                start=(qc == 0), stop=(qc == 1))
            K.inc(ins, "TE")

        @b.vector
        def _(v):
            v.wait_ge(K.sem("TE"), K.c["TE"])
            ins = v.reduce_max(SM[0:64, 379:380], ps_sa[:], axis=AXX)
            K.inc(ins, "VE")
            v.wait_ge(K.sem("VE"), K.c["VE"])
            ins = v.tensor_scalar(SM[0:64, 379:380], SM[0:64, 379:380], -0.0625, None, op0=ALU.mult)
            K.inc(ins, "VE")

        @b.scalar
        def _(s):
            s.wait_ge(K.sem("VE"), K.c["VE"])
            ins = s.activation(S32[0:64, 2048:2112], ps_sa[:], AF.Exp,
                               bias=SM[0:64, 379:380], scale=0.0625,
                               accum_out=SM[0:64, 380:381])
            K.inc(ins, "SC")

        @b.vector
        def _(v):
            v.wait_ge(K.sem("SC"), K.c["SC"])
            ins = v.reciprocal(SM[0:64, 380:381], SM[0:64, 380:381])
            K.inc(ins, "VE")
            v.wait_ge(K.sem("VE"), K.c["VE"])
            ins = v.tensor_scalar(pstage[:, 0:64], S32[0:64, 2048:2112], SM[0:64, 380:381], None, op0=ALU.mult)
            K.inc(ins, "VE")

        @b.sync
        def _(sy):
            sy.wait_ge(K.sem("VE"), K.c["VE"])
            ins = sy.dma_start_transpose(probsT[:, 0:64].rearrange("p c -> p c"), pstage)
            K.inc(ins, "LFS", 16)
    barrier(K)

    if SF_MAX < 14:
        return
    with (nc.psum_tensor("ps_vs", [64, 1024], F32) as ps_vs,
          nc.Block("sfvs") as b):
        @b.tensor
        def _(te):
            for dc in range(8):
                for n2 in range(2):
                    te.matmul(ps_vs[:, n2 * 512:(n2 + 1) * 512],
                              SnTb[:, dc, :],
                              wbig(dc)[:, n2 * 512:(n2 + 1) * 512],
                              start=(dc == 0), stop=False)
            for n2 in range(2):
                ins = te.matmul(ps_vs[:, n2 * 512:(n2 + 1) * 512],
                                ONEB[0:1, 0:64],
                                SMB[0:1, 832 + n2 * 512:832 + (n2 + 1) * 512],
                                start=False, stop=True)
            K.inc(ins, "TE")

        @b.scalar
        def _(s):
            s.wait_ge(K.sem("TE"), K.c["TE"])
            ins = s.copy(vslp[0:64, :], ps_vs[:])
            K.inc(ins, "SC")

        @b.gpsimd
        def _(gp):
            gp.wait_ge(K.sem("SC"), K.c["SC"])
            for dc in range(8):
                ins = gp.dma_start(out=wbig(dc), in_=inp[P + "wvp"][dc * 128:(dc + 1) * 128, :])
                K.inc(ins, "LF", 16)
    barrier(K)

    if SF_MAX < 15:
        return
    with (nc.psum_tensor("ps_sc2", [128, 8, 64], F32) as ps_sc2,
          nc.psum_tensor("ps_st", [1, 128], F32) as ps_st,
          nc.Block("sfcomm") as b):
        @b.tensor
        def _(te):
            for dc in range(8):
                ins = te.matmul(ps_sc2[:, dc, :],
                                vslp[:, dc * 128:(dc + 1) * 128],
                                probsT,
                                start=True, stop=True)
            K.inc(ins, "TE")

        @b.vector
        def _(v):
            v.wait_ge(K.sem("TE"), K.c["TE"])
            for dc in range(8):
                v.wait_ge(K.sem("VE"), K.c["VE"])
                ins = v.tensor_tensor(yf[:, dc, :], SnT[:, dc, :], ps_sc2[:, dc, :], op=ALU.add)
                K.inc(ins, "VE")

        @b.scalar
        def _(s):
            s.wait_ge(K.sem("VE"), K.c["VE"])
            ins = s.activation(ScT[:, 0:8, :].rearrange("p c s -> p (c s)"),
                               yf[:, 0:8, :].rearrange("p c s -> p (c s)"), AF.Square)
            K.inc(ins, "SC")

        @b.tensor
        def _(te):
            te.wait_ge(K.sem("SC"), K.c["SC"])
            for dc in range(8):
                te.matmul(ps_st[:, 0:64], SM[:, 1279:1280], yf[:, dc, :],
                          start=(dc == 0), stop=(dc == 7))
            for dc in range(8):
                ins = te.matmul(ps_st[:, 64:128], SM[:, 1279:1280], ScT[:, dc, :],
                                start=(dc == 0), stop=(dc == 7))
            K.inc(ins, "TE")

        @b.vector
        def _(v):
            v.wait_ge(K.sem("TE"), K.c["TE"])
            ins = v.tensor_scalar(SM[0:1, 384:448], ps_st[:, 0:64], 1.0 / D, None, op0=ALU.mult)
            K.inc(ins, "VE")
            v.wait_ge(K.sem("VE"), K.c["VE"])
            ins = v.tensor_scalar(SM[0:1, 448:512], ps_st[:, 64:128], 1.0 / D, None, op0=ALU.mult)
            K.inc(ins, "VE")
            v.wait_ge(K.sem("VE"), K.c["VE"])
            ins = v.tensor_tensor(tp1[0:1, 0:64], SM[0:1, 384:448], SM[0:1, 384:448], op=ALU.mult)
            K.inc(ins, "VE")
            v.wait_ge(K.sem("VE"), K.c["VE"])
            ins = v.tensor_tensor(SM[0:1, 448:512], SM[0:1, 448:512], tp1[0:1, 0:64], op=ALU.subtract)
            K.inc(ins, "VE")
            v.wait_ge(K.sem("VE"), K.c["VE"])
            ins = v.tensor_scalar(SM[0:1, 448:512], SM[0:1, 448:512], EPS, None, op0=ALU.add)
            K.inc(ins, "VE")

        @b.scalar
        def _(s):
            s.wait_ge(K.sem("VE"), K.c["VE"])
            ins = s.activation(SM[0:1, 448:512], SM[0:1, 448:512], AF.Sqrt)
            K.inc(ins, "SC")

        @b.vector
        def _(v):
            v.wait_ge(K.sem("SC"), K.c["SC"])
            ins = v.reciprocal(SM[0:1, 448:512], SM[0:1, 448:512])
            K.inc(ins, "VE")

        @b.sync
        def _(sy):
            sy.wait_ge(K.sem("VE"), K.c["VE"])
            ins = sy.dma_start(out=sc_park[28:29, 0:64], in_=SM[0:1, 384:448])
            K.inc(ins, "LFS", 16)
            ins = sy.dma_start(out=sc_park[29:30, 0:64], in_=SM[0:1, 448:512])
            K.inc(ins, "LFS", 16)
            sy.wait_ge(K.sem("LFS"), K.c["LFS"])
            ins = sy.dma_start(out=S32[:, 2304:2368],
                               in_=sc_park[28:29, 0:64].broadcast_to([128, 64]))
            K.inc(ins, "LFS", 16)
            ins = sy.dma_start(out=S32[:, 2368:2432],
                               in_=sc_park[29:30, 0:64].broadcast_to([128, 64]))
            K.inc(ins, "LFS", 16)
    barrier(K)

    with nc.Block("sfln2") as b:
        @b.vector
        def _(v):
            for dc in range(8):
                v.wait_ge(K.sem("VE"), K.c["VE"])
                ins = v.tensor_tensor(ynf[:, dc, :], yf[:, dc, :],
                                      S32[:, 2304:2368], op=ALU.subtract)
                K.inc(ins, "VE")
                v.wait_ge(K.sem("VE"), K.c["VE"])
                ins = v.tensor_tensor(ynf[:, dc, :], ynf[:, dc, :], S32[:, 2368:2432], op=ALU.mult)
                K.inc(ins, "VE")
                v.wait_ge(K.sem("VE"), K.c["VE"])
                ins = v.tensor_scalar(ynf[:, dc, :], ynf[:, dc, :],
                                      SM[:, 350 + dc:351 + dc], SM[:, 358 + dc:359 + dc],
                                      op0=ALU.mult, op1=ALU.add)
                K.inc(ins, "VE")
                v.wait_ge(K.sem("VE"), K.c["VE"])
                ins = v.tensor_copy(ynb[:, dc, :], ynf[:, dc, :])
                K.inc(ins, "VE")
    barrier(K)

    if SF_MAX < 17:
        return
    with nc.Block("sfbvp") as b:
        @b.gpsimd
        def _(gp):
            ins = gp.dma_start(out=SMB[0:1, 256:1280],
                               in_=inp[P + "bvp"].rearrange("(a d) -> a d", a=1))
            K.inc(ins, "LF", 16)
    barrier(K)
    with (nc.psum_tensor("ps_sv", [64, 1024], F32) as ps_sv,
          nc.Block("sfsv") as b):
        @b.tensor
        def _(te):
            for dc in range(8):
                for n2 in range(2):
                    te.matmul(ps_sv[:, n2 * 512:(n2 + 1) * 512],
                              ynb[:, dc, :],
                              wbig(dc)[:, n2 * 512:(n2 + 1) * 512],
                              start=(dc == 0), stop=False)
            for n2 in range(2):
                ins = te.matmul(ps_sv[:, n2 * 512:(n2 + 1) * 512],
                                ONEB[0:1, 0:64],
                                SMB[0:1, 256 + n2 * 512:256 + (n2 + 1) * 512],
                                start=False, stop=True)
            K.inc(ins, "TE")

        @b.scalar
        def _(s):
            s.wait_ge(K.sem("TE"), K.c["TE"])
            ins = s.copy(svp[0:64, :], ps_sv[:])
            K.inc(ins, "SC")

        @b.gpsimd
        def _(gp):
            gp.wait_ge(K.sem("SC"), K.c["SC"])
            for dc in range(8):
                ins = gp.dma_start(out=wbig(dc), in_=inp[P + "wo"][dc * 128:(dc + 1) * 128, :])
                K.inc(ins, "LF", 16)
    barrier(K)

    if SF_MAX < 18:
        return
    b_op = K.c.get("OP", 0); K.sem("OP")
    b_oe = K.c.get("OE", 0); K.sem("OE")
    NQ2 = max(1, T // 512)
    qw2 = 512 if T >= 512 else T
    with (nc.psum_tensor("ps_ot", [128, 2, 512], F32) as ps_ot,
          nc.Block("sfot") as b):
        @b.tensor
        def _(te):
            i = 0
            for dc in range(8):
                for n in range(NQ2):
                    if i >= 2:
                        te.wait_ge(K.sems["OE"], b_oe + i - 1)
                    ins = te.matmul(ps_ot[:, i % 2, 0:qw2],
                                    svp[:, dc * 128:(dc + 1) * 128],
                                    alphaT[:, n * qw2:(n + 1) * qw2],
                                    start=True, stop=True)
                    K.inc(ins, "OP")
                    i += 1

        @b.scalar
        def _(s):
            i = 0
            for dc in range(8):
                for n in range(NQ2):
                    s.wait_ge(K.sems["OP"], b_op + i + 1)
                    ins = s.copy(AB[:, dc * 2048 + n * qw2:dc * 2048 + (n + 1) * qw2],
                                 ps_ot[:, i % 2, 0:qw2])
                    K.inc(ins, "OE")
                    i += 1
    barrier(K)

    if SF_MAX < 19:
        return
    with nc.Block("sfbo") as b:
        @b.gpsimd
        def _(gp):
            ins = gp.dma_start(out=SMB[0:1, 832:1856],
                               in_=inp[P + "bo"].rearrange("(a d) -> a d", a=1))
            K.inc(ins, "LF", 16)
    barrier(K)
    b_po = K.c.get("PO", 0); K.sem("PO")
    b_res = K.c.get("RES", 0); K.sem("RES")
    b_xlb = [K.c.get("XLB0", 0), K.c.get("XLB1", 0)]; K.sem("XLB0"); K.sem("XLB1")
    with (nc.psum_tensor("ps_fo", [128, 2, 1024], F32) as ps_fo,
          nc.Block("sffin") as b):
        @b.sync
        def _(sy):
            for t in range(TT):
                if t >= 2:
                    sy.wait_ge(K.sems["RES"], b_res + t - 1)
                ins = sy.dma_start(out=S32[:, (t % 2) * 1024:((t % 2) + 1) * 1024],
                                   in_=x_park[t * 128:(t + 1) * 128, :])
                K.inc(ins, "XLB" + str(t % 2), 16)

        @b.tensor
        def _(te):
            for t in range(TT):
                if t >= 2:
                    te.wait_ge(K.sems["RES"], b_res + t - 1)
                for dc in range(8):
                    for n2 in range(2):
                        te.matmul(ps_fo[:, t % 2, n2 * 512:(n2 + 1) * 512],
                                  AB[:, dc * 2048 + t * 128:dc * 2048 + (t + 1) * 128],
                                  wbig(dc)[:, n2 * 512:(n2 + 1) * 512],
                                  start=(dc == 0), stop=False)
                for n2 in range(2):
                    ins = te.matmul(ps_fo[:, t % 2, n2 * 512:(n2 + 1) * 512],
                                    ONEB[0:1, 0:128],
                                    SMB[0:1, 832 + n2 * 512:832 + (n2 + 1) * 512],
                                    start=False, stop=True)
                K.inc(ins, "PO")

        @b.vector
        def _(v):
            for t in range(TT):
                v.wait_ge(K.sems["PO"], b_po + t + 1)
                v.wait_ge(K.sems["XLB" + str(t % 2)], b_xlb[t % 2] + 16 * (t // 2 + 1))
                ins = v.tensor_tensor(AX[:, t * 1024:(t + 1) * 1024],
                                      ps_fo[:, t % 2, :],
                                      S32[:, (t % 2) * 1024:((t % 2) + 1) * 1024],
                                      op=ALU.add)
                K.inc(ins, "RES")
    barrier(K)



_WNAMES = [
    "L0_attn_Wq", "L0_attn_bq", "L0_attn_Wk", "L0_attn_bk", "L0_attn_Wv",
    "L0_attn_bv", "L0_attn_Wo", "L0_attn_bo", "L0_ffn_router_w",
    "L0_ffn_w_fc", "L0_ffn_b_fc", "L0_ffn_w_proj", "L0_ffn_b_proj",
    "L1_attn_Wq", "L1_attn_Wk", "L1_attn_Wv", "L1_attn_Wo",
    "L1_ffn_slot_init", "L1_ffn_w1", "L1_ffn_b1", "L1_ffn_w2", "L1_ffn_b2",
    "L1_ffn_tau", "L1_ffn_Wih", "L1_ffn_Whh", "L1_ffn_bih", "L1_ffn_bhh",
    "L1_ffn_wq", "L1_ffn_bq", "L1_ffn_wk", "L1_ffn_bk", "L1_ffn_wv",
    "L1_ffn_bv", "L1_ffn_ln_g", "L1_ffn_ln_b", "L1_ffn_wvp", "L1_ffn_bvp",
    "L1_ffn_wo", "L1_ffn_bo",
]


def kernel(**inputs):
    x = np.ascontiguousarray(np.asarray(inputs["L0_attn_x"], dtype=np.float32))
    B, T, Dm = x.shape
    assert (B, Dm) == (8, 1024)
    nc = build_program(T=T)
    base = {k: np.ascontiguousarray(np.asarray(inputs[k], dtype=np.float32))
            for k in _WNAMES}
    in_maps = []
    for b in range(8):
        m = dict(base)
        m["x"] = np.ascontiguousarray(x[b])
        in_maps.append(m)
    res = run_bass_kernel_spmd(nc, in_maps, core_ids=list(range(8)))
    out = np.stack([res.results[b]["y"] for b in range(8)], axis=0)
    return out.astype(np.float32)


# version 04
# speedup vs baseline: 1.0645x; 1.0645x over previous
import math
from contextlib import ExitStack

import numpy as np
import ml_dtypes

import concourse.bass as bass
import concourse.mybir as mybir
from concourse.bass_utils import run_bass_kernel_spmd

F32 = mybir.dt.float32
BF16 = mybir.dt.bfloat16
BF = ml_dtypes.bfloat16
AF = mybir.ActivationFunctionType
ALU = mybir.AluOpType
AXX = mybir.AxisListType.X

D = 1024
NH = 16
HD = 64
KVH = 4
NE = 8
HF = 4096
NS = 64
SDQ = 256
RDH = 512
EPS = 1e-5

ENGS = ("sync", "gpsimd", "tensor", "scalar", "vector")


class KC:
    def __init__(self, nc, ctx):
        self.nc = nc
        self.ctx = ctx
        self.sems = {}
        self.c = {}
        self.dsems = set()

    def sem(self, name):
        if name not in self.sems:
            self.sems[name] = self.ctx.enter_context(self.nc.semaphore(name))
            self.c[name] = 0
        return self.sems[name]

    def inc(self, ins, name, n=1):
        ins.then_inc(self.sem(name), n)
        self.c[name] += n
        if n >= 16:
            self.dsems.add(name)
        return self.c[name]


def barrier(K, engines=ENGS):
    nc = K.nc
    for e in engines:
        K.sem("B" + e)
    snap = {e: K.c["B" + e] for e in engines}
    dwaits = [(s, K.c[s]) for s in sorted(K.dsems) if K.c[s] > 0]
    with nc.Block(None) as b:
        def mk(e):
            def f(eng):
                if e == "sync":
                    for s, v in dwaits:
                        eng.wait_ge(K.sems[s], v)
                eng.drain()
                eng.sem_inc(K.sems["B" + e], 1)
                for o in engines:
                    if o != e:
                        eng.wait_ge(K.sems["B" + o], snap[o] + 1)
            return f
        for e in engines:
            getattr(b, e)(mk(e))
    for e in engines:
        K.c["B" + e] += 1


def build_program(T=2048, blocks=("l0a", "l0f", "l1a", "l1f")):
    TT = T // 128
    nc = bass.Bass("TRN2", target_bir_lowering=False, debug=False)

    inp = {}

    def I(name, shape):
        inp[name] = nc.dram_tensor(name, list(shape), F32, kind="ExternalInput").ap()

    I("x", (T, D))
    I("L0_attn_Wq", (D, D)); I("L0_attn_bq", (D,))
    I("L0_attn_Wk", (D, D)); I("L0_attn_bk", (D,))
    I("L0_attn_Wv", (D, D)); I("L0_attn_bv", (D,))
    I("L0_attn_Wo", (D, D)); I("L0_attn_bo", (D,))
    I("L0_ffn_router_w", (NE, D))
    I("L0_ffn_w_fc", (NE, D, HF)); I("L0_ffn_b_fc", (NE, HF))
    I("L0_ffn_w_proj", (NE, HF, D)); I("L0_ffn_b_proj", (NE, D))
    I("L1_attn_Wq", (D, D)); I("L1_attn_Wk", (KVH * HD, D)); I("L1_attn_Wv", (KVH * HD, D))
    I("L1_attn_Wo", (D, D))
    I("L1_ffn_slot_init", (NS, D))
    I("L1_ffn_w1", (2 * D, RDH)); I("L1_ffn_b1", (RDH,))
    I("L1_ffn_w2", (RDH, NS)); I("L1_ffn_b2", (NS,))
    I("L1_ffn_tau", (1,))
    I("L1_ffn_Wih", (3 * D, D)); I("L1_ffn_Whh", (3 * D, D))
    I("L1_ffn_bih", (3 * D,)); I("L1_ffn_bhh", (3 * D,))
    I("L1_ffn_wq", (D, SDQ)); I("L1_ffn_bq", (SDQ,))
    I("L1_ffn_wk", (D, SDQ)); I("L1_ffn_bk", (SDQ,))
    I("L1_ffn_wv", (D, D)); I("L1_ffn_bv", (D,))
    I("L1_ffn_ln_g", (D,)); I("L1_ffn_ln_b", (D,))
    I("L1_ffn_wvp", (D, D)); I("L1_ffn_bvp", (D,))
    I("L1_ffn_wo", (D, D)); I("L1_ffn_bo", (D,))

    y_out = nc.dram_tensor("y", [T, D], F32, kind="ExternalOutput").ap()
    x_park = nc.dram_tensor("x_park", [T, D], F32, kind="Internal").ap()
    sc_park = nc.dram_tensor("sc_park", [64, 64], F32, kind="Internal").ap()

    pos = np.arange(T, dtype=np.float64).reshape(TT, 128).T
    invf = 10000.0 ** (-(np.arange(32, dtype=np.float64) / 32.0))
    ang = pos[:, :, None] * invf[None, None, :]
    cos_c = nc.inline_tensor(np.ascontiguousarray(np.cos(ang).astype(np.float32).reshape(128, TT * 32)), "cos_c").ap()
    sin_c = nc.inline_tensor(np.ascontiguousarray(np.sin(ang).astype(np.float32).reshape(128, TT * 32)), "sin_c").ap()
    ident_c = nc.inline_tensor(np.eye(128, dtype=BF), "ident_c").ap()
    identf_c = nc.inline_tensor(np.eye(128, dtype=np.float32), "identf_c").ap()
    tri_c = nc.inline_tensor(np.triu(np.ones((128, 128), np.float32)).astype(BF), "tri_c").ap()
    hc = np.zeros((128, 2), np.float32); hc[0:64, 0] = 1.0; hc[64:128, 1] = 1.0
    hcols_c = nc.inline_tensor(hc, "hcols_c").ap()

    with ExitStack() as ctx:
        En = ctx.enter_context
        K = KC(nc, ctx)

        AXA = En(nc.sbuf_tensor("AXA", [128, 16 * 1024], F32))
        AA = En(nc.sbuf_tensor("AA", [128, 16384], BF16))
        AB = En(nc.sbuf_tensor("AB", [128, 16384], BF16))
        AC = En(nc.sbuf_tensor("AC", [128, 16640], BF16))
        WB = En(nc.sbuf_tensor("WB", [128, 8192], BF16))
        S32 = En(nc.sbuf_tensor("S32", [128, 3072], F32))
        SM = En(nc.sbuf_tensor("SM", [128, 1280], F32))
        SMB = En(nc.sbuf_tensor("SMB", [128, 2048], BF16))
        ONEB = En(nc.sbuf_tensor("ONEB", [128, 2048], BF16))
        COS = En(nc.sbuf_tensor("COS", [128, TT * 32], F32))
        SIN = En(nc.sbuf_tensor("SIN", [128, TT * 32], F32))
        IDN = En(nc.sbuf_tensor("IDN", [128, 128], BF16))
        IDF = En(nc.sbuf_tensor("IDF", [128, 128], F32))
        TRI = En(nc.sbuf_tensor("TRI", [128, 128], BF16))
        HCF = En(nc.sbuf_tensor("HCF", [128, 2], F32))

        AXb = AXA[:].bitcast(BF16)

        st = dict(T=T, TT=TT, inp=inp, y=y_out, x_park=x_park, sc_park=sc_park,
                  AX=AXA, AXb=AXb, AA=AA, AB=AB, AC=AC, WB=WB, S32=S32, SM=SM,
                  SMB=SMB, ONEB=ONEB, COS=COS, SIN=SIN, IDN=IDN, IDF=IDF,
                  TRI=TRI, HCF=HCF)

        with nc.Block("init") as b:
            @b.sync
            def _(sy):
                for t in range(TT):
                    ins = sy.dma_start(out=AXA[:, t * 1024:(t + 1) * 1024],
                                       in_=inp["x"][t * 128:(t + 1) * 128, :])
                    K.inc(ins, "LD", 16)
                K.inc(sy.dma_start(out=COS[:], in_=cos_c), "LD", 16)
                K.inc(sy.dma_start(out=SIN[:], in_=sin_c), "LD", 16)
                K.inc(sy.dma_start(out=IDN[:], in_=ident_c), "LD", 16)
                K.inc(sy.dma_start(out=IDF[:], in_=identf_c), "LD", 16)
                K.inc(sy.dma_start(out=TRI[:], in_=tri_c), "LD", 16)
                K.inc(sy.dma_start(out=HCF[:], in_=hcols_c), "LD", 16)

            @b.vector
            def _(v):
                K.inc(v.memset(ONEB[:], 1.0), "VE")
        barrier(K)

        if "l0a" in blocks:
            emit_attn(K, st, layer=0)
            barrier(K)
        if "l0f" in blocks:
            emit_moe(K, st)
            barrier(K)
        if "l1a" in blocks:
            emit_attn(K, st, layer=1)
            barrier(K)
        if "l1f" in blocks:
            emit_slotffn(K, st)
            barrier(K)

        with nc.Block("fin") as b:
            @b.sync
            def _(sy):
                for t in range(TT):
                    ins = sy.dma_start(out=y_out[t * 128:(t + 1) * 128, :],
                                       in_=AXA[:, t * 1024:(t + 1) * 1024])
                    K.inc(ins, "ST", 16)
                sy.wait_ge(K.sem("ST"), K.c["ST"])

    return nc



def emit_ln_lnT(K, st, park_x, sink=None, sink_persistent=False):
    nc = K.nc
    K.uid = getattr(K, "uid", 0) + 1
    TT = st["TT"]
    AX, S32, SM, AA, WB = st["AX"], st["S32"], st["SM"], st["AA"], st["WB"]
    IDN, x_park = st["IDN"], st["x_park"]
    if sink is None:
        def sink(t):
            return WB[:, (t % 2) * 1024:((t % 2) + 1) * 1024]
    b_lns = K.c.get("LNS", 0); K.sem("LNS")
    b_lnv = K.c.get("LNV", 0); K.sem("LNV")
    b_trt = K.c.get("TRT", 0); K.sem("TRT")
    b_tre = K.c.get("TRE", 0); K.sem("TRE")
    K.sem("XPK")

    def smo(t, i):
        return SM[:, 16 * (t % 2) + i: 16 * (t % 2) + i + 1]

    with (nc.psum_tensor("ps_lntr%d" % K.uid, [128, 2, 8, 128], BF16) as ps_tr,
          nc.Block("ln%d" % K.uid) as b):

        @b.scalar
        def _(s):
            for t in range(TT):
                if t >= 2:
                    s.wait_ge(K.sems["LNV"], b_lnv + 2 * (t - 2) + 2)
                ins = s.activation(S32[:, 1024:2048], AX[:, t * 1024:(t + 1) * 1024],
                                   AF.Square, accum_out=smo(t, 6))
                K.inc(ins, "LNS")
                s.wait_ge(K.sems["LNV"], b_lnv + 2 * t + 1)
                ins = s.activation(smo(t, 3), smo(t, 2), AF.Sqrt)
                K.inc(ins, "LNS")

        @b.vector
        def _(v):
            for t in range(TT):
                xt = AX[:, t * 1024:(t + 1) * 1024]
                if t >= 2:
                    v.wait_ge(K.sems["LNS"], b_lns + 2 * (t - 2) + 2)
                ins = v.reduce_sum(smo(t, 0), xt, axis=AXX)
                K.inc(ins, "VEq")
                v.wait_ge(K.sems["LNS"], b_lns + 2 * t + 1)
                v.wait_ge(K.sems["VEq"], K.c["VEq"])
                ins = v.tensor_scalar(smo(t, 0), smo(t, 0), 1.0 / D, None, op0=ALU.mult)
                K.inc(ins, "VEq")
                ins = v.tensor_scalar(smo(t, 1), smo(t, 6), 1.0 / D, None, op0=ALU.mult)
                K.inc(ins, "VEq")
                v.wait_ge(K.sems["VEq"], K.c["VEq"])
                ins = v.tensor_tensor(smo(t, 2), smo(t, 0), smo(t, 0), op=ALU.mult)
                K.inc(ins, "VEq")
                v.wait_ge(K.sems["VEq"], K.c["VEq"])
                ins = v.tensor_tensor(smo(t, 2), smo(t, 1), smo(t, 2), op=ALU.subtract)
                K.inc(ins, "VEq")
                v.wait_ge(K.sems["VEq"], K.c["VEq"])
                ins = v.tensor_scalar(smo(t, 2), smo(t, 2), EPS, None, op0=ALU.add)
                K.inc(ins, "LNV")
                v.wait_ge(K.sems["LNS"], b_lns + 2 * t + 2)
                ins = v.reciprocal(smo(t, 4), smo(t, 3))
                K.inc(ins, "VEq")
                if (not sink_persistent) and t >= 2:
                    v.wait_ge(K.sems["TRT"], b_trt + (t - 2) + 1)
                v.wait_ge(K.sems["VEq"], K.c["VEq"])
                ins = v.tensor_scalar(sink(t), xt, smo(t, 0), smo(t, 4),
                                      op0=ALU.subtract, op1=ALU.mult)
                K.inc(ins, "LNV")

                if t >= 1:
                    tp = t - 1
                    v.wait_ge(K.sems["TRT"], b_trt + tp + 1)
                    dst = AA[:].rearrange("p (c n) -> p c n", c=8)[:, :, tp * 128:(tp + 1) * 128]
                    ins = v.tensor_copy(dst, ps_tr[:, tp % 2, :, :])
                    K.inc(ins, "TRE")
            tp = TT - 1
            v.wait_ge(K.sems["TRT"], b_trt + tp + 1)
            dst = AA[:].rearrange("p (c n) -> p c n", c=8)[:, :, tp * 128:(tp + 1) * 128]
            ins = v.tensor_copy(dst, ps_tr[:, tp % 2, :, :])
            K.inc(ins, "TRE")

        @b.tensor
        def _(te):
            for t in range(TT):
                te.wait_ge(K.sems["LNV"], b_lnv + 2 * t + 2)
                if t >= 2:
                    te.wait_ge(K.sems["TRE"], b_tre + (t - 2) + 1)
                lt = sink(t)
                for c in range(8):
                    ins = te.transpose(ps_tr[:, t % 2, c, :], lt[:, c * 128:(c + 1) * 128], IDN[:])
                K.inc(ins, "TRT")

        if park_x:
            @b.sync
            def _(sy):
                for t in range(TT):
                    sy.wait_ge(K.sems["LNV"], b_lnv + 2 * t + 2)
                    ins = sy.dma_start(out=x_park[t * 128:(t + 1) * 128, :],
                                       in_=AX[:, t * 1024:(t + 1) * 1024])
                    K.inc(ins, "XPK", 16)
    barrier(K)





def emit_transpose_tiles(K, st, src, dst, nt, nch=8):
    nc = K.nc
    K.uid = getattr(K, "uid", 0) + 1
    IDN = st["IDN"]
    b_t = K.c.get("TRT", 0); K.sem("TRT")
    b_e = K.c.get("TRE", 0); K.sem("TRE")
    with (nc.psum_tensor("ps_gtr%d" % K.uid, [128, 2, nch, 128], BF16) as ps_tr,
          nc.Block("gtr%d" % K.uid) as b):
        @b.tensor
        def _(te):
            for t in range(nt):
                if t >= 2:
                    te.wait_ge(K.sems["TRE"], b_e + (t - 2) + 1)
                s = src(t)
                for c in range(nch):
                    ins = te.transpose(ps_tr[:, t % 2, c, :], s[:, c * 128:(c + 1) * 128], IDN[:])
                K.inc(ins, "TRT")

        @b.vector
        def _(v):
            for t in range(nt):
                v.wait_ge(K.sems["TRT"], b_t + t + 1)
                ins = v.tensor_copy(dst(t), ps_tr[:, t % 2, 0:nch, :])
                K.inc(ins, "TRE")
    barrier(K)



def emit_attn(K, st, layer):
    nc = K.nc
    L_ = layer
    TT = st["TT"]; T = st["T"]
    AX, AXb, AA, AB, AC = st["AX"], st["AXb"], st["AA"], st["AB"], st["AC"]
    WB, S32, SM, SMB = st["WB"], st["S32"], st["SM"], st["SMB"]
    ONEB, COS, SIN, IDN, IDF, TRI, HCF = (st["ONEB"], st["COS"], st["SIN"],
                                          st["IDN"], st["IDF"], st["TRI"], st["HCF"])
    inp, x_park, sc_park = st["inp"], st["x_park"], st["sc_park"]
    causal = (layer == 1)
    L = "L0_attn_" if layer == 0 else "L1_attn_"
    nvg = NH if layer == 0 else KVH
    vw = 65 * nvg

    def lnT(c):
        return AA[:, c * 2048:(c + 1) * 2048]

    def qT(c):
        return AB[:, c * 2048:(c + 1) * 2048]

    def kTp(c):
        return AXb[:, c * 2048:(c + 1) * 2048]

    def stag(par):
        return AXb[:, 16384 + par * 2048:16384 + (par + 1) * 2048]

    def expb(par):
        return AXb[:, 20480 + par * 2048:20480 + (par + 1) * 2048]

    def qrot(par):
        return AXb[:, 24576 + par * 1024:24576 + (par + 1) * 1024]

    def ktokbuf(par):
        return AXb[:, 26624 + par * 384:26624 + (par + 1) * 384]

    def vext(t):
        return AC[:, t * vw:t * vw + vw]

    def wqt(dcin):
        return AC[:, 4352 + dcin * 1024:4352 + (dcin + 1) * 1024]

    def wkt(dcin):
        return AC[:, 12544 + dcin * 256:12544 + (dcin + 1) * 256]

    def wvt(dcin):
        return AC[:, 14592 + dcin * 256:14592 + (dcin + 1) * 256]

    emit_ln_lnT(K, st, park_x=True)

    with nc.Block("aprep%d" % layer) as b:
        @b.vector
        def _(v):
            v.memset(stag(0)[64:128, :], 0.0)
            v.memset(stag(1)[0:64, :], 0.0)
            v.memset(SM[:, 64:80], 0.0)
            if layer == 1:
                v.memset(ktokbuf(0), 0.0)
                v.memset(ktokbuf(1), 0.0)
            ins = v.memset(AC[:], 1.0)
            K.inc(ins, "VE")
    barrier(K)

    if layer == 0:
        _l0_projections(K, st, L, lnT, qT, kTp, qrot, vext)
    else:
        _l1_projections(K, st, L, lnT, qT, kTp, ktokbuf, vext, wqt, wkt, wvt)
    barrier(K)

    _emit_negM(K, st, layer)
    barrier(K)

    QC = (TT * 128) // 512 if TT >= 4 else 1
    qcw = 512 if TT >= 4 else TT * 128
    b_sg = K.c.get("SG", 0); K.sem("SG")
    b_ss = K.c.get("SS", 0); K.sem("SS")
    b_se = K.c.get("SE", 0); K.sem("SE")
    b_sav = K.c.get("SAV", 0); K.sem("SAV")
    b_msk = K.c.get("MSK", 0); K.sem("MSK")
    b_nrm = K.c.get("NRM2", 0); K.sem("NRM2")

    def qcs(kt):
        if not causal:
            return list(range(QC))
        return [qc for qc in range(QC) if (qc + 1) * qcw > kt * 128]

    def qts(kt):
        return [q for q in range(TT) if (q >= kt or not causal)]

    nexp_h = [sum(len(qcs(kt)) for kt in range(TT)) for _ in range(NH)]
    ss_after = [b_ss + nexp_h[0] * (h + 1) for h in range(NH)]
    nbank = max(1, (TT * 128 * 4) // 2048)
    zw = min(512, TT * 128)

    with (nc.psum_tensor("ps_sc_l%d" % layer, [128, 4, 512], F32) as ps_sc,
          nc.psum_tensor("ps_ctx_l%d" % layer, [128, TT * 128], F32) as ps_ctx,
          nc.Block("heads%d" % layer) as b):

        @b.gpsimd
        def _(gp):
            for h in range(NH):
                par_h = h % 2
                if h >= 2:
                    gp.wait_ge(K.sems["SS"], ss_after[h - 2])
                src_c = (h // 2) if layer == 0 else (h // 4)
                ins = gp.tensor_copy(stag(par_h)[64 * par_h:64 * par_h + 64, 0:T],
                                     kTp(src_c)[64 * par_h:64 * par_h + 64, 0:T])
                K.inc(ins, "SG")

        @b.tensor
        def _(te):
            g_idx = 0
            for h in range(NH):
                par_h = h % 2
                ch = h // 2
                g = h if layer == 0 else h // 4
                te.wait_ge(K.sems["SG"], b_sg + h + 1)
                if h >= 1:
                    te.wait_ge(K.sems["NRM2"], b_nrm + TT * h)
                for bk in range(nbank):
                    te.matmul(ps_ctx[:, bk * zw:(bk + 1) * zw],
                              stag(0)[64:65, 0:128], stag(0)[64:65, 0:zw],
                              start=True, stop=True)
                for kt in range(TT):
                    gkt = h * TT + kt
                    if gkt >= 2:
                        te.wait_ge(K.sems["SAV"], b_sav + gkt - 1)
                    for qc in qcs(kt):
                        if g_idx >= 4:
                            te.wait_ge(K.sems["SE"], b_se + g_idx - 3)
                        ins = te.matmul(ps_sc[:, g_idx % 4, 0:qcw],
                                        stag(par_h)[:, kt * 128:(kt + 1) * 128],
                                        qT(ch)[:, qc * qcw:(qc + 1) * qcw],
                                        start=True, stop=True)
                        K.inc(ins, "SS")
                        g_idx += 1
                    te.wait_ge(K.sems["SE"], b_se + nexp_h[0] * h + sum(len(qcs(k2)) for k2 in range(kt + 1)))
                    if causal:
                        te.wait_ge(K.sems["MSK"], b_msk + h * TT + kt + 1)
                    for qt in qts(kt):
                        last = (kt == (qt if causal else TT - 1))
                        ins = te.matmul(ps_ctx[:, qt * 128:qt * 128 + 65],
                                        expb(kt % 2)[:, qt * 128:(qt + 1) * 128],
                                        vext(kt)[:, 65 * g:65 * g + 65],
                                        start=False, stop=last,
                                        skip_group_check=True)
                    K.inc(ins, "SAV")

        @b.scalar
        def _(s):
            g_idx = 0
            for h in range(NH):
                for kt in range(TT):
                    gkt = h * TT + kt
                    if gkt >= 2:
                        s.wait_ge(K.sems["SAV"], b_sav + gkt - 1)
                    if causal and gkt >= 1:
                        s.wait_ge(K.sems["MSK"], b_msk + max(0, gkt - 1))
                    for qc in qcs(kt):
                        s.wait_ge(K.sems["SS"], b_ss + g_idx + 1)
                        ins = s.activation(expb(kt % 2)[:, qc * qcw:(qc + 1) * qcw],
                                           ps_sc[:, g_idx % 4, 0:qcw], AF.Exp,
                                           bias=SM[:, 112 + h:113 + h], scale=0.125)
                        K.inc(ins, "SE")
                        g_idx += 1

        @b.vector
        def _(v):
            for h in range(NH):
                if causal:
                    for kt in range(TT):
                        v.wait_ge(K.sems["SE"], b_se + nexp_h[0] * h + sum(len(qcs(k2)) for k2 in range(kt + 1)))
                        sl = expb(kt % 2)[:, kt * 128:(kt + 1) * 128]
                        ins = v.tensor_tensor(sl, sl, TRI[:], op=ALU.mult)
                        K.inc(ins, "MSK")
                v.wait_ge(K.sems["SAV"], b_sav + TT * (h + 1))
                for qt in range(TT):
                    v.wait_ge(K.sems["NRM2"], K.c["NRM2"])
                    ins = v.reciprocal(SM[:, 70:71], ps_ctx[:, qt * 128 + 64:qt * 128 + 65])
                    K.inc(ins, "VE")
                    v.wait_ge(K.sems["VE"], K.c["VE"])
                    ins = v.tensor_scalar(AA[:, qt * 1024 + 64 * h:qt * 1024 + 64 * h + 64],
                                          ps_ctx[:, qt * 128:qt * 128 + 64],
                                          SM[:, 70:71], None, op0=ALU.mult)
                    K.inc(ins, "NRM2")

    barrier(K)

    emit_transpose_tiles(K, st,
                         src=lambda t: AA[:, t * 1024:(t + 1) * 1024],
                         dst=lambda t: AB[:].rearrange("p (c n) -> p c n", c=8)[:, :, t * 128:(t + 1) * 128],
                         nt=TT)

    b_wl = K.c.get("WL", 0); K.sem("WL")
    b_po = K.c.get("PO", 0); K.sem("PO")
    b_res = K.c.get("RES", 0); K.sem("RES")
    b_xlb = [K.c.get("XLB0", 0), K.c.get("XLB1", 0)]; K.sem("XLB0"); K.sem("XLB1")
    b_wtt = K.c.get("WTT", 0); K.sem("WTT")
    b_wte = K.c.get("WTE", 0); K.sem("WTE")

    with (nc.psum_tensor("ps_o_l%d" % layer, [128, 2, 1024], F32) as ps_o,
          nc.psum_tensor("ps_wt_l%d" % layer, [128, 2, 8, 128], BF16) as ps_wt,
          nc.Block("attout%d" % layer) as b):

        @b.gpsimd
        def _(gp):
            for c in range(8):
                ins = gp.dma_start(out=WB[:, c * 1024:(c + 1) * 1024],
                                   in_=inp[L + "Wo"][c * 128:(c + 1) * 128, :])
                K.inc(ins, "WL", 16)
            if layer == 0:
                ins = gp.dma_start(out=SMB[0:1, 0:1024],
                                   in_=inp[L + "bo"].rearrange("(a d) -> a d", a=1))
                K.inc(ins, "WL", 16)

        if layer == 1:
            @b.tensor
            def _(te):
                te.wait_ge(K.sems["WL"], b_wl + 16 * 8)
                for rc in range(8):
                    if rc >= 2:
                        te.wait_ge(K.sems["WTE"], b_wte + rc - 1)
                    for dc in range(8):
                        ins = te.transpose(ps_wt[:, rc % 2, dc, :],
                                           WB[:, rc * 1024 + dc * 128:rc * 1024 + (dc + 1) * 128],
                                           IDN[:])
                    K.inc(ins, "WTT")

            @b.vector
            def _(v):
                for rc in range(8):
                    v.wait_ge(K.sems["WTT"], b_wtt + rc + 1)
                    dst = AC[:, 0:8192].rearrange("p (c n) -> p c n", c=8)[:, :, rc * 128:(rc + 1) * 128]
                    ins = v.tensor_copy(dst, ps_wt[:, rc % 2, 0:8, :])
                    K.inc(ins, "WTE")

        @b.sync
        def _(sy):
            for t in range(TT):
                if t >= 2:
                    sy.wait_ge(K.sems["RES"], b_res + t - 1)
                ins = sy.dma_start(out=S32[:, (t % 2) * 1024:((t % 2) + 1) * 1024],
                                   in_=x_park[t * 128:(t + 1) * 128, :])
                K.inc(ins, "XLB" + str(t % 2), 16)

        def wo_rhs(c):
            if layer == 0:
                return WB[:, c * 1024:(c + 1) * 1024]
            return AC[:, c * 1024:(c + 1) * 1024]

        @b.tensor
        def _(te):
            if layer == 0:
                te.wait_ge(K.sems["WL"], b_wl + 16 * 9)
            else:
                te.wait_ge(K.sems["WTE"], b_wte + 8)
            for t in range(TT):
                if t >= 2:
                    te.wait_ge(K.sems["RES"], b_res + t - 1)
                for c in range(8):
                    for n in range(2):
                        ins = te.matmul(ps_o[:, t % 2, n * 512:(n + 1) * 512],
                                        AB[:, c * 2048 + t * 128:c * 2048 + (t + 1) * 128],
                                        wo_rhs(c)[:, n * 512:(n + 1) * 512],
                                        start=(c == 0), stop=(c == 7 and layer == 1))
                if layer == 0:
                    for n in range(2):
                        ins = te.matmul(ps_o[:, t % 2, n * 512:(n + 1) * 512],
                                        ONEB[0:1, 0:128],
                                        SMB[0:1, n * 512:(n + 1) * 512],
                                        start=False, stop=True)
                K.inc(ins, "PO")

        @b.vector
        def _(v):
            for t in range(TT):
                v.wait_ge(K.sems["PO"], b_po + t + 1)
                v.wait_ge(K.sems["XLB" + str(t % 2)], b_xlb[t % 2] + 16 * (t // 2 + 1))
                ins = v.tensor_tensor(AX[:, t * 1024:(t + 1) * 1024],
                                      ps_o[:, t % 2, :],
                                      S32[:, (t % 2) * 1024:((t % 2) + 1) * 1024],
                                      op=ALU.add)
                K.inc(ins, "RES")
    barrier(K)



def _l0_projections(K, st, L, lnT, qT, kTp, qrot, vext):
    nc = K.nc
    TT = st["TT"]
    AX, AA, AB, AC = st["AX"], st["AA"], st["AB"], st["AC"]
    WB, S32, SM, SMB = st["WB"], st["S32"], st["SM"], st["SMB"]
    ONEB, COS, SIN, IDN, IDF = st["ONEB"], st["COS"], st["SIN"], st["IDN"], st["IDF"]
    inp = st["inp"]

    def q32s(t):
        return S32[:, (t % 2) * 1024:((t % 2) + 1) * 1024]

    T1 = S32[:, 2048:2560]
    T2 = S32[:, 2560:3072]

    for proj in ("q", "k", "v"):
        b_wl = K.c.get("WL", 0); K.sem("WL")
        b_pp = K.c.get("PP", 0); K.sem("PP")
        b_qe = K.c.get("QE", 0); K.sem("QE")
        b_qr = K.c.get("QR", 0); K.sem("QR")
        b_trt = K.c.get("TRT", 0); K.sem("TRT")
        b_tre = K.c.get("TRE", 0); K.sem("TRE")
        b_nrt = K.c.get("NRT", 0); K.sem("NRT")
        b_nrm = K.c.get("NRM", 0); K.sem("NRM")
        Wn, bn = L + "W" + proj, L + "b" + proj
        runslot = {"q": 64, "k": 65}.get(proj)

        def tdst(tp):
            if proj == "q":
                return AB[:].rearrange("p (c n) -> p c n", c=8)[:, :, tp * 128:(tp + 1) * 128]
            return st["AXb"][:, 0:16384].rearrange("p (c n) -> p c n", c=8)[:, :, tp * 128:(tp + 1) * 128]

        with (nc.psum_tensor("ps_p" + proj, [128, 2, 1024], F32) as ps_p,
              nc.psum_tensor("ps_t" + proj, [128, 2, 8, 128], BF16) as ps_tr,
              nc.psum_tensor("ps_n" + proj, [16, 128], F32) as ps_n,
              nc.Block("proj_" + L + proj) as b):

            @b.gpsimd
            def _(gp):
                for c in range(8):
                    ins = gp.dma_start(out=WB[:, c * 1024:(c + 1) * 1024],
                                       in_=inp[Wn][c * 128:(c + 1) * 128, :])
                    K.inc(ins, "WL", 16)
                ins = gp.dma_start(out=SMB[0:1, 0:1024],
                                   in_=inp[bn].rearrange("(a d) -> a d", a=1))
                K.inc(ins, "WL", 16)

            def emit_proj_mms(te, t):
                for c in range(8):
                    for n in range(2):
                        te.matmul(ps_p[:, t % 2, n * 512:(n + 1) * 512],
                                  lnT(c)[:, t * 128:(t + 1) * 128],
                                  WB[:, c * 1024 + n * 512:c * 1024 + (n + 1) * 512],
                                  start=(c == 0), stop=False)
                for n in range(2):
                    ins = te.matmul(ps_p[:, t % 2, n * 512:(n + 1) * 512],
                                    ONEB[0:1, 0:128],
                                    SMB[0:1, n * 512:(n + 1) * 512],
                                    start=False, stop=True)
                K.inc(ins, "PP")

            def emit_transposes(te, tp):
                te.wait_ge(K.sems["QR"], b_qr + tp + 1)
                te.wait_ge(K.sems["NRM"], b_nrm + 2 * tp)
                nrm = SM[:, 32 + 16 * (tp % 2):48 + 16 * (tp % 2)]
                ins = te.transpose(ps_n[:], nrm, IDF[:])
                K.inc(ins, "NRT")
                if tp >= 2:
                    te.wait_ge(K.sems["TRE"], b_tre + tp - 1)
                for c in range(8):
                    ins = te.transpose(ps_tr[:, tp % 2, c, :],
                                       qrot(tp % 2)[:, c * 128:(c + 1) * 128], IDN[:])
                K.inc(ins, "TRT")

            if proj == "v":
                @b.tensor
                def _(te):
                    te.wait_ge(K.sems["WL"], b_wl + 16 * 9)
                    for t in range(TT):
                        if t >= 2:
                            te.wait_ge(K.sems["QE"], b_qe + t - 1)
                        emit_proj_mms(te, t)

                @b.scalar
                def _(s):
                    for t in range(TT):
                        s.wait_ge(K.sems["PP"], b_pp + t + 1)
                        dst = vext(t).rearrange("p (g w) -> p g w", w=65)[:, :, 0:64]
                        ins = s.copy(dst, ps_p[:, t % 2, :].rearrange("p (g w) -> p g w", w=64))
                        K.inc(ins, "QE")
            else:
                @b.tensor
                def _(te):
                    te.wait_ge(K.sems["WL"], b_wl + 16 * 9)
                    for t in range(TT):
                        if t >= 2:
                            te.wait_ge(K.sems["QE"], b_qe + t - 1)
                        emit_proj_mms(te, t)
                        if t >= 1:
                            emit_transposes(te, t - 1)
                    emit_transposes(te, TT - 1)

                @b.scalar
                def _(s):
                    for t in range(TT):
                        s.wait_ge(K.sems["PP"], b_pp + t + 1)
                        if t >= 2:
                            s.wait_ge(K.sems["QR"], b_qr + t - 1)
                        ins = s.copy(q32s(t), ps_p[:, t % 2, :])
                        K.inc(ins, "QE")

                @b.vector
                def _(v):
                    for t in range(TT):
                        v.wait_ge(K.sems["QE"], b_qe + t + 1)
                        if t >= 2:
                            v.wait_ge(K.sems["TRT"], b_trt + t - 1)
                        if t >= 1:
                            v.wait_ge(K.sems["QR"], b_qr + t)
                        x32 = q32s(t).rearrange("p (h two c) -> p h two c", two=2, c=32)
                        x1, x2 = x32[:, :, 0, :], x32[:, :, 1, :]
                        qr = qrot(t % 2).rearrange("p (h two c) -> p h two c", two=2, c=32)
                        cosb = COS[:, t * 32:(t + 1) * 32].unsqueeze(1).broadcast_to([128, 16, 32])
                        sinb = SIN[:, t * 32:(t + 1) * 32].unsqueeze(1).broadcast_to([128, 16, 32])
                        t1v = T1.rearrange("p (h c) -> p h c", c=32)
                        t2v = T2.rearrange("p (h c) -> p h c", c=32)
                        v.wait_ge(K.sem("VEq"), K.c["VEq"])
                        ins = v.tensor_tensor(t1v, x2, sinb, op=ALU.mult)
                        K.inc(ins, "VEq")
                        ins = v.tensor_tensor(t2v, x1, cosb, op=ALU.mult)
                        K.inc(ins, "VEq")
                        v.wait_ge(K.sems["VEq"], K.c["VEq"])
                        ins = v.tensor_tensor(qr[:, :, 0, :], t2v, t1v, op=ALU.subtract)
                        K.inc(ins, "VEq")
                        v.wait_ge(K.sems["VEq"], K.c["VEq"])
                        ins = v.tensor_tensor(t1v, x1, sinb, op=ALU.mult)
                        K.inc(ins, "VEq")
                        ins = v.tensor_tensor(t2v, x2, cosb, op=ALU.mult)
                        K.inc(ins, "VEq")
                        v.wait_ge(K.sems["VEq"], K.c["VEq"])
                        ins = v.tensor_tensor(qr[:, :, 1, :], t1v, t2v, op=ALU.add)
                        K.inc(ins, "VEq")
                        nrm = SM[:, 32 + 16 * (t % 2):48 + 16 * (t % 2)]
                        for hh in range(2):
                            v.wait_ge(K.sems["VEq"], K.c["VEq"])
                            ins = v.tensor_tensor(T1, sl8 := q32s(t)[:, hh * 512:(hh + 1) * 512], sl8, op=ALU.mult)
                            K.inc(ins, "VEq")
                            v.wait_ge(K.sems["VEq"], K.c["VEq"])
                            ins = v.reduce_sum(nrm[:, hh * 8:(hh + 1) * 8],
                                               T1.rearrange("p (h c) -> p h c", c=64), axis=AXX)
                            if hh == 0:
                                K.inc(ins, "VEq")
                        K.inc(ins, "QR")
                        if t >= 1:
                            _norm_and_qt_evac(K, v, t - 1, b_nrt, b_trt, ps_n, ps_tr,
                                              SM, runslot, tdst)
                    _norm_and_qt_evac(K, v, TT - 1, b_nrt, b_trt, ps_n, ps_tr,
                                      SM, runslot, tdst)
        barrier(K)


def _norm_and_qt_evac(K, v, tp, b_nrt, b_trt, ps_n, ps_tr, SM, runslot, tdst):
    v.wait_ge(K.sems["NRT"], b_nrt + tp + 1)
    ins = v.reduce_max(SM[0:16, 70:71], ps_n[0:16, :], axis=AXX)
    K.inc(ins, "NRM")
    v.wait_ge(K.sems["NRM"], K.c["NRM"])
    ins = v.tensor_tensor(SM[0:16, runslot:runslot + 1],
                          SM[0:16, runslot:runslot + 1],
                          SM[0:16, 70:71], op=ALU.max)
    K.inc(ins, "NRM")
    v.wait_ge(K.sems["TRT"], b_trt + tp + 1)
    ins = v.tensor_copy(tdst(tp), ps_tr[:, tp % 2, :, :])
    K.inc(ins, "TRE")



def _emit_negM(K, st, layer):
    nc = K.nc
    SM, SMB, ONEB = st["SM"], st["SMB"], st["ONEB"]
    sc_park = st["sc_park"]
    b_sy = K.c.get("NMS", 0); K.sem("NMS")
    b_ve = K.c.get("NMV", 0); K.sem("NMV")
    with (nc.psum_tensor("ps_nm_l%d" % layer, [128, 16], F32) as ps_nm,
          nc.Block("negM%d" % layer) as b):
        @b.sync
        def _(sy):
            if layer == 0:
                ins = sy.dma_start(out=sc_park[0:16, 0:2], in_=SM[0:16, 64:66],
                                   allow_slow_non_contiguous=True)
                K.inc(ins, "NMS", 16)
                sy.wait_ge(K.sems["NMS"], b_sy + 16)
                ins = sy.dma_start(out=SM[0:1, 80:96],
                                   in_=sc_park[0:16, 0:1].rearrange("p c -> c p"),
                                   allow_slow_non_contiguous=True)
                K.inc(ins, "NMS", 16)
                ins = sy.dma_start(out=SM[0:1, 96:112],
                                   in_=sc_park[0:16, 1:2].rearrange("p c -> c p"),
                                   allow_slow_non_contiguous=True)
                K.inc(ins, "NMS", 16)
            else:
                ins = sy.dma_start(out=sc_park[0:2, 0:8], in_=SM[0:2, 64:72],
                                   allow_slow_non_contiguous=True)
                K.inc(ins, "NMS", 16)
                ins = sy.dma_start(out=sc_park[0:4, 8:9], in_=SM[0:4, 72:73],
                                   allow_slow_non_contiguous=True)
                K.inc(ins, "NMS", 16)
                sy.wait_ge(K.sems["NMS"], b_sy + 32)
                ins = sy.dma_start(out=SM[0:1, 80:96].rearrange("a (c p) -> a c p", c=8),
                                   in_=sc_park[0:2, 0:8].rearrange("p c -> c p").unsqueeze(0),
                                   allow_slow_non_contiguous=True)
                K.inc(ins, "NMS", 16)
                ins = sy.dma_start(out=SM[0:1, 96:100],
                                   in_=sc_park[0:4, 8:9].rearrange("p c -> c p"),
                                   allow_slow_non_contiguous=True)
                K.inc(ins, "NMS", 16)

        @b.vector
        def _(v):
            v.wait_ge(K.sems["NMS"], K.c["NMS"])
            if layer == 0:
                ins = v.tensor_tensor(SM[0:1, 80:96], SM[0:1, 80:96], SM[0:1, 96:112], op=ALU.mult)
            else:
                a = SM[0:1, 80:96].rearrange("a (g r) -> a g r", g=4)
                nk = SM[0:1, 96:100].rearrange("a (g r) -> a g r", g=4).broadcast_to([1, 4, 4])
                ins = v.tensor_tensor(a, a, nk, op=ALU.mult)
            K.inc(ins, "NMV")

        @b.scalar
        def _(s):
            s.wait_ge(K.sems["NMV"], b_ve + 1)
            ins = s.activation(SM[0:1, 80:96], SM[0:1, 80:96], AF.Sqrt)
            K.inc(ins, "NMV")

        @b.vector
        def _(v):
            v.wait_ge(K.sems["NMV"], b_ve + 2)
            ins = v.tensor_scalar(SM[0:1, 80:96], SM[0:1, 80:96], -0.125, -0.25,
                                  op0=ALU.mult, op1=ALU.add)
            K.inc(ins, "NMV")
            v.wait_ge(K.sems["NMV"], b_ve + 3)
            ins = v.tensor_copy(SMB[0:1, 1024:1040], SM[0:1, 80:96])
            K.inc(ins, "NMV")

        @b.tensor
        def _(te):
            te.wait_ge(K.sems["NMV"], b_ve + 4)
            ins = te.matmul(ps_nm[:], ONEB[0:1, 0:128], SMB[0:1, 1024:1040],
                            start=True, stop=True)
            K.inc(ins, "NMV")

        @b.scalar
        def _(s):
            s.wait_ge(K.sems["NMV"], b_ve + 5)
            ins = s.copy(SM[:, 112:128], ps_nm[:])
            K.inc(ins, "NMV")



def _l1_projections(K, st, L, lnT, qT, kTp, ktokbuf, vext, wqt, wkt, wvt):
    nc = K.nc
    TT = st["TT"]
    AB, AC, WB, S32, SM = st["AB"], st["AC"], st["WB"], st["S32"], st["SM"]
    IDN, IDF, HCF = st["IDN"], st["IDF"], st["HCF"]
    inp = st["inp"]
    T1 = S32[:, 2048:2560]
    T2 = S32[:, 2560:3072]

    b_wl = K.c.get("WL", 0); K.sem("WL")
    b_wtl = [K.c.get("WTL0", 0), K.c.get("WTL1", 0)]; K.sem("WTL0"); K.sem("WTL1")
    b_wtt = K.c.get("WTT", 0); K.sem("WTT")
    b_wte = K.c.get("WTE", 0); K.sem("WTE")
    plan = [("Wq", 8, wqt, 1024), ("Wk", 2, wkt, 256), ("Wv", 2, wvt, 256)]
    with (nc.psum_tensor("ps_wt1", [128, 2, 8, 128], BF16) as ps_wt,
          nc.Block("l1wt") as b):
        @b.gpsimd
        def _(gp):
            i = 0
            for name, nrc, dstf, ncol in plan:
                for rc in range(nrc):
                    if i >= 2:
                        gp.wait_ge(K.sems["WTT"], b_wtt + i - 1)
                    ins = gp.dma_start(out=WB[:, (i % 2) * 1024:((i % 2) + 1) * 1024],
                                       in_=inp[L + name][rc * 128:(rc + 1) * 128, :])
                    K.inc(ins, "WTL" + str(i % 2), 16)
                    i += 1

        @b.tensor
        def _(te):
            i = 0
            for name, nrc, dstf, ncol in plan:
                for rc in range(nrc):
                    te.wait_ge(K.sems["WTL" + str(i % 2)], b_wtl[i % 2] + 16 * (i // 2 + 1))
                    if i >= 2:
                        te.wait_ge(K.sems["WTE"], b_wte + i - 1)
                    for dc in range(8):
                        ins = te.transpose(ps_wt[:, i % 2, dc, :],
                                           WB[:, (i % 2) * 1024 + dc * 128:(i % 2) * 1024 + (dc + 1) * 128],
                                           IDN[:])
                    K.inc(ins, "WTT")
                    i += 1

        @b.vector
        def _(v):
            i = 0
            for name, nrc, dstf, ncol in plan:
                for rc in range(nrc):
                    v.wait_ge(K.sems["WTT"], b_wtt + i + 1)
                    for dc in range(8):
                        ins = v.tensor_copy(dstf(dc)[:, rc * 128:(rc + 1) * 128],
                                            ps_wt[:, i % 2, dc, :])
                    K.inc(ins, "WTE")
                    i += 1
    barrier(K)

    b_qp = K.c.get("QP", 0); K.sem("QP")
    b_qev = K.c.get("QEv", 0); K.sem("QEv")
    b_ns = K.c.get("NS", 0); K.sem("NS")
    b_nmm = K.c.get("NMM", 0); K.sem("NMM")
    b_nrm = K.c.get("NRM", 0); K.sem("NRM")
    NQ = (TT * 128) // 512 if TT >= 4 else 1
    qw = 512 if TT >= 4 else TT * 128
    iters = [(dco, n) for dco in range(8) for n in range(NQ)]
    with (nc.psum_tensor("ps_q1", [128, 2, 512], F32) as ps_q,
          nc.psum_tensor("ps_nn", [2, 2, 512], F32) as ps_nn,
          nc.Block("l1q") as b):
        @b.tensor
        def _(te):
            for i, (dco, n) in enumerate(iters):
                if i >= 2:
                    te.wait_ge(K.sems["QEv"], b_qev + i - 1)
                for dci in range(8):
                    ins = te.matmul(ps_q[:, i % 2, 0:qw],
                                    wqt(dci)[:, dco * 128:(dco + 1) * 128],
                                    lnT(dci)[:, n * qw:(n + 1) * qw],
                                    start=(dci == 0), stop=(dci == 7))
                K.inc(ins, "QP")
                if i >= 1:
                    te.wait_ge(K.sems["NS"], b_ns + i)
                    if i >= 3:
                        te.wait_ge(K.sems["NRM"], b_nrm + 2 * (i - 2))
                    ins = te.matmul(ps_nn[:, (i - 1) % 2, 0:qw], HCF[:],
                                    (T1 if (i - 1) % 2 == 0 else T2)[:, 0:qw],
                                    start=True, stop=True)
                    K.inc(ins, "NMM")
            i = len(iters)
            te.wait_ge(K.sems["NS"], b_ns + i)
            te.wait_ge(K.sems["NRM"], b_nrm + 2 * (i - 2))
            ins = te.matmul(ps_nn[:, (i - 1) % 2, 0:qw], HCF[:],
                            (T1 if (i - 1) % 2 == 0 else T2)[:, 0:qw],
                            start=True, stop=True)
            K.inc(ins, "NMM")

        @b.scalar
        def _(s):
            for i, (dco, n) in enumerate(iters):
                s.wait_ge(K.sems["QP"], b_qp + i + 1)
                dst = qT(dco)[:, n * qw:(n + 1) * qw]
                ins = s.copy(dst, ps_q[:, i % 2, 0:qw])
                K.inc(ins, "QEv")
                if i >= 2:
                    s.wait_ge(K.sems["NMM"], b_nmm + i - 1)
                s.wait_ge(K.sems["QEv"], K.c["QEv"])
                ins = s.activation((T1 if i % 2 == 0 else T2)[:, 0:qw], dst, AF.Square)
                K.inc(ins, "NS")

        @b.vector
        def _(v):
            for i, (dco, n) in enumerate(iters):
                v.wait_ge(K.sems["NMM"], b_nmm + i + 1)
                v.wait_ge(K.sems["NRM"], K.c["NRM"])
                ins = v.reduce_max(SM[0:2, 70:71], ps_nn[:, i % 2, 0:qw], axis=AXX)
                K.inc(ins, "NRM")
                v.wait_ge(K.sems["NRM"], K.c["NRM"])
                ins = v.tensor_tensor(SM[0:2, 64 + dco:65 + dco],
                                      SM[0:2, 64 + dco:65 + dco],
                                      SM[0:2, 70:71], op=ALU.max)
                K.inc(ins, "NRM")
    barrier(K)

    b_kp = K.c.get("KP", 0); K.sem("KP"); K.sem("VEq")
    b_ke = K.c.get("KE", 0); K.sem("KE")
    b_kr = K.c.get("KR", 0); K.sem("KR")
    b_ktt = K.c.get("KTT", 0); K.sem("KTT")
    b_kte = K.c.get("KTE", 0); K.sem("KTE")
    b_nrt = K.c.get("NRT", 0); K.sem("NRT")
    b_nrm2 = K.c.get("NRM", 0)
    gp_pairs = [(j, g, p) for j in range(5) for g in range(4) for p in range(2)
                if 1 + g - p == j]
    with (nc.psum_tensor("ps_k1", [128, 2, 512], F32) as ps_k,
          nc.psum_tensor("ps_kt", [128, 2, 8, 128], BF16) as ps_kt,
          nc.psum_tensor("ps_n1", [16, 128], F32) as ps_n,
          nc.Block("l1k") as b):
        @b.tensor
        def _(te):
            for t in range(TT):
                if t >= 2:
                    te.wait_ge(K.sems["KE"], b_ke + t - 1)
                for dci in range(8):
                    ins = te.matmul(ps_k[:, t % 2, 0:256],
                                    lnT(dci)[:, t * 128:(t + 1) * 128],
                                    wkt(dci),
                                    start=(dci == 0), stop=(dci == 7))
                K.inc(ins, "KP")
                if t >= 1:
                    _l1k_transposes(K, te, t - 1, ktokbuf, ps_kt, ps_n, SM, IDN, IDF,
                                    b_kr, b_kte, b_nrm2)
            _l1k_transposes(K, te, TT - 1, ktokbuf, ps_kt, ps_n, SM, IDN, IDF,
                            b_kr, b_kte, b_nrm2)

        @b.scalar
        def _(s):
            for t in range(TT):
                s.wait_ge(K.sems["KP"], b_kp + t + 1)
                if t >= 2:
                    s.wait_ge(K.sems["KR"], b_kr + t - 1)
                ins = s.copy(ktokbuf(t % 2)[:, 64:320], ps_k[:, t % 2, 0:256])
                K.inc(ins, "KE")

        @b.vector
        def _(v):
            for t in range(TT):
                v.wait_ge(K.sems["KE"], b_ke + t + 1)
                if t >= 1:
                    v.wait_ge(K.sems["KR"], b_kr + t)
                sl = ktokbuf(t % 2)[:, 64:320]
                ins = v.tensor_tensor(T1[:, 0:256], sl, sl, op=ALU.mult)
                K.inc(ins, "VEq")
                v.wait_ge(K.sems["VEq"], K.c["VEq"])
                ins = v.reduce_sum(SM[:, 32 + 16 * (t % 2):36 + 16 * (t % 2)],
                                   T1[:, 0:256].rearrange("p (g c) -> p g c", c=64),
                                   axis=AXX)
                K.inc(ins, "KR")
                if t >= 1:
                    _l1k_nrm_evac(K, v, t - 1, ps_kt, ps_n, kTp, SM, gp_pairs,
                                  b_ktt, b_nrt)
            _l1k_nrm_evac(K, v, TT - 1, ps_kt, ps_n, kTp, SM, gp_pairs,
                          b_ktt, b_nrt)
    barrier(K)

    b_vp = K.c.get("VP", 0); K.sem("VP")
    b_vev = K.c.get("VEv", 0); K.sem("VEv")
    with (nc.psum_tensor("ps_v1", [128, 2, 512], F32) as ps_v,
          nc.Block("l1v") as b):
        @b.tensor
        def _(te):
            for t in range(TT):
                if t >= 2:
                    te.wait_ge(K.sems["VEv"], b_vev + t - 1)
                for dci in range(8):
                    ins = te.matmul(ps_v[:, t % 2, 0:256],
                                    lnT(dci)[:, t * 128:(t + 1) * 128],
                                    wvt(dci),
                                    start=(dci == 0), stop=(dci == 7))
                K.inc(ins, "VP")

        @b.scalar
        def _(s):
            for t in range(TT):
                s.wait_ge(K.sems["VP"], b_vp + t + 1)
                dst = vext(t).rearrange("p (g w) -> p g w", w=65)[:, :, 0:64]
                ins = s.copy(dst, ps_v[:, t % 2, 0:256].rearrange("p (g w) -> p g w", w=64))
                K.inc(ins, "VEv")
    barrier(K)


def _l1k_transposes(K, te, tp, ktokbuf, ps_kt, ps_n, SM, IDN, IDF, b_kr, b_kte, b_nrm2):
    te.wait_ge(K.sems["KR"], b_kr + tp + 1)
    te.wait_ge(K.sems["NRM"], b_nrm2 + 2 * tp)
    ins = te.transpose(ps_n[:], SM[:, 32 + 16 * (tp % 2):48 + 16 * (tp % 2)], IDF[:])
    K.inc(ins, "NRT")
    if tp >= 2:
        te.wait_ge(K.sems["KTE"], b_kte + tp - 1)
    for j in range(5):
        ins = te.transpose(ps_kt[:, tp % 2, j, :],
                           ktokbuf(tp % 2)[:, j * 64:j * 64 + 128], IDN[:])
    K.inc(ins, "KTT")


def _l1k_nrm_evac(K, v, tp, ps_kt, ps_n, kTp, SM, gp_pairs, b_ktt, b_nrt):
    v.wait_ge(K.sems["NRT"], b_nrt + tp + 1)
    ins = v.reduce_max(SM[0:4, 70:71], ps_n[0:4, :], axis=AXX)
    K.inc(ins, "NRM")
    v.wait_ge(K.sems["NRM"], K.c["NRM"])
    ins = v.tensor_tensor(SM[0:4, 72:73], SM[0:4, 72:73], SM[0:4, 70:71], op=ALU.max)
    K.inc(ins, "NRM")
    v.wait_ge(K.sems["KTT"], b_ktt + tp + 1)
    for j, g, p in gp_pairs:
        ins = v.tensor_copy(kTp(g)[64 * p:64 * p + 64, tp * 128:(tp + 1) * 128],
                            ps_kt[64 * p:64 * p + 64, tp % 2, j, :])
    K.inc(ins, "KTE")



def emit_moe(K, st):
    nc = K.nc
    TT = st["TT"]; T = st["T"]
    AX, AA, AB, AC = st["AX"], st["AA"], st["AB"], st["AC"]
    WB, S32, SM, SMB, ONEB = st["WB"], st["S32"], st["SM"], st["SMB"], st["ONEB"]
    inp = st["inp"]
    NTT = max(1, T // 256)
    nsub = 2 if T >= 256 else 1

    def lnT(c):
        return AA[:, c * 2048:(c + 1) * 2048]

    emit_ln_lnT(K, st, park_x=False)

    b_rl = K.c.get("RL", 0); K.sem("RL")
    b_rt = K.c.get("RTT", 0); K.sem("RTT")
    b_rp = K.c.get("RP", 0); K.sem("RP")
    b_rv = K.c.get("RV", 0); K.sem("RV")
    b_rs = K.c.get("RS", 0); K.sem("RS")
    with (nc.psum_tensor("ps_rt", [128, 2, 512], F32) as ps_rt,
          nc.Block("router") as b):
        @b.vector
        def _(v):
            ins = v.memset(WB[0:16, 4224:5248], 0.0)
            K.inc(ins, "RMZ")

        @b.gpsimd
        def _(gp):
            gp.wait_ge(K.sem("RMZ"), K.c["RMZ"])
            ins = gp.dma_start(out=WB[0:8, 4224:5248], in_=inp["L0_ffn_router_w"])
            K.inc(ins, "RL", 16)

        @b.sync
        def _(sy):
            sy.wait_ge(K.sems["RL"], b_rl + 16)
            for c in range(8):
                ins = sy.dma_start_transpose(WB[:, 4096 + c * 16:4096 + (c + 1) * 16],
                                             WB[0:16, 4224 + c * 128:4224 + (c + 1) * 128])
                K.inc(ins, "RTT", 16)

        @b.tensor
        def _(te):
            te.wait_ge(K.sems["RTT"], b_rt + 128)
            for t in range(TT):
                if t >= 2:
                    te.wait_ge(K.sems["RV"], b_rv + 16 * (t - 2) + 1)
                for c in range(8):
                    ins = te.matmul(ps_rt[:, t % 2, 0:8],
                                    lnT(c)[:, t * 128:(t + 1) * 128],
                                    WB[:, 4096 + c * 16:4096 + c * 16 + 8],
                                    start=(c == 0), stop=(c == 7))
                K.inc(ins, "RP")

        @b.vector
        def _(v):
            for t in range(TT):
                v.wait_ge(K.sems["RP"], b_rp + t + 1)
                lg = SM[:, 256:264]; p = SM[:, 264:272]; pw = SM[:, 272:280]
                c0 = SM[:, 280:281]; c1 = SM[:, 281:282]; c2 = SM[:, 282:283]
                c3 = SM[:, 283:284]; c4 = SM[:, 284:285]
                if t > 0:
                    v.wait_ge(K.sems["RV"], K.c["RV"])
                ins = v.tensor_copy(lg, ps_rt[:, t % 2, 0:8]); K.inc(ins, "RV")
                v.wait_ge(K.sems["RV"], K.c["RV"])
                ins = v.reduce_max(c0, lg, axis=AXX); K.inc(ins, "RV")
                v.wait_ge(K.sems["RV"], K.c["RV"])
                ins = v.tensor_scalar(c0, c0, -1.0, None, op0=ALU.mult); K.inc(ins, "RV")
                v.wait_ge(K.sems["RS"], b_rs + t + 1)
                ins = v.reciprocal(c2, c1); K.inc(ins, "RV")
                v.wait_ge(K.sems["RV"], K.c["RV"])
                ins = v.tensor_scalar(p, p, c2, None, op0=ALU.mult); K.inc(ins, "RV")
                v.wait_ge(K.sems["RV"], K.c["RV"])
                ins = v.reduce_max(c3, p, axis=AXX); K.inc(ins, "RV")
                v.wait_ge(K.sems["RV"], K.c["RV"])
                ins = v.tensor_scalar(pw, p, c3, None, op0=ALU.is_ge); K.inc(ins, "RV")
                v.wait_ge(K.sems["RV"], K.c["RV"])
                ins = v.tensor_tensor(lg, p, pw, op=ALU.subtract); K.inc(ins, "RV")
                v.wait_ge(K.sems["RV"], K.c["RV"])
                ins = v.reduce_max(c4, lg, axis=AXX); K.inc(ins, "RV")
                v.wait_ge(K.sems["RV"], K.c["RV"])
                ins = v.tensor_scalar(p, lg, c4, None, op0=ALU.is_ge); K.inc(ins, "RV")
                v.wait_ge(K.sems["RV"], K.c["RV"])
                ins = v.tensor_tensor(c0, c3, c4, op=ALU.add); K.inc(ins, "RV")
                v.wait_ge(K.sems["RV"], K.c["RV"])
                ins = v.reciprocal(c1, c0); K.inc(ins, "RV")
                v.wait_ge(K.sems["RV"], K.c["RV"])
                ins = v.tensor_tensor(c3, c3, c1, op=ALU.mult); K.inc(ins, "RV")
                v.wait_ge(K.sems["RV"], K.c["RV"])
                ins = v.tensor_tensor(c4, c4, c1, op=ALU.mult); K.inc(ins, "RV")
                v.wait_ge(K.sems["RV"], K.c["RV"])
                ins = v.tensor_scalar(pw, pw, c3, None, op0=ALU.mult); K.inc(ins, "RV")
                v.wait_ge(K.sems["RV"], K.c["RV"])
                ins = v.scalar_tensor_tensor(SM[:, 128 + 8 * t:136 + 8 * t],
                                             p, c4, pw, op0=ALU.mult, op1=ALU.add)
                K.inc(ins, "RV")

        @b.scalar
        def _(s):
            for t in range(TT):
                s.wait_ge(K.sems["RV"], b_rv + 16 * t + 3)
                ins = s.activation(SM[:, 264:272], SM[:, 256:264], AF.Exp,
                                   bias=SM[:, 280:281], accum_out=SM[:, 281:282])
                K.inc(ins, "RS")
    barrier(K)

    b_fcl = K.c.get("FCL", 0); K.sem("FCL")
    b_wpl = K.c.get("WPL", 0); K.sem("WPL")
    b_hp = K.c.get("HP", 0); K.sem("HP")
    b_hg = K.c.get("HG", 0); K.sem("HG")
    b_yp = K.c.get("YP", 0); K.sem("YP")
    b_ye = K.c.get("YE", 0); K.sem("YE")
    b_bf = K.c.get("BFC", 0); K.sem("BFC")
    b_bft = K.c.get("BFT", 0); K.sem("BFT")
    b_bfv = K.c.get("BFV", 0); K.sem("BFV")
    b_bpj = [K.c.get("BPJ0", 0), K.c.get("BPJ1", 0)]; K.sem("BPJ0"); K.sem("BPJ1")
    b_ye2 = K.c.get("YE2", 0); K.sem("YE2")

    JQ = 8
    equarters = [(e, q) for e in range(NE) for q in range(4)]
    NI = len(equarters)

    def bfc_cols(e):
        return SM[:, 32 + 32 * (e % 2):64 + 32 * (e % 2)]

    def bprojB(e):
        return WB[:, 5248 + 1024 * (e % 2):5248 + 1024 * (e % 2) + 1024]

    def wfc_c(i, j):
        return AB[:, (i % 2) * 8192 + j * 1024:(i % 2) * 8192 + (j + 1) * 1024]

    def wpj_c(i, j):
        return AC[:, (i % 2) * 8192 + j * 1024:(i % 2) * 8192 + (j + 1) * 1024]

    with (nc.psum_tensor("ps_a", [128, 2, 512], F32) as ps_a,
          nc.psum_tensor("ps_y", [128, nsub, 1024], F32) as ps_y,
          nc.Block("experts") as b):

        @b.gpsimd
        def _(gp):
            for i, (e, q) in enumerate(equarters):
                for j in range(JQ):
                    if i >= 1 and j == 0:
                        gp.wait_ge(K.sems["FCL"], b_fcl + 16 * JQ * i)
                    if i >= 2:
                        gp.wait_ge(K.sems["HG"], b_hg + (i - 1) * NTT * JQ)
                    hc0 = (JQ * q + j) * 128
                    ins = gp.dma_start(
                        out=wfc_c(i, j).rearrange("p (dc f) -> p dc f", dc=8),
                        in_=inp["L0_ffn_w_fc"][e, :, hc0:hc0 + 128].rearrange("(dc p) f -> p dc f", p=128))
                    K.inc(ins, "FCL", 16)
                for j in range(JQ):
                    if i >= 1 and j == 0:
                        gp.wait_ge(K.sems["WPL"], b_wpl + 16 * JQ * i)
                    if i >= 2:
                        gp.wait_ge(K.sems["YP"], b_yp + (i - 1) * NTT)
                    hc0 = (JQ * q + j) * 128
                    ins = gp.dma_start(out=wpj_c(i, j),
                                       in_=inp["L0_ffn_w_proj"][e, hc0:hc0 + 128, :])
                    K.inc(ins, "WPL", 16)
                if q == 0:
                    if e >= 1:
                        gp.wait_ge(K.sems["BFT"], b_bft + 16 * e)
                    ins = gp.dma_start(out=WB[0:32, 7296:7424],
                                       in_=inp["L0_ffn_b_fc"][e].rearrange("(a b) -> a b", a=32))
                    K.inc(ins, "BFC", 16)
                    if e >= 2:
                        gp.wait_ge(K.sems["YE2"], b_ye2 + TT * (e - 1))
                    ins = gp.dma_start(out=bprojB(e),
                                       in_=inp["L0_ffn_b_proj"][e:e + 1, :].broadcast_to([128, 1024]))
                    K.inc(ins, "BPJ" + str(e % 2), 16)

        @b.sync
        def _(sy):
            for e in range(NE):
                sy.wait_ge(K.sems["BFC"], b_bf + 16 * (e + 1))
                if e >= 2:
                    sy.wait_ge(K.sems["BFV"], b_bfv + e - 1)
                ins = sy.dma_start_transpose(SMB[:, 32 * (e % 2):32 * (e % 2) + 32],
                                             WB[0:32, 7296:7424])
                K.inc(ins, "BFT", 16)

        @b.vector
        def _(v):
            for i, (e, q) in enumerate(equarters):
                if q == 0:
                    v.wait_ge(K.sems["BFT"], b_bft + 16 * (e + 1))
                    ins = v.tensor_copy(bfc_cols(e), SMB[:, 32 * (e % 2):32 * (e % 2) + 32])
                    K.inc(ins, "BFV")
                for tt in range(NTT):
                    v.wait_ge(K.sems["YP"], b_yp + i * NTT + tt + 1)
                    for sub in range(nsub):
                        tile = tt * nsub + sub
                        ins = v.scalar_tensor_tensor(
                            AX[:, tile * 1024:(tile + 1) * 1024],
                            ps_y[:, sub, :],
                            SM[:, 128 + 8 * tile + e:129 + 8 * tile + e],
                            AX[:, tile * 1024:(tile + 1) * 1024],
                            op0=ALU.mult, op1=ALU.add)
                        K.inc(ins, "YE")
                    v.wait_ge(K.sems["YE"], K.c["YE"])
                if q == 3:
                    v.wait_ge(K.sems["BPJ" + str(e % 2)], b_bpj[e % 2] + 16 * (e // 2 + 1))
                    for tile in range(TT):
                        ins = v.scalar_tensor_tensor(
                            AX[:, tile * 1024:(tile + 1) * 1024],
                            bprojB(e),
                            SM[:, 128 + 8 * tile + e:129 + 8 * tile + e],
                            AX[:, tile * 1024:(tile + 1) * 1024],
                            op0=ALU.mult, op1=ALU.add)
                        K.inc(ins, "YE2")
                        v.wait_ge(K.sems["YE2"], K.c["YE2"])

        @b.tensor
        def _(te):
            for i, (e, q) in enumerate(equarters):
                for tt in range(NTT):
                    if tt == 0:
                        te.wait_ge(K.sems["FCL"], b_fcl + 16 * JQ * (i + 1))
                    for j in range(JQ):
                        it = tt * JQ + j
                        git = i * NTT * JQ + it
                        if git >= 2:
                            te.wait_ge(K.sems["HG"], b_hg + git - 1)
                        for dci in range(8):
                            ins = te.matmul(ps_a[:, it % 2, 0:256],
                                            wfc_c(i, j)[:, dci * 128:(dci + 1) * 128],
                                            lnT(dci)[:, tt * 256:tt * 256 + 256],
                                            start=(dci == 0), stop=(dci == 7))
                        K.inc(ins, "HP")
                    te.wait_ge(K.sems["HG"], b_hg + i * NTT * JQ + (tt + 1) * JQ)
                    if tt == 0:
                        te.wait_ge(K.sems["WPL"], b_wpl + 16 * JQ * (i + 1))
                    if i * NTT + tt >= 1:
                        te.wait_ge(K.sems["YE"], b_ye + (i * NTT + tt) * nsub)
                    for j in range(JQ):
                        for sub in range(nsub):
                            for n2 in range(2):
                                ins = te.matmul(ps_y[:, sub, n2 * 512:(n2 + 1) * 512],
                                                WB[:, j * 256 + sub * 128:j * 256 + sub * 128 + 128],
                                                wpj_c(i, j)[:, n2 * 512:(n2 + 1) * 512],
                                                start=(j == 0), stop=(j == JQ - 1))
                    K.inc(ins, "YP")

        @b.scalar
        def _(s):
            for i, (e, q) in enumerate(equarters):
                if q == 0:
                    s.wait_ge(K.sems["BFV"], b_bfv + e + 1)
                for tt in range(NTT):
                    for j in range(JQ):
                        it = tt * JQ + j
                        git = i * NTT * JQ + it
                        s.wait_ge(K.sems["HP"], b_hp + git + 1)
                        if i * NTT + tt >= 1:
                            s.wait_ge(K.sems["YP"], b_yp + i * NTT + tt)
                        ins = s.activation(WB[:, j * 256:(j + 1) * 256],
                                           ps_a[:, it % 2, 0:256], AF.Gelu_apprx_tanh,
                                           bias=bfc_cols(e)[:, JQ * q + j:JQ * q + j + 1])
                        K.inc(ins, "HG")
    barrier(K)



SF_MAX = 99

def emit_slotffn(K, st):
    nc = K.nc
    TT = st["TT"]; T = st["T"]
    AX, AXb, AA, AB, AC = st["AX"], st["AXb"], st["AA"], st["AB"], st["AC"]
    WB, S32, SM, SMB, ONEB = st["WB"], st["S32"], st["SM"], st["SMB"], st["ONEB"]
    IDN, IDF = st["IDN"], st["IDF"]
    inp, x_park, sc_park = st["inp"], st["x_park"], st["sc_park"]
    P = "L1_ffn_"

    def lnT(c):
        return AA[:, c * 2048:(c + 1) * 2048]

    alphaT = AXb[:, 0:2048]
    svp = AXb[:, 2048:3072]
    vslp = AXb[:, 3072:4096]

    def hTs(hc):
        return AXb[:, 4096 + hc * 2048:4096 + (hc + 1) * 2048]

    probsT = AXb[:, 12288:12352]
    S0T = AXb[:, 12416:12928].rearrange("p (c s) -> p c s", s=64)
    SnTb = AXb[:, 12928:13440].rearrange("p (c s) -> p c s", s=64)
    ynb = AXb[:, 13440:13952].rearrange("p (c s) -> p c s", s=64)
    siT = AXb[:, 13952:14464].rearrange("p (c s) -> p c s", s=64)

    def wtb(par):
        return AXb[:, 14464 + par * 1024:14464 + (par + 1) * 1024]

    def alpha_t(t):
        return AXb[:, 16576 + t * 128:16576 + t * 128 + 128]

    qsT = AXb[:, 8448:8576].rearrange("p (c s) -> p c s", s=64)
    ksT = AXb[:, 8576:8704].rearrange("p (c s) -> p c s", s=64)
    s0stage = AXb[0:64, 4096:5120]
    sistage = AXb[0:64, 5120:6144]
    pstage = AXb[0:64, 6144:6272]
    giT = AX[:, 9312:10848].rearrange("p (c s) -> p c s", s=64)
    ghT = AX[:, 10848:12384].rearrange("p (c s) -> p c s", s=64)
    SnT = AX[:, 12384:12896].rearrange("p (c s) -> p c s", s=64)
    yf = AX[:, 12896:13408].rearrange("p (c s) -> p c s", s=64)
    ynf = AX[:, 13408:13920].rearrange("p (c s) -> p c s", s=64)
    S0f = AX[:, 15872:16384].rearrange("p (c s) -> p c s", s=64)
    ScT = AX[:, 15360:15872].rearrange("p (c s) -> p c s", s=64)
    sif = AX[:, 14336:15360]

    def wq_c(dc):
        return AC[:, dc * 256:(dc + 1) * 256]

    def wk_c(dc):
        return AC[:, 2048 + dc * 256:2048 + (dc + 1) * 256]

    def wbig(dc):
        return AC[:, 4096 + dc * 1024:4096 + (dc + 1) * 1024]

    emit_ln_lnT(K, st, park_x=True,
                sink=lambda t: AB[:, t * 1024:(t + 1) * 1024],
                sink_persistent=True)

    b_ld = K.c.get("LF", 0); K.sem("LF")
    with nc.Block("sfprep") as b:
        @b.vector
        def _(v):
            v.memset(alphaT[64:128, :], 0.0)
            v.memset(svp[64:128, :], 0.0)
            v.memset(vslp[64:128, :], 0.0)
            v.memset(probsT[64:128, :], 0.0)
            v.memset(AXb[:, 16576:18624], 0.0)
            v.memset(AXb[0:64, 6144:6272], 0.0)
            ins = v.memset(SM[:, 1279:1280], 1.0)
            K.inc(ins, "VE")

        @b.gpsimd
        def _(gp):
            for c in range(8):
                ins = gp.dma_start(out=WB[:, c * 512:(c + 1) * 512],
                                   in_=inp[P + "w1"][c * 128:(c + 1) * 128, :])
                K.inc(ins, "LF", 16)
            for c in range(8):
                ins = gp.dma_start(out=WB[:, 4096 + c * 512:4096 + (c + 1) * 512],
                                   in_=inp[P + "w1"][1024 + c * 128:1024 + (c + 1) * 128, :])
                K.inc(ins, "LF", 16)
            for c in range(4):
                ins = gp.dma_start(out=SMB[:, c * 64:(c + 1) * 64],
                                   in_=inp[P + "w2"][c * 128:(c + 1) * 128, :])
                K.inc(ins, "LF", 16)
            ins = gp.dma_start(out=s0stage, in_=inp[P + "slot_init"])
            K.inc(ins, "LF", 16)
            ins = gp.dma_start(out=SMB[0:1, 256:768], in_=inp[P + "b1"].rearrange("(a d) -> a d", a=1))
            K.inc(ins, "LF", 16)
            ins = gp.dma_start(out=SMB[0:1, 768:832], in_=inp[P + "b2"].rearrange("(a d) -> a d", a=1))
            K.inc(ins, "LF", 16)
            ins = gp.dma_start(out=SMB[0:1, 832:1856], in_=inp[P + "bv"].rearrange("(a d) -> a d", a=1))
            K.inc(ins, "LF", 16)
            ins = gp.dma_start(out=SM[0:1, 300:301], in_=inp[P + "tau"].rearrange("(a d) -> a d", a=1))
            K.inc(ins, "LF", 16)
            for dc in range(8):
                ins = gp.dma_start(out=wq_c(dc), in_=inp[P + "wq"][dc * 128:(dc + 1) * 128, :])
                K.inc(ins, "LF", 16)
            for dc in range(8):
                ins = gp.dma_start(out=wk_c(dc), in_=inp[P + "wk"][dc * 128:(dc + 1) * 128, :])
                K.inc(ins, "LF", 16)
            for dc in range(8):
                ins = gp.dma_start(out=wbig(dc), in_=inp[P + "wv"][dc * 128:(dc + 1) * 128, :])
                K.inc(ins, "LF", 16)

        @b.sync
        def _(sy):
            sy.wait_ge(K.sem("LF"), K.c["LF"])
            for c in range(8):
                ins = sy.dma_start_transpose(S0T[:, c, :], s0stage[:, c * 128:(c + 1) * 128])
                K.inc(ins, "LFS", 16)
            ins = sy.dma_start(out=sc_park[0:1, 16:17], in_=SM[0:1, 300:301])
            K.inc(ins, "LFS", 16)
    barrier(K)

    bc_plan = [("bih", 302, 24), ("bhh", 326, 24), ("ln_g", 350, 8),
               ("ln_b", 358, 8), ("bq", 366, 2), ("bk", 368, 2)]
    bstage = AXb[0:32, 6400:6528]
    btr = AXb[:, 6528:6560]
    K.sem("BCL"); K.sem("BCT"); K.sem("BCV"); K.sem("BCM")
    b_bcl = K.c["BCL"]; b_bct = K.c["BCT"]; b_bcv = K.c["BCV"]
    with nc.Block("sfbcols") as b:
        @b.vector
        def _(v):
            ins = v.memset(AXb[0:32, 6400:6528], 0.0)
            K.inc(ins, "BCM")
            for i, (nm, col0, ncol) in enumerate(bc_plan):
                v.wait_ge(K.sems["BCT"], b_bct + 16 * (i + 1))
                ins = v.tensor_copy(SM[:, col0:col0 + ncol], btr[:, 0:ncol])
                K.inc(ins, "BCV")

        @b.gpsimd
        def _(gp):
            gp.wait_ge(K.sems["BCM"], K.c["BCM"])
            for i, (nm, col0, ncol) in enumerate(bc_plan):
                if i >= 1:
                    gp.wait_ge(K.sems["BCT"], b_bct + 16 * i)
                ins = gp.dma_start(out=bstage[0:ncol * 128 // 128, 0:128] if False else bstage[0:ncol, 0:128],
                                   in_=inp[P + nm].rearrange("(c p) -> c p", p=128))
                K.inc(ins, "BCL", 16)

        @b.sync
        def _(sy):
            for i, (nm, col0, ncol) in enumerate(bc_plan):
                sy.wait_ge(K.sems["BCL"], b_bcl + 16 * (i + 1))
                if i >= 1:
                    sy.wait_ge(K.sems["BCV"], b_bcv + i)
                ins = sy.dma_start_transpose(btr, bstage)
                K.inc(ins, "BCT", 16)
    barrier(K)

    if SF_MAX < 3:
        return
    with nc.Block("sftau") as b:
        @b.scalar
        def _(s):
            ins = s.activation(SM[0:1, 300:301], SM[0:1, 300:301], AF.Abs)
            K.inc(ins, "SC")

        @b.vector
        def _(v):
            v.wait_ge(K.sem("SC"), K.c["SC"])
            ins = v.tensor_scalar(SM[0:1, 300:301], SM[0:1, 300:301], 0.1, None, op0=ALU.add)
            K.inc(ins, "VE")
            v.wait_ge(K.sem("VE"), K.c["VE"])
            ins = v.reciprocal(SM[0:1, 300:301], SM[0:1, 300:301])
            K.inc(ins, "VE")

        @b.sync
        def _(sy):
            sy.wait_ge(K.sem("VE"), K.c["VE"])
            ins = sy.dma_start(out=sc_park[0:1, 17:18], in_=SM[0:1, 300:301])
            K.inc(ins, "LFS", 16)
            sy.wait_ge(K.sem("LFS"), K.c["LFS"])
            ins = sy.dma_start(out=SM[:, 301:302],
                               in_=sc_park[0:1, 17:18].broadcast_to([128, 1]))
            K.inc(ins, "LFS", 16)
    barrier(K)

    with nc.Block("s0widen") as b:
        @b.vector
        def _(v):
            ins = v.tensor_copy(S0f, S0T)
            K.inc(ins, "VE")
    barrier(K)

    with (nc.psum_tensor("ps_sm", [1, 1024], F32) as ps_sm,
          nc.psum_tensor("ps_cr", [1, 512], F32) as ps_cr,
          nc.Block("sfsmean") as b):
        @b.tensor
        def _(te):
            for n2 in range(2):
                ins = te.matmul(ps_sm[:, n2 * 512:(n2 + 1) * 512], ONEB[0:64, 0:1],
                                s0stage[:, n2 * 512:(n2 + 1) * 512], start=True, stop=True)
            K.inc(ins, "TE")

        @b.scalar
        def _(s):
            s.wait_ge(K.sem("TE"), K.c["TE"])
            ins = s.activation(S32[0:1, 0:1024], ps_sm[:], AF.Identity, scale=1.0 / NS)
            K.inc(ins, "SC")

        @b.sync
        def _(sy):
            pass
    barrier(K)
    with nc.Block("sfsmean2") as b:
        @b.sync
        def _(sy):
            ins = sy.dma_start(out=sc_park[4:20, 0:64].rearrange("r c -> (r c)").unsqueeze(0),
                               in_=S32[0:1, 0:1024])
            K.inc(ins, "LFS", 16)

        @b.vector
        def _(v):
            ins = v.memset(AXb[0:16, 6208:6336], 0.0)
            K.inc(ins, "VE")

        @b.gpsimd
        def _(gp):
            gp.wait_ge(K.sem("LFS"), K.c["LFS"])
            gp.wait_ge(K.sem("VE"), K.c["VE"])
            ins = gp.dma_start(out=AXb[0:8, 6208:6336],
                               in_=sc_park[4:20, 0:64].rearrange("(a r) c -> a (r c)", a=8))
            K.inc(ins, "LF", 16)

        @b.sync
        def _(sy2):
            pass
    barrier(K)
    with nc.Block("sfsmean3") as b:
        @b.sync
        def _(sy):
            ins = sy.dma_start_transpose(AXb[:, 6336:6352], AXb[0:16, 6208:6336])
            K.inc(ins, "LFS", 16)
    barrier(K)

    with (nc.psum_tensor("ps_cr2", [1, 512], F32) as ps_cr,
          nc.Block("sfcrow") as b):
        @b.tensor
        def _(te):
            smT = AXb[:, 6336:6352]
            for dc in range(8):
                te.matmul(ps_cr[:], smT[:, dc:dc + 1],
                          WB[:, 4096 + dc * 512:4096 + (dc + 1) * 512],
                          start=(dc == 0), stop=False)
            ins = te.matmul(ps_cr[:], ONEB[0:1, 0:1], SMB[0:1, 256:768],
                            start=False, stop=True)
            K.inc(ins, "TE")

        @b.scalar
        def _(s):
            s.wait_ge(K.sem("TE"), K.c["TE"])
            ins = s.copy(S32[0:1, 1024:1536], ps_cr[:])
            K.inc(ins, "SC")

        @b.sync
        def _(sy):
            sy.wait_ge(K.sem("SC"), K.c["SC"])
            ins = sy.dma_start(out=sc_park[20:28, 0:64].rearrange("r c -> (r c)").unsqueeze(0),
                               in_=S32[0:1, 1024:1536])
            K.inc(ins, "LFS", 16)

        @b.gpsimd
        def _(gp):
            gp.wait_ge(K.sem("LFS"), K.c["LFS"])
            gp.wait_ge(K.sem("BCM"), K.c["BCM"])
            ins = gp.dma_start(out=bstage[0:4, 0:128],
                               in_=sc_park[20:28, 0:64].rearrange("(a r) c -> a (r c)", a=4))
            K.inc(ins, "LF", 16)
    barrier(K)
    with nc.Block("sfcrow2") as b:
        @b.sync
        def _(sy):
            ins = sy.dma_start_transpose(btr, bstage)
            K.inc(ins, "LFS", 16)

        @b.vector
        def _(v):
            v.wait_ge(K.sem("LFS"), K.c["LFS"])
            ins = v.tensor_copy(SM[:, 370:374], btr[:, 0:4])
            K.inc(ins, "VE")
    barrier(K)

    if SF_MAX < 6:
        return
    b_hp = K.c.get("HP2", 0); K.sem("HP2")
    b_hg = K.c.get("HG2", 0); K.sem("HG2")
    NQ = max(1, (T) // 512)
    qw = 512 if T >= 512 else T
    with (nc.psum_tensor("ps_h2", [128, 2, 512], F32) as ps_h,
          nc.Block("sfh") as b):
        @b.tensor
        def _(te):
            i = 0
            for hc in range(4):
                for n in range(NQ):
                    if i >= 2:
                        te.wait_ge(K.sems["HG2"], b_hg + i - 1)
                    for c in range(8):
                        ins = te.matmul(ps_h[:, i % 2, 0:qw],
                                        WB[:, c * 512 + hc * 128:c * 512 + (hc + 1) * 128],
                                        lnT(c)[:, n * qw:(n + 1) * qw],
                                        start=(c == 0), stop=(c == 7))
                    K.inc(ins, "HP2")
                    i += 1

        @b.scalar
        def _(s):
            i = 0
            for hc in range(4):
                for n in range(NQ):
                    s.wait_ge(K.sems["HP2"], b_hp + i + 1)
                    ins = s.activation(hTs(hc)[:, n * qw:(n + 1) * qw],
                                       ps_h[:, i % 2, 0:qw], AF.Gelu,
                                       bias=SM[:, 370 + hc:371 + hc])
                    K.inc(ins, "HG2")
                    i += 1
    barrier(K)

    if SF_MAX < 7:
        return
    b_lp = K.c.get("LP", 0); K.sem("LP")
    b_lv = K.c.get("LV", 0); K.sem("LV")
    b_ls = K.c.get("LS", 0); K.sem("LS")
    b_lt = K.c.get("LT", 0); K.sem("LT")
    lg = S32[:, 2048:2112]; ew = S32[:, 2112:2176]; wk_ = S32[:, 2176:2240]
    m1c = SM[:, 374:375]; m8c = SM[:, 375:376]; tmpc = SM[:, 376:377]; sumc = SM[:, 377:378]
    with (nc.psum_tensor("ps_lg", [128, 2, 512], F32) as ps_lg,
          nc.Block("sflg") as b):
        @b.tensor
        def _(te):
            for t in range(TT):
                if t >= 2:
                    te.wait_ge(K.sems["LV"], b_lv + 31 * (t - 2) + 1)
                for hc in range(4):
                    te.matmul(ps_lg[:, t % 2, 0:64],
                              hTs(hc)[:, t * 128:(t + 1) * 128],
                              SMB[:, hc * 64:(hc + 1) * 64],
                              start=(hc == 0), stop=False)
                ins = te.matmul(ps_lg[:, t % 2, 0:64], ONEB[0:1, 0:128],
                                SMB[0:1, 768:832], start=False, stop=True)
                K.inc(ins, "LP")

        @b.vector
        def _(v):
            for t in range(TT):
                v.wait_ge(K.sems["LP"], b_lp + t + 1)
                if t > 0:
                    v.wait_ge(K.sems["LV"], K.c["LV"])
                ins = v.tensor_scalar(lg, ps_lg[:, t % 2, 0:64], SM[:, 301:302], None, op0=ALU.mult)
                K.inc(ins, "LV")
                v.wait_ge(K.sems["LV"], K.c["LV"])
                ins = v.tensor_copy(wk_, lg); K.inc(ins, "LV")
                for it in range(8):
                    v.wait_ge(K.sems["LV"], K.c["LV"])
                    ins = v.reduce_max(tmpc if it > 0 else m1c, wk_, axis=AXX)
                    K.inc(ins, "LV")
                    if it < 7:
                        v.wait_ge(K.sems["LV"], K.c["LV"])
                        mm = m1c if it == 0 else tmpc
                        ins = v.tensor_scalar(ew, wk_, mm, None, op0=ALU.is_ge)
                        K.inc(ins, "LV")
                        v.wait_ge(K.sems["LV"], K.c["LV"])
                        ins = v.scalar_tensor_tensor(wk_, ew, -1e30, wk_,
                                                     op0=ALU.mult, op1=ALU.add)
                        K.inc(ins, "LV")
                    else:
                        v.wait_ge(K.sems["LV"], K.c["LV"])
                        ins = v.tensor_copy(m8c, tmpc); K.inc(ins, "LV")
                v.wait_ge(K.sems["LV"], K.c["LV"])
                ins = v.tensor_scalar(ew, lg, m8c, None, op0=ALU.is_ge); K.inc(ins, "LV")
                v.wait_ge(K.sems["LV"], K.c["LV"])
                ins = v.tensor_scalar(tmpc, m1c, -1.0, None, op0=ALU.mult); K.inc(ins, "LV")
                v.wait_ge(K.sems["LS"], b_ls + t + 1)
                ins = v.tensor_tensor(ew, ew, wk_, op=ALU.mult); K.inc(ins, "LV")
                v.wait_ge(K.sems["LV"], K.c["LV"])
                ins = v.reduce_sum(sumc, ew, axis=AXX); K.inc(ins, "LV")
                v.wait_ge(K.sems["LV"], K.c["LV"])
                ins = v.reciprocal(sumc, sumc); K.inc(ins, "LV")
                v.wait_ge(K.sems["LV"], K.c["LV"])
                ins = v.tensor_scalar(alpha_t(t)[:, 0:64], ew, sumc, None, op0=ALU.mult)
                K.inc(ins, "LV")

        @b.scalar
        def _(s):
            for t in range(TT):
                s.wait_ge(K.sems["LV"], b_lv + 31 * t + 27)
                ins = s.activation(wk_, lg, AF.Exp, bias=tmpc)
                K.inc(ins, "LS")

        @b.sync
        def _(sy):
            for t in range(TT):
                sy.wait_ge(K.sems["LV"], b_lv + 31 * (t + 1))
                ins = sy.dma_start_transpose(alphaT[:, t * 128:(t + 1) * 128], alpha_t(t))
                K.inc(ins, "LT", 16)
    barrier(K)

    if SF_MAX < 8:
        return
    with (nc.psum_tensor("ps_si", [64, 1024], F32) as ps_si,
          nc.psum_tensor("ps_dn", [64, 512], F32) as ps_dn,
          nc.Block("sfsi") as b):
        @b.tensor
        def _(te):
            for t in range(TT):
                for n2 in range(2):
                    ins = te.matmul(ps_si[:, n2 * 512:(n2 + 1) * 512],
                                    alpha_t(t)[:, 0:64],
                                    AB[:, t * 1024 + n2 * 512:t * 1024 + (n2 + 1) * 512],
                                    start=(t == 0), stop=(t == TT - 1))
                ins2 = te.matmul(ps_dn[:, 0:1], alpha_t(t)[:, 0:64], ONEB[:, 0:1],
                                 start=(t == 0), stop=(t == TT - 1))
            K.inc(ins2, "TE")

        @b.vector
        def _(v):
            v.wait_ge(K.sem("TE"), K.c["TE"])
            ins = v.tensor_scalar(SM[0:64, 378:379], ps_dn[:, 0:1], 1e-8, None, op0=ALU.add)
            K.inc(ins, "VE")
            v.wait_ge(K.sem("VE"), K.c["VE"])
            ins = v.reciprocal(SM[0:64, 378:379], SM[0:64, 378:379])
            K.inc(ins, "VE")
            v.wait_ge(K.sem("VE"), K.c["VE"])
            ins = v.tensor_scalar(sif[0:64, :], ps_si[:], SM[0:64, 378:379], None, op0=ALU.mult)
            K.inc(ins, "VE")
            v.wait_ge(K.sem("VE"), K.c["VE"])
            ins = v.tensor_copy(sistage, sif[0:64, :])
            K.inc(ins, "VE")

        @b.sync
        def _(sy):
            sy.wait_ge(K.sem("VE"), K.c["VE"])
            for c in range(8):
                ins = sy.dma_start_transpose(siT[:, c, :], sistage[:, c * 128:(c + 1) * 128])
                K.inc(ins, "LFS", 16)
    barrier(K)

    if SF_MAX < 10:
        return
    def wtb2(par):
        return AXb[:, 6400 + par * 1024:6400 + (par + 1) * 1024]

    for wname, rhsv, dstv, bcol0 in (("Wih", siT, giT, 302), ("Whh", S0T, ghT, 326)):
        b_gl = [K.c.get("GL0", 0), K.c.get("GL1", 0)]; K.sem("GL0"); K.sem("GL1")
        b_gt = K.c.get("GT", 0); K.sem("GT")
        b_ge = K.c.get("GE", 0); K.sem("GE")
        b_gp2 = K.c.get("GP2", 0); K.sem("GP2")
        b_gs = K.c.get("GS", 0); K.sem("GS")
        with (nc.psum_tensor("ps_gt" + wname, [128, 2, 8, 128], BF16) as ps_gt,
              nc.psum_tensor("ps_gg" + wname, [128, 2, 512], F32) as ps_gg,
              nc.Block("gru" + wname) as b):
            @b.gpsimd
            def _(gp):
                for gc in range(24):
                    if gc >= 2:
                        gp.wait_ge(K.sems["GT"], b_gt + gc - 1)
                    ins = gp.dma_start(out=wtb(gc % 2),
                                       in_=inp[P + wname][gc * 128:(gc + 1) * 128, :])
                    K.inc(ins, "GL" + str(gc % 2), 16)

            @b.tensor
            def _(te):
                for gc in range(24):
                    te.wait_ge(K.sems["GL" + str(gc % 2)], b_gl[gc % 2] + 16 * (gc // 2 + 1))
                    if gc >= 2:
                        te.wait_ge(K.sems["GE"], b_ge + gc - 1)
                    for dc in range(8):
                        ins = te.transpose(ps_gt[:, gc % 2, dc, :],
                                           wtb(gc % 2)[:, dc * 128:(dc + 1) * 128], IDN[:])
                    K.inc(ins, "GT")
                    if gc >= 1:
                        gp_ = gc - 1
                        te.wait_ge(K.sems["GE"], b_ge + gp_ + 1)
                        if gp_ >= 2:
                            te.wait_ge(K.sems["GS"], b_gs + gp_ - 1)
                        for dc in range(8):
                            ins = te.matmul(ps_gg[:, gp_ % 2, 0:64],
                                            wtb2(gp_ % 2)[:, dc * 128:(dc + 1) * 128],
                                            rhsv[:, dc, :],
                                            start=(dc == 0), stop=(dc == 7))
                        K.inc(ins, "GP2")
                gp_ = 23
                te.wait_ge(K.sems["GE"], b_ge + gp_ + 1)
                te.wait_ge(K.sems["GS"], b_gs + gp_ - 1)
                for dc in range(8):
                    ins = te.matmul(ps_gg[:, gp_ % 2, 0:64],
                                    wtb2(gp_ % 2)[:, dc * 128:(dc + 1) * 128],
                                    rhsv[:, dc, :],
                                    start=(dc == 0), stop=(dc == 7))
                K.inc(ins, "GP2")

            @b.vector
            def _(v):
                for gc in range(24):
                    v.wait_ge(K.sems["GT"], b_gt + gc + 1)
                    if gc >= 2:
                        v.wait_ge(K.sems["GP2"], b_gp2 + gc - 1)
                    ins = v.tensor_copy(wtb2(gc % 2).rearrange("p (c n) -> p c n", c=8),
                                        ps_gt[:, gc % 2, 0:8, :])
                    K.inc(ins, "GE")

            @b.scalar
            def _(s):
                for gc in range(24):
                    s.wait_ge(K.sems["GP2"], b_gp2 + gc + 1)
                    ins = s.activation(dstv[:, gc, :], ps_gg[:, gc % 2, 0:64], AF.Identity,
                                       bias=SM[:, bcol0 + gc:bcol0 + gc + 1])
                    K.inc(ins, "GS")
        barrier(K)

    if SF_MAX < 11:
        return
    rr = S32[:, 2048:2112]; zz = S32[:, 2112:2176]; nn = S32[:, 2176:2240]
    tp1 = S32[:, 2240:2304]
    b_gv = K.c.get("GV", 0); K.sem("GV")
    b_gsc = K.c.get("GSC", 0); K.sem("GSC")
    with nc.Block("sfgate") as b:
        @b.vector
        def _(v):
            for dc in range(8):
                if dc > 0:
                    v.wait_ge(K.sems["GSC"], b_gsc + 3 * dc)
                v.wait_ge(K.sems["GV"], K.c["GV"])
                ins = v.tensor_tensor(rr, giT[:, dc, :], ghT[:, dc, :], op=ALU.add)
                K.inc(ins, "GV")
                ins = v.tensor_tensor(zz, giT[:, dc + 8, :], ghT[:, dc + 8, :], op=ALU.add)
                K.inc(ins, "GV")
                v.wait_ge(K.sems["GSC"], b_gsc + 3 * dc + 2)
                ins = v.tensor_tensor(tp1, rr, ghT[:, dc + 16, :], op=ALU.mult)
                K.inc(ins, "GV")
                v.wait_ge(K.sems["GV"], K.c["GV"])
                ins = v.tensor_tensor(nn, giT[:, dc + 16, :], tp1, op=ALU.add)
                K.inc(ins, "GV")
                v.wait_ge(K.sems["GSC"], b_gsc + 3 * dc + 3)
                ins = v.tensor_tensor(tp1, S0f[:, dc, :], nn, op=ALU.subtract)
                K.inc(ins, "GV")
                v.wait_ge(K.sems["GV"], K.c["GV"])
                ins = v.tensor_tensor(tp1, zz, tp1, op=ALU.mult)
                K.inc(ins, "GV")
                v.wait_ge(K.sems["GV"], K.c["GV"])
                ins = v.tensor_tensor(SnT[:, dc, :], nn, tp1, op=ALU.add)
                K.inc(ins, "GV")
                v.wait_ge(K.sems["GV"], K.c["GV"])
                ins = v.tensor_copy(SnTb[:, dc, :], SnT[:, dc, :])
                K.inc(ins, "GV")

        @b.scalar
        def _(s):
            for dc in range(8):
                s.wait_ge(K.sems["GV"], b_gv + 8 * dc + 2)
                ins = s.activation(rr, rr, AF.Sigmoid)
                K.inc(ins, "GSC")
                ins = s.activation(zz, zz, AF.Sigmoid)
                K.inc(ins, "GSC")
                s.wait_ge(K.sems["GV"], b_gv + 8 * dc + 4)
                ins = s.activation(nn, nn, AF.Tanh)
                K.inc(ins, "GSC")
    barrier(K)

    if SF_MAX < 12:
        return
    with (nc.psum_tensor("ps_qs", [128, 4, 512], F32) as ps_qs,
          nc.Block("sfqk") as b):
        @b.tensor
        def _(te):
            for i, (wc, bcol) in enumerate(((wq_c, 366), (wk_c, 368))):
                for qc in range(2):
                    for dc in range(8):
                        ins = te.matmul(ps_qs[:, i * 2 + qc, 0:64],
                                        wc(dc)[:, qc * 128:(qc + 1) * 128],
                                        SnTb[:, dc, :],
                                        start=(dc == 0), stop=(dc == 7))
                    K.inc(ins, "TE")

        @b.scalar
        def _(s):
            for i in range(2):
                for qc in range(2):
                    s.wait_ge(K.sem("TE"), K.c["TE"] - 3 + i * 2 + qc)
                    dv = qsT if i == 0 else ksT
                    bcol = 366 if i == 0 else 368
                    ins = s.activation(dv[:, qc, :], ps_qs[:, i * 2 + qc, 0:64], AF.Identity,
                                       bias=SM[:, bcol + qc:bcol + qc + 1])
                    K.inc(ins, "SC")
    barrier(K)

    if SF_MAX < 13:
        return
    with (nc.psum_tensor("ps_sa", [64, 64], F32) as ps_sa,
          nc.Block("sfsa") as b):
        @b.tensor
        def _(te):
            for qc in range(2):
                ins = te.matmul(ps_sa[:], qsT[:, qc, :], ksT[:, qc, :],
                                start=(qc == 0), stop=(qc == 1))
            K.inc(ins, "TE")

        @b.vector
        def _(v):
            v.wait_ge(K.sem("TE"), K.c["TE"])
            ins = v.reduce_max(SM[0:64, 379:380], ps_sa[:], axis=AXX)
            K.inc(ins, "VE")
            v.wait_ge(K.sem("VE"), K.c["VE"])
            ins = v.tensor_scalar(SM[0:64, 379:380], SM[0:64, 379:380], -0.0625, None, op0=ALU.mult)
            K.inc(ins, "VE")

        @b.scalar
        def _(s):
            s.wait_ge(K.sem("VE"), K.c["VE"])
            ins = s.activation(S32[0:64, 2048:2112], ps_sa[:], AF.Exp,
                               bias=SM[0:64, 379:380], scale=0.0625,
                               accum_out=SM[0:64, 380:381])
            K.inc(ins, "SC")

        @b.vector
        def _(v):
            v.wait_ge(K.sem("SC"), K.c["SC"])
            ins = v.reciprocal(SM[0:64, 380:381], SM[0:64, 380:381])
            K.inc(ins, "VE")
            v.wait_ge(K.sem("VE"), K.c["VE"])
            ins = v.tensor_scalar(pstage[:, 0:64], S32[0:64, 2048:2112], SM[0:64, 380:381], None, op0=ALU.mult)
            K.inc(ins, "VE")

        @b.sync
        def _(sy):
            sy.wait_ge(K.sem("VE"), K.c["VE"])
            ins = sy.dma_start_transpose(probsT[:, 0:64].rearrange("p c -> p c"), pstage)
            K.inc(ins, "LFS", 16)
    barrier(K)

    if SF_MAX < 14:
        return
    with (nc.psum_tensor("ps_vs", [64, 1024], F32) as ps_vs,
          nc.Block("sfvs") as b):
        @b.tensor
        def _(te):
            for dc in range(8):
                for n2 in range(2):
                    te.matmul(ps_vs[:, n2 * 512:(n2 + 1) * 512],
                              SnTb[:, dc, :],
                              wbig(dc)[:, n2 * 512:(n2 + 1) * 512],
                              start=(dc == 0), stop=False)
            for n2 in range(2):
                ins = te.matmul(ps_vs[:, n2 * 512:(n2 + 1) * 512],
                                ONEB[0:1, 0:64],
                                SMB[0:1, 832 + n2 * 512:832 + (n2 + 1) * 512],
                                start=False, stop=True)
            K.inc(ins, "TE")

        @b.scalar
        def _(s):
            s.wait_ge(K.sem("TE"), K.c["TE"])
            ins = s.copy(vslp[0:64, :], ps_vs[:])
            K.inc(ins, "SC")

        @b.gpsimd
        def _(gp):
            gp.wait_ge(K.sem("SC"), K.c["SC"])
            for dc in range(8):
                ins = gp.dma_start(out=wbig(dc), in_=inp[P + "wvp"][dc * 128:(dc + 1) * 128, :])
                K.inc(ins, "LF", 16)
    barrier(K)

    if SF_MAX < 15:
        return
    with (nc.psum_tensor("ps_sc2", [128, 8, 64], F32) as ps_sc2,
          nc.psum_tensor("ps_st", [1, 128], F32) as ps_st,
          nc.Block("sfcomm") as b):
        @b.tensor
        def _(te):
            for dc in range(8):
                ins = te.matmul(ps_sc2[:, dc, :],
                                vslp[:, dc * 128:(dc + 1) * 128],
                                probsT,
                                start=True, stop=True)
            K.inc(ins, "TE")

        @b.vector
        def _(v):
            v.wait_ge(K.sem("TE"), K.c["TE"])
            for dc in range(8):
                v.wait_ge(K.sem("VE"), K.c["VE"])
                ins = v.tensor_tensor(yf[:, dc, :], SnT[:, dc, :], ps_sc2[:, dc, :], op=ALU.add)
                K.inc(ins, "VE")

        @b.scalar
        def _(s):
            s.wait_ge(K.sem("VE"), K.c["VE"])
            ins = s.activation(ScT[:, 0:8, :].rearrange("p c s -> p (c s)"),
                               yf[:, 0:8, :].rearrange("p c s -> p (c s)"), AF.Square)
            K.inc(ins, "SC")

        @b.tensor
        def _(te):
            te.wait_ge(K.sem("SC"), K.c["SC"])
            for dc in range(8):
                te.matmul(ps_st[:, 0:64], SM[:, 1279:1280], yf[:, dc, :],
                          start=(dc == 0), stop=(dc == 7))
            for dc in range(8):
                ins = te.matmul(ps_st[:, 64:128], SM[:, 1279:1280], ScT[:, dc, :],
                                start=(dc == 0), stop=(dc == 7))
            K.inc(ins, "TE")

        @b.vector
        def _(v):
            v.wait_ge(K.sem("TE"), K.c["TE"])
            ins = v.tensor_scalar(SM[0:1, 384:448], ps_st[:, 0:64], 1.0 / D, None, op0=ALU.mult)
            K.inc(ins, "VE")
            v.wait_ge(K.sem("VE"), K.c["VE"])
            ins = v.tensor_scalar(SM[0:1, 448:512], ps_st[:, 64:128], 1.0 / D, None, op0=ALU.mult)
            K.inc(ins, "VE")
            v.wait_ge(K.sem("VE"), K.c["VE"])
            ins = v.tensor_tensor(tp1[0:1, 0:64], SM[0:1, 384:448], SM[0:1, 384:448], op=ALU.mult)
            K.inc(ins, "VE")
            v.wait_ge(K.sem("VE"), K.c["VE"])
            ins = v.tensor_tensor(SM[0:1, 448:512], SM[0:1, 448:512], tp1[0:1, 0:64], op=ALU.subtract)
            K.inc(ins, "VE")
            v.wait_ge(K.sem("VE"), K.c["VE"])
            ins = v.tensor_scalar(SM[0:1, 448:512], SM[0:1, 448:512], EPS, None, op0=ALU.add)
            K.inc(ins, "VE")

        @b.scalar
        def _(s):
            s.wait_ge(K.sem("VE"), K.c["VE"])
            ins = s.activation(SM[0:1, 448:512], SM[0:1, 448:512], AF.Sqrt)
            K.inc(ins, "SC")

        @b.vector
        def _(v):
            v.wait_ge(K.sem("SC"), K.c["SC"])
            ins = v.reciprocal(SM[0:1, 448:512], SM[0:1, 448:512])
            K.inc(ins, "VE")

        @b.sync
        def _(sy):
            sy.wait_ge(K.sem("VE"), K.c["VE"])
            ins = sy.dma_start(out=sc_park[28:29, 0:64], in_=SM[0:1, 384:448])
            K.inc(ins, "LFS", 16)
            ins = sy.dma_start(out=sc_park[29:30, 0:64], in_=SM[0:1, 448:512])
            K.inc(ins, "LFS", 16)
            sy.wait_ge(K.sem("LFS"), K.c["LFS"])
            ins = sy.dma_start(out=S32[:, 2304:2368],
                               in_=sc_park[28:29, 0:64].broadcast_to([128, 64]))
            K.inc(ins, "LFS", 16)
            ins = sy.dma_start(out=S32[:, 2368:2432],
                               in_=sc_park[29:30, 0:64].broadcast_to([128, 64]))
            K.inc(ins, "LFS", 16)
    barrier(K)

    with nc.Block("sfln2") as b:
        @b.vector
        def _(v):
            for dc in range(8):
                v.wait_ge(K.sem("VE"), K.c["VE"])
                ins = v.tensor_tensor(ynf[:, dc, :], yf[:, dc, :],
                                      S32[:, 2304:2368], op=ALU.subtract)
                K.inc(ins, "VE")
                v.wait_ge(K.sem("VE"), K.c["VE"])
                ins = v.tensor_tensor(ynf[:, dc, :], ynf[:, dc, :], S32[:, 2368:2432], op=ALU.mult)
                K.inc(ins, "VE")
                v.wait_ge(K.sem("VE"), K.c["VE"])
                ins = v.tensor_scalar(ynf[:, dc, :], ynf[:, dc, :],
                                      SM[:, 350 + dc:351 + dc], SM[:, 358 + dc:359 + dc],
                                      op0=ALU.mult, op1=ALU.add)
                K.inc(ins, "VE")
                v.wait_ge(K.sem("VE"), K.c["VE"])
                ins = v.tensor_copy(ynb[:, dc, :], ynf[:, dc, :])
                K.inc(ins, "VE")
    barrier(K)

    if SF_MAX < 17:
        return
    with nc.Block("sfbvp") as b:
        @b.gpsimd
        def _(gp):
            ins = gp.dma_start(out=SMB[0:1, 256:1280],
                               in_=inp[P + "bvp"].rearrange("(a d) -> a d", a=1))
            K.inc(ins, "LF", 16)
    barrier(K)
    with (nc.psum_tensor("ps_sv", [64, 1024], F32) as ps_sv,
          nc.Block("sfsv") as b):
        @b.tensor
        def _(te):
            for dc in range(8):
                for n2 in range(2):
                    te.matmul(ps_sv[:, n2 * 512:(n2 + 1) * 512],
                              ynb[:, dc, :],
                              wbig(dc)[:, n2 * 512:(n2 + 1) * 512],
                              start=(dc == 0), stop=False)
            for n2 in range(2):
                ins = te.matmul(ps_sv[:, n2 * 512:(n2 + 1) * 512],
                                ONEB[0:1, 0:64],
                                SMB[0:1, 256 + n2 * 512:256 + (n2 + 1) * 512],
                                start=False, stop=True)
            K.inc(ins, "TE")

        @b.scalar
        def _(s):
            s.wait_ge(K.sem("TE"), K.c["TE"])
            ins = s.copy(svp[0:64, :], ps_sv[:])
            K.inc(ins, "SC")

        @b.gpsimd
        def _(gp):
            gp.wait_ge(K.sem("SC"), K.c["SC"])
            for dc in range(8):
                ins = gp.dma_start(out=wbig(dc), in_=inp[P + "wo"][dc * 128:(dc + 1) * 128, :])
                K.inc(ins, "LF", 16)
    barrier(K)

    if SF_MAX < 18:
        return
    b_op = K.c.get("OP", 0); K.sem("OP")
    b_oe = K.c.get("OE", 0); K.sem("OE")
    NQ2 = max(1, T // 512)
    qw2 = 512 if T >= 512 else T
    with (nc.psum_tensor("ps_ot", [128, 2, 512], F32) as ps_ot,
          nc.Block("sfot") as b):
        @b.tensor
        def _(te):
            i = 0
            for dc in range(8):
                for n in range(NQ2):
                    if i >= 2:
                        te.wait_ge(K.sems["OE"], b_oe + i - 1)
                    ins = te.matmul(ps_ot[:, i % 2, 0:qw2],
                                    svp[:, dc * 128:(dc + 1) * 128],
                                    alphaT[:, n * qw2:(n + 1) * qw2],
                                    start=True, stop=True)
                    K.inc(ins, "OP")
                    i += 1

        @b.scalar
        def _(s):
            i = 0
            for dc in range(8):
                for n in range(NQ2):
                    s.wait_ge(K.sems["OP"], b_op + i + 1)
                    ins = s.copy(AB[:, dc * 2048 + n * qw2:dc * 2048 + (n + 1) * qw2],
                                 ps_ot[:, i % 2, 0:qw2])
                    K.inc(ins, "OE")
                    i += 1
    barrier(K)

    if SF_MAX < 19:
        return
    with nc.Block("sfbo") as b:
        @b.gpsimd
        def _(gp):
            ins = gp.dma_start(out=SMB[0:1, 832:1856],
                               in_=inp[P + "bo"].rearrange("(a d) -> a d", a=1))
            K.inc(ins, "LF", 16)
    barrier(K)
    b_po = K.c.get("PO", 0); K.sem("PO")
    b_res = K.c.get("RES", 0); K.sem("RES")
    b_xlb = [K.c.get("XLB0", 0), K.c.get("XLB1", 0)]; K.sem("XLB0"); K.sem("XLB1")
    with (nc.psum_tensor("ps_fo", [128, 2, 1024], F32) as ps_fo,
          nc.Block("sffin") as b):
        @b.sync
        def _(sy):
            for t in range(TT):
                if t >= 2:
                    sy.wait_ge(K.sems["RES"], b_res + t - 1)
                ins = sy.dma_start(out=S32[:, (t % 2) * 1024:((t % 2) + 1) * 1024],
                                   in_=x_park[t * 128:(t + 1) * 128, :])
                K.inc(ins, "XLB" + str(t % 2), 16)

        @b.tensor
        def _(te):
            for t in range(TT):
                if t >= 2:
                    te.wait_ge(K.sems["RES"], b_res + t - 1)
                for dc in range(8):
                    for n2 in range(2):
                        te.matmul(ps_fo[:, t % 2, n2 * 512:(n2 + 1) * 512],
                                  AB[:, dc * 2048 + t * 128:dc * 2048 + (t + 1) * 128],
                                  wbig(dc)[:, n2 * 512:(n2 + 1) * 512],
                                  start=(dc == 0), stop=False)
                for n2 in range(2):
                    ins = te.matmul(ps_fo[:, t % 2, n2 * 512:(n2 + 1) * 512],
                                    ONEB[0:1, 0:128],
                                    SMB[0:1, 832 + n2 * 512:832 + (n2 + 1) * 512],
                                    start=False, stop=True)
                K.inc(ins, "PO")

        @b.vector
        def _(v):
            for t in range(TT):
                v.wait_ge(K.sems["PO"], b_po + t + 1)
                v.wait_ge(K.sems["XLB" + str(t % 2)], b_xlb[t % 2] + 16 * (t // 2 + 1))
                ins = v.tensor_tensor(AX[:, t * 1024:(t + 1) * 1024],
                                      ps_fo[:, t % 2, :],
                                      S32[:, (t % 2) * 1024:((t % 2) + 1) * 1024],
                                      op=ALU.add)
                K.inc(ins, "RES")
    barrier(K)



_WNAMES = [
    "L0_attn_Wq", "L0_attn_bq", "L0_attn_Wk", "L0_attn_bk", "L0_attn_Wv",
    "L0_attn_bv", "L0_attn_Wo", "L0_attn_bo", "L0_ffn_router_w",
    "L0_ffn_w_fc", "L0_ffn_b_fc", "L0_ffn_w_proj", "L0_ffn_b_proj",
    "L1_attn_Wq", "L1_attn_Wk", "L1_attn_Wv", "L1_attn_Wo",
    "L1_ffn_slot_init", "L1_ffn_w1", "L1_ffn_b1", "L1_ffn_w2", "L1_ffn_b2",
    "L1_ffn_tau", "L1_ffn_Wih", "L1_ffn_Whh", "L1_ffn_bih", "L1_ffn_bhh",
    "L1_ffn_wq", "L1_ffn_bq", "L1_ffn_wk", "L1_ffn_bk", "L1_ffn_wv",
    "L1_ffn_bv", "L1_ffn_ln_g", "L1_ffn_ln_b", "L1_ffn_wvp", "L1_ffn_bvp",
    "L1_ffn_wo", "L1_ffn_bo",
]


def kernel(**inputs):
    x = np.ascontiguousarray(np.asarray(inputs["L0_attn_x"], dtype=np.float32))
    B, T, Dm = x.shape
    assert (B, Dm) == (8, 1024)
    nc = build_program(T=T)
    base = {k: np.ascontiguousarray(np.asarray(inputs[k], dtype=np.float32))
            for k in _WNAMES}
    in_maps = []
    for b in range(8):
        m = dict(base)
        m["x"] = np.ascontiguousarray(x[b])
        in_maps.append(m)
    res = run_bass_kernel_spmd(nc, in_maps, core_ids=list(range(8)))
    out = np.stack([res.results[b]["y"] for b in range(8)], axis=0)
    return out.astype(np.float32)
